# version 24
# baseline (speedup 1.0000x reference)
import numpy as np
from contextlib import ExitStack

import concourse.bass as bass
import concourse.mybir as mybir
from concourse.bass_utils import run_bass_kernel_spmd

F32 = mybir.dt.float32
F32R = mybir.dt.float32r
F16 = mybir.dt.float16
BF16 = mybir.dt.bfloat16
ALU = mybir.AluOpType
ACT = mybir.ActivationFunctionType
AX = mybir.AxisListType

CH = 64
H = 32
DM = 1024
DI = 2048
EPS = 1e-5
C21 = float(2 ** 20 + 1)


class Plan:
    ENGINES = ("TE", "VE", "SE")
    QUEUES = ("SQ", "GQ")
    NSLOT = 8

    def __init__(self, nc, es):
        self.nc = nc
        self.sems = {e: es.enter_context(nc.semaphore("sem_" + e)) for e in self.ENGINES}
        for q in self.QUEUES:
            for i in range(self.NSLOT):
                self.sems[f"{q}{i}"] = es.enter_context(nc.semaphore(f"sem_{q}{i}"))
        self.sems["CC"] = es.enter_context(nc.semaphore("sem_CC"))
        self.sems["GC"] = es.enter_context(nc.semaphore("sem_GC"))
        self.cnt = {e: 0 for e in self.sems}
        self.qidx = {q: 0 for q in self.QUEUES}
        self.ops = {e: [] for e in list(self.ENGINES) + list(self.QUEUES)}
        self.waited = {e: {} for e in list(self.ENGINES) + list(self.QUEUES)}

    def _waits(self, eng, deps, extra=()):
        need = {}
        for d in list(deps) + list(extra):
            if d is None:
                continue
            te_, tc = d
            if tc > need.get(te_, 0):
                need[te_] = tc
        waits = []
        for te_, tc in need.items():
            if self.waited[eng].get(te_, 0) < tc:
                self.waited[eng][te_] = tc
                waits.append((te_, tc))
        return waits

    def emit(self, eng, fn, deps=()):
        if eng in self.QUEUES:
            slot = self.qidx[eng] % self.NSLOT
            self.qidx[eng] += 1
            name = f"{eng}{slot}"
            extra = []
            if self.cnt[name] > 0:
                extra = [(name, self.cnt[name])]
            waits = self._waits(eng, deps, extra)
            self.cnt[name] += 16
            self.ops[eng].append((waits, fn, name, 16))
            return (name, self.cnt[name])
        waits = self._waits(eng, deps)
        self.cnt[eng] += 1
        self.ops[eng].append((waits, fn, eng, 1))
        return (eng, self.cnt[eng])

    def emit_cc(self, fn, deps=()):
        waits = self._waits("GQ", deps)
        self.cnt["CC"] += 1
        self.ops["GQ"].append((waits, fn, "CC", 1))
        return ("CC", self.cnt["CC"])

    def emit_gc(self, fn, deps=()):
        waits = self._waits("GQ", deps)
        self.cnt["GC"] += 1
        self.ops["GQ"].append((waits, fn, "GC", 1))
        return ("GC", self.cnt["GC"])

    def qdone(self, q):
        return [(f"{q}{i}", self.cnt[f"{q}{i}"]) for i in range(self.NSLOT)]

    def realize(self, block):
        secs = {"TE": block.tensor, "VE": block.vector, "SE": block.scalar,
                "SQ": block.sync, "GQ": block.gpsimd}
        sems = self.sems
        for ename in list(self.ENGINES) + list(self.QUEUES):
            ops = self.ops[ename]

            def make(ops=ops):
                def body(eng):
                    for waits, fn, sname, step in ops:
                        for te_, tc in waits:
                            eng.wait_ge(sems[te_], tc)
                        fn(eng).then_inc(sems[sname], step)
                return body
            secs[ename](make())


class Slot:
    __slots__ = ("w", "r")

    def __init__(self):
        self.w = []
        self.r = []

    def rd(self):
        return list(self.w)

    def wd(self):
        return list(self.w) + self.r

    def wrote(self, tok):
        self.w = [tok]
        self.r = []

    def wrote_many(self, toks):
        self.w = list(toks)
        self.r = []

    def read(self, tok):
        self.r.append(tok)


def host_constants():
    c = {}
    sm4 = np.zeros((128, 128), np.float32)
    for b in (0, 32, 64, 96):
        sm4[b, :64] = 1.0
        sm4[b + 1, 64:] = 1.0
    c["cseqm"] = sm4
    trl = np.tril(np.ones((64, 64), np.float32)).T
    c["ctril"] = np.concatenate([trl, trl], axis=0)
    i2 = np.zeros((128, 64), np.float32)
    for r in range(128):
        i2[r, r % 64] = 1.0
    c["ci2"] = i2
    c["ci2r"] = np.ascontiguousarray(np.tile(i2, (1, 8)))
    c["cident"] = np.eye(128, dtype=np.float32)
    c["cones"] = np.ones((1, 128), np.float32)
    return c


def build(nch=32, debug=False):
    L = nch * CH
    nc = bass.Bass("TRN2", target_bir_lowering=False, debug=False)

    xin = nc.dram_tensor("xin", [2, L, DM], F32, kind="ExternalInput")
    w1t = nc.dram_tensor("w1t", [DM, 4256], F32, kind="ExternalInput")
    w2t = nc.dram_tensor("w2t", [DI, DM], F32, kind="ExternalInput")
    w3t = nc.dram_tensor("w3t", [DM, DM], F32, kind="ExternalInput")
    convw = nc.dram_tensor("convw", [128, 68], F32, kind="ExternalInput")
    convb = nc.dram_tensor("convb", [128, 17], F32, kind="ExternalInput")
    dtbd = nc.dram_tensor("dtb", [32, 1], F32, kind="ExternalInput")
    alogd = nc.dram_tensor("alog", [32, 1], F32, kind="ExternalInput")
    drepd = nc.dram_tensor("drep", [128, 32], F32, kind="ExternalInput")
    rmsw16 = nc.dram_tensor("rmsw16", [128, 16], F32, kind="ExternalInput")
    gammad = nc.dram_tensor("gamma", [1, DM], F32, kind="ExternalInput")
    betad = nc.dram_tensor("beta", [1, DM], F32, kind="ExternalInput")
    fbiasd = nc.dram_tensor("fbias", [1, DM], F32, kind="ExternalInput")
    cseqm = nc.dram_tensor("cseqm", [128, 128], F32, kind="ExternalInput")
    ctrild = nc.dram_tensor("ctril", [128, 64], F32, kind="ExternalInput")
    ci2d = nc.dram_tensor("ci2", [128, 64], F32, kind="ExternalInput")
    ci2rd = nc.dram_tensor("ci2r", [128, 512], F32, kind="ExternalInput")
    cidentd = nc.dram_tensor("cident", [128, 128], F32, kind="ExternalInput")
    conesd = nc.dram_tensor("cones", [1, 128], F32, kind="ExternalInput")

    yout = nc.dram_tensor("yout", [2, L, DM], F32, kind="ExternalOutput")
    x1_hbm = nc.dram_tensor("x1s", [nch * 128, DM], F32,
                            kind=("ExternalOutput" if debug else "Internal"))
    cc_in = nc.dram_tensor("cc_in", [128, 1], F32)
    cc_out = nc.dram_tensor("cc_out", [128, 1], F32)
    wmx_hbm = nc.dram_tensor("wmx", [128, 1], F32)

    es = ExitStack()
    with es:
        P = Plan(nc, es)

        def sb(stk, name, shape, dtype):
            return stk.enter_context(nc.sbuf_tensor(name, shape, dtype))

        def pt(name, shape, dtype=F32):
            return es.enter_context(nc.psum_tensor(name, shape, dtype))

        idb_s = sb(es, "idb_s", [128, 128], BF16)
        ones_s = sb(es, "ones_s", [1, 128], F32)
        amtmp = sb(es, "amtmp", [128, 1], F32)
        amrow = sb(es, "amrow", [1, 128], F32)
        cols = {}
        for n in ["ssq", "ssq2", "sdc", "rsc", "hsum", "hsq", "muc", "varc", "nmr",
                  "vsum", "vsq", "mu2", "var2", "rs2", "nmr2", "epsc", "invdic", "onecol",
                  "nhalfc", "lnv", "hsqn", "negmu"]:
            cols[n] = sb(es, n, [128, 1], F32)

        ps_ip = [pt(f"ps_ip{i}", [128, 512]) for i in range(2)]
        ps_eb = [pt(f"ps_eb{i}", [128, 512]) for i in range(2)]
        ps_y = [pt(f"ps_y{i}", [128, 512]) for i in range(2)]
        ps_st = ps_eb
        ps_tb = pt("ps_tb", [128, 1024], BF16)
        ps_tb2 = pt("ps_tb2", [128, 1024], BF16)

        sl = {}

        def S(name):
            sl[name] = Slot()
            return sl[name]

        for n in ["idb", "ones", "amtmp", "amrow"] + list(cols):
            S(n)
        for n in ["ps_ip0", "ps_ip1", "ps_eb0", "ps_eb1", "ps_y0", "ps_y1",
                  "ps_tb", "ps_tb2"]:
            S(n)
        sl["ps_st0"] = sl["ps_eb0"]
        sl["ps_st1"] = sl["ps_eb1"]

        es1 = ExitStack()
        W1s = [sb(es1, f"w1_{k}", [128, 4256], BF16) for k in range(8)]
        W2s = [sb(es1, f"w2_{k}", [128, DM], BF16) for k in range(16)]
        rmsw_s = sb(es1, "rmsw_s", [128, 16], F32)
        cw_s = sb(es1, "cw_s", [128, 68], F32)
        cb_s = sb(es1, "cb_s", [128, 17], F32)
        dtb_s = sb(es1, "dtb_s", [32, 1], F32)
        alog_s = sb(es1, "alog_s", [32, 1], F32)
        acol = sb(es1, "acol", [32, 1], F32)
        drep_s = sb(es1, "drep_s", [128, 32], F32)
        seqm_s = sb(es1, "seqm_s", [128, 128], F16)
        tril_s = sb(es1, "tril_s", [128, 64], F32)
        triu_sb = sb(es1, "triu_sb", [128, 64], BF16)
        i2r_s = sb(es1, "i2r_s", [128, 512], BF16)
        idf_s = sb(es1, "idf_s", [32, 32], F32)
        di_s = sb(es1, "di_s", [128, 2048], BF16)
        gbc = sb(es1, "gbc", [128, DM], BF16)
        bbc = sb(es1, "bbc", [128, DM], BF16)

        xtm = [sb(es1, f"xtm{i}", [128, DM], F32) for i in range(2)]
        xbf1 = sb(es1, "xbf1", [128, DM], BF16)
        xfm = sb(es1, "xfm", [128, 1024], BF16)
        zact = sb(es1, "zact", [128, DI], BF16)
        xbcr = sb(es1, "xbcr", [128, 16 * 134], BF16)
        bcr = sb(es1, "bcr", [128, 134], BF16)
        tails = sb(es1, "tails", [128, 102], BF16)
        xsconv = sb(es1, "xsconv", [128, DI], BF16)
        bcc = sb(es1, "bcc", [128, 128], BF16)
        cwdg = sb(es1, "cwdg", [128, 68 * 128], BF16)
        dtfm = sb(es1, "dtfm", [32, 128], F32)
        dafm = sb(es1, "dafm", [32, 128], F32)
        edafm = sb(es1, "edafm", [32, 128], F32)
        dttm = sb(es1, "dttm", [128, 32], F32)
        datm = sb(es1, "datm", [128, 32], F32)
        eacshl = sb(es1, "eacshl", [64, 64], F32)
        flats1 = sb(es1, "flats1", [64, 2048], F16)
        flats2 = [flats1, flats1]
        btm = sb(es1, "btm", [128, 64], BF16)
        cttm = sb(es1, "cttm", [128, 64], BF16)
        bfm2 = sb(es1, "bfm2", [128, 64], BF16)
        cfm2 = sb(es1, "cfm2", [128, 64], BF16)
        scoresm = sb(es1, "scoresm", [128, 64], BF16)
        escan = sb(es1, "escan", [128, 2048], BF16)
        slb = sb(es1, "slb", [128, 2048], BF16)
        cdt = sb(es1, "cdt", [128, 2048], BF16)
        bdec = sb(es1, "bdec", [128, 2048], BF16)
        xstm = sb(es1, "xstm", [128, 2048], BF16)
        xdtm = sb(es1, "xdtm", [128, 2048], BF16)
        expacsbc = sb(es1, "expacsbc", [128, 2048], BF16)
        edec = [sb(es1, f"edec{i}", [128, 32], BF16) for i in range(2)]
        tstate = sb(es1, "tstate", [128, 2048], BF16)
        yn = sb(es1, "yn", [128, DI], BF16)
        stsb = sb(es1, "stsb", [128, 2048], BF16)
        h1 = sb(es1, "h1", [128, DM], F32)
        a2 = sb(es1, "a2", [128, DM], F32)
        vv = h1
        x1t = a2
        amaxc = [sb(es1, f"amaxc{i}", [128, 1], F32) for i in range(2)]
        import os as _os
        if _os.environ.get("KSLACK"):
            print("phase1 sbuf remaining:", nc.sbuf_bytes_remaining)

        for n in ["w1", "w2", "rmsw", "cw", "cb", "dtb", "alog", "acol", "drep",
                  "seqm", "tril", "triu", "i2", "idf", "di", "gbc", "bbc",
                  "xfm", "zact", "xbcr", "bcr", "tails", "xsconv", "bcc",
                  "cwdg", "dtfm", "dafm", "edafm", "dttm", "datm",
                  "eacshl", "i2r", "btm", "cttm", "bfm2", "cfm2", "scoresm",
                  "escan", "slb", "cdt", "bdec", "xstm", "xdtm", "expacsbc",
                  "tstate", "yn", "stsb", "h1", "a2", "vv", "x1t"]:
            S(n)
        sl["xtm"] = [Slot(), Slot()]
        S("xbf1")
        sl["vv"] = sl["h1"]
        sl["x1t"] = sl["a2"]
        _fsl = Slot()
        sl["flats"] = [_fsl, _fsl]
        sl["edec"] = [Slot(), Slot()]
        sl["amaxc"] = [Slot(), Slot()]

        def R(n):
            return sl[n].rd()

        def Wd(n):
            return sl[n].wd()

        def ve(fn, deps):
            return P.emit("VE", fn, deps)

        def se(fn, deps):
            return P.emit("SE", fn, deps)

        def te(fn, deps):
            return P.emit("TE", fn, deps)

        def gq(fn, deps):
            return P.emit("GQ", fn, deps)

        def sq(fn, deps):
            return P.emit("SQ", fn, deps)

        def g_dma(dst, src, deps):
            return gq(lambda g, dst=dst, src=src: g.dma_start(out=dst, in_=src), deps)

        def q_dma(dst, src, deps):
            return sq(lambda s, dst=dst, src=src: s.dma_start(out=dst, in_=src), deps)

        w1toks = []
        for k in range(8):
            w1toks.append(g_dma(W1s[k][:, :], w1t[k * 128:(k + 1) * 128, :], Wd("w1")))
        w2toks = []
        for k in range(16):
            w2toks.append(g_dma(W2s[k][:, :], w2t[k * 128:(k + 1) * 128, :], Wd("w2")))
        for (dst, src, n) in [(rmsw_s, rmsw16, "rmsw"), (cw_s, convw, "cw"),
                              (cb_s, convb, "cb"), (dtb_s, dtbd, "dtb"),
                              (alog_s, alogd, "alog"), (drep_s, drepd, "drep"),
                              (seqm_s, cseqm, "seqm"), (tril_s, ctrild, "tril"),
                              (triu_sb, ctrild, "triu"),
                              (i2r_s, ci2rd, "i2r"),
                              (idb_s, cidentd, "idb"),
                              (ones_s, conesd, "ones")]:
            t = g_dma(dst[:, :], src[:, :], Wd(n))
            sl[n].wrote(t)
        sl["w1"].wrote_many(w1toks)
        w2stoks = []
        for k in range(16):
            w2stoks.append(ve((lambda k: lambda v: v.tensor_scalar(
                W2s[k][:, :], W2s[k][:, :], rmsw_s[:, k:k + 1], None, ALU.mult))(k),
                [w2toks[k]] + R("rmsw")))
        sl["w2"].wrote_many(w2stoks)
        t = se(lambda s: s.activation(acol[:, 0:1], alog_s[:, 0:1], ACT.Exp),
               R("alog") + Wd("acol"))
        sl["acol"].wrote(t)
        t = ve(lambda v: v.tensor_scalar(acol[:, 0:1], acol[:, 0:1], -1.0, None, ALU.mult),
               Wd("acol"))
        sl["acol"].wrote(t)
        div0 = bass.AP(i2r_s.ap().tensor, 0, [[512, 128], [0, 32], [1, 64]])
        div1 = bass.AP(drep_s.ap().tensor, 0, [[32, 128], [1, 32], [0, 64]])
        t = ve(lambda v: v.tensor_tensor(di_s[:, :], div0, div1, ALU.mult),
               R("i2r") + R("drep") + Wd("di"))
        sl["di"].wrote(t)
        grow = h1
        brow = a2
        sg, sbr = sl["h1"], sl["a2"]
        t = g_dma(grow[0:1, :], gammad[:, :], sg.wd())
        sg.wrote(t)
        t = g_dma(brow[0:1, :], betad[:, :], sbr.wd())
        sbr.wrote(t)
        for (row, rs_, dst, dn) in ((grow, sg, gbc, "gbc"), (brow, sbr, bbc, "bbc")):
            for hhalf in range(2):
                t = te((lambda row, hhalf: lambda tt: tt.matmul(
                    ps_ip[hhalf][:, 0:512], ones_s[0:1, :],
                    row[0:1, hhalf * 512:(hhalf + 1) * 512],
                    start=True, stop=True))(row, hhalf),
                    rs_.rd() + R("ones") + Wd(f"ps_ip{hhalf}"))
                sl[f"ps_ip{hhalf}"].wrote(t)
                rs_.read(t)
                t = ve((lambda dst, hhalf: lambda v: v.tensor_copy(
                    dst[:, hhalf * 512:(hhalf + 1) * 512], ps_ip[hhalf][:, 0:512]))(dst, hhalf),
                    R(f"ps_ip{hhalf}") + Wd(dn))
                sl[f"ps_ip{hhalf}"].read(t)
            sl[dn].wrote(t)
        t = ve(lambda v: v.memset(tstate[:, :], 0.0), Wd("tstate"))
        sl["tstate"].wrote(t)
        for i in range(2):
            t = ve((lambda i: lambda v: v.memset(amaxc[i][:, :], 0.0))(i),
                   sl["amaxc"][i].wd())
            sl["amaxc"][i].wrote(t)
        t = ve(lambda v: v.memset(tails[:, :], 0.0), Wd("tails"))
        sl["tails"].wrote(t)
        t = ve(lambda v: v.memset(cols["epsc"][:, :], float(EPS)), Wd("epsc"))
        sl["epsc"].wrote(t)
        t = ve(lambda v: v.memset(cols["invdic"][:, :], 1.0 / DI), Wd("invdic"))
        sl["invdic"].wrote(t)
        t = ve(lambda v: v.memset(cols["onecol"][:, :], 1.0), Wd("onecol"))
        sl["onecol"].wrote(t)
        t = ve(lambda v: v.memset(cols["nhalfc"][:, :], -0.5), Wd("nhalfc"))
        sl["nhalfc"].wrote(t)
        t = g_dma(idf_s[:, :], cidentd[0:32, 0:32], Wd("idf"))
        sl["idf"].wrote(t)
        for mk in range(68):
            t = ve((lambda mk: lambda v: v.tensor_scalar(
                cwdg[:, mk * 128:(mk + 1) * 128], idb_s[:, :], cw_s[:, mk:mk + 1],
                None, ALU.mult))(mk),
                R("idb") + R("cw") + (Wd("cwdg") if mk == 0 else []))
        sl["cwdg"].wrote(t)

        def load_xbf(c):
            src_ = bass.AP(xin.ap().tensor, c * CH * DM, [[DM, 64], [1, DM]])
            t1 = g_dma(xbf1[0:64, :], src_, sl["xbf1"].wd())
            src_ = bass.AP(xin.ap().tensor, L * DM + c * CH * DM, [[DM, 64], [1, DM]])
            t2 = g_dma(xbf1[64:128, :], src_, [])
            sl["xbf1"].wrote_many([t1, t2])

        def load_x(c):
            p2 = c % 2
            tl = []
            for s_ in range(2):
                src = bass.AP(xin.ap().tensor, s_ * L * DM + c * CH * DM, [[DM, 64], [1, DM]])
                tl.append(q_dma(xtm[p2][s_ * 64:(s_ + 1) * 64, :], src,
                                sl["xtm"][p2].wd() if s_ == 0 else []))
            sl["xtm"][p2].wrote_many(tl)


        def xprep(c):
            for k in range(8):
                t = te((lambda k: lambda tt: tt.transpose(
                    ps_tb2[:, k * 128:(k + 1) * 128], xbf1[:, k * 128:(k + 1) * 128],
                    idb_s[:, :]))(k),
                    R("xbf1") + R("idb") + (Wd("ps_tb2") if k == 0 else []))
            sl["ps_tb2"].wrote(t)
            sl["xbf1"].read(t)
            t = ve(lambda v: v.tensor_copy(xfm[:, :], ps_tb2[:, :]),
                   R("ps_tb2") + Wd("xfm"))
            sl["ps_tb2"].read(t)
            sl["xfm"].wrote(t)

        def head(c):
            p = c % 2
            if c + 1 < nch:
                load_xbf(c + 1)

            for mg in range(5):
                bank = mg % 2
                mlist = (list(range((mg - 1) * 4, mg * 4)) if mg > 0 else [16, 17])
                first = True
                for m in mlist:
                    rows = 32 if m == 17 else 128
                    for k in range(8):
                        t = te((lambda m, k, bank, rows: lambda tt: tt.matmul(
                            ps_ip[bank][0:rows, (m % 4) * 128:(m % 4) * 128 + 128],
                            W1s[k][:, DI + m * 128: DI + m * 128 + rows],
                            xfm[:, k * 128:(k + 1) * 128],
                            start=(k == 0), stop=(k == 7)))(m, k, bank, rows),
                            R("xfm") + R("w1") + (Wd(f"ps_ip{bank}") if first else []))
                        first = False
                sl[f"ps_ip{bank}"].wrote(t)
                sl["xfm"].read(t)
                if mlist[-1] <= 15:
                    dst = bass.AP(xbcr.ap().tensor, mlist[0] * 134 + 3,
                                  [[16 * 134, 128], [134, len(mlist)], [67, 2], [1, 64]])
                    src = bass.AP(ps_ip[bank].ap().tensor, 0,
                                  [[512, 128], [128, len(mlist)], [64, 2], [1, 64]])
                    t = ve((lambda dst, src: lambda v: v.tensor_copy(dst, src))(dst, src),
                           R(f"ps_ip{bank}") + Wd("xbcr"))
                    sl[f"ps_ip{bank}"].read(t)
                    sl["xbcr"].wrote(t)
                else:
                    bdst = bass.AP(bcr.ap().tensor, 3, [[134, 128], [67, 2], [1, 64]])
                    bsrc = bass.AP(ps_ip[bank].ap().tensor, 0, [[512, 128], [64, 2], [1, 64]])
                    tse = se((lambda bank: lambda s: s.activation(
                        dtfm[:, :], ps_ip[bank][0:32, 128:256], ACT.Exp,
                        bias=dtb_s[:, 0:1]))(bank),
                        R(f"ps_ip{bank}") + R("dtb") + Wd("dtfm"))
                    sl[f"ps_ip{bank}"].read(tse)
                    sl["dtfm"].wrote(tse)
                    tb = ve((lambda bdst, bsrc: lambda v: v.tensor_copy(bdst, bsrc))(bdst, bsrc),
                            R(f"ps_ip{bank}") + [tse] + Wd("bcr"))
                    sl[f"ps_ip{bank}"].read(tb)
                    sl["bcr"].wrote(tb)
                    t = se(lambda s: s.activation(dtfm[:, :], dtfm[:, :], ACT.Ln,
                                                  bias=cols["onecol"][0:32, 0:1]),
                           Wd("dtfm") + R("onecol"))
                    sl["dtfm"].wrote(t)

            t = ve(lambda v: v.tensor_scalar(dafm[:, :], dtfm[:, :], acol[:, 0:1], None, ALU.mult),
                   R("dtfm") + R("acol") + Wd("dafm"))
            sl["dafm"].wrote(t)
            t = se(lambda s: s.activation(edafm[:, :], dafm[:, :], ACT.Exp),
                   R("dafm") + Wd("edafm"))
            sl["edafm"].wrote(t)

            hd = bass.AP(xbcr.ap().tensor, 0, [[16 * 134, 128], [134, 16], [67, 2], [1, 3]])
            hd2 = bass.AP(bcr.ap().tensor, 0, [[134, 128], [67, 2], [1, 3]])
            if c > 0:
                hs = bass.AP(tails.ap().tensor, 0, [[102, 128], [1, 96]])
                t = ve(lambda v, hd=hd, hs=hs: v.tensor_copy(hd, hs),
                       R("tails") + Wd("xbcr"))
                sl["tails"].read(t)
                sl["xbcr"].wrote(t)
                hs2 = bass.AP(tails.ap().tensor, 96, [[102, 128], [1, 6]])
                t = ve(lambda v, hd2=hd2, hs2=hs2: v.tensor_copy(hd2, hs2),
                       R("tails") + Wd("bcr"))
                sl["tails"].read(t)
                sl["bcr"].wrote(t)
            else:
                t = ve(lambda v, hd=hd: v.memset(hd, 0.0), Wd("xbcr"))
                sl["xbcr"].wrote(t)
                t = ve(lambda v, hd2=hd2: v.memset(hd2, 0.0), Wd("bcr"))
                sl["bcr"].wrote(t)

            for cg in range(5):
                bank = (cg + 1) % 2
                mlist = list(range(cg * 4, min(17, cg * 4 + 4)))
                first = True
                for m in mlist:
                    srcb = bcr if m == 16 else xbcr
                    off = 0 if m == 16 else m * 134
                    sn = "bcr" if m == 16 else "xbcr"
                    pstep = 134 if m == 16 else 16 * 134
                    for k in range(4):
                        win = bass.AP(srcb.ap().tensor, off + k, [[pstep, 128], [67, 2], [1, 64]])
                        t = te((lambda m, k, bank, win: lambda tt: tt.matmul(
                            ps_ip[bank][:, (m % 4) * 128:(m % 4) * 128 + 128],
                            cwdg[:, (4 * m + k) * 128:(4 * m + k + 1) * 128], win,
                            start=(k == 0), stop=(k == 3)))(m, k, bank, win),
                            R(sn) + R("cwdg") + (Wd(f"ps_ip{bank}") if first else []))
                        first = False
                sl[f"ps_ip{bank}"].wrote(t)
                sl["xbcr"].read(t)
                sl["bcr"].read(t)
                if mlist[-1] != 16:
                    cbv = cb_s[:, mlist[0]:mlist[0] + 1]
                    for m in mlist:
                        t = se((lambda m, bank: lambda s: s.activation(
                            xsconv[:, m * 128:(m + 1) * 128],
                            ps_ip[bank][:, (m % 4) * 128:(m % 4) * 128 + 128],
                            ACT.Silu, bias=cb_s[:, m:m + 1]))(m, bank),
                            R(f"ps_ip{bank}") + R("cb") + (Wd("xsconv") if m == 0 else []))
                        if m == 0:
                            sl["xsconv"].wrote(t)
                        sl[f"ps_ip{bank}"].read(t)
                else:
                    for m in mlist:
                        dstt, dn = (bcc, "bcc") if m == 16 else (xsconv, "xsconv")
                        t = se((lambda m, bank, dstt: lambda s: s.activation(
                            dstt[:, 0:128] if dstt is bcc else dstt[:, m * 128:(m + 1) * 128],
                            ps_ip[bank][:, (m % 4) * 128:(m % 4) * 128 + 128],
                            ACT.Silu, bias=cb_s[:, m:m + 1]))(m, bank, dstt),
                            R(f"ps_ip{bank}") + R("cb") + (Wd(dn) if dstt is bcc else []))
                        if dstt is bcc:
                            sl["bcc"].wrote(t)
                        sl[f"ps_ip{bank}"].read(t)
            sl["xsconv"].wrote(("SE", P.cnt["SE"]))
            ts_src = bass.AP(xbcr.ap().tensor, 64, [[16 * 134, 128], [134, 16], [67, 2], [1, 3]])
            ts_dst = bass.AP(tails.ap().tensor, 0, [[102, 128], [1, 96]])
            t = ve(lambda v, a=ts_dst, b=ts_src: v.tensor_copy(a, b),
                   R("xbcr") + Wd("tails"))
            sl["xbcr"].read(t)
            sl["tails"].wrote(t)
            tb_src = bass.AP(bcr.ap().tensor, 64, [[134, 128], [67, 2], [1, 3]])
            tb_dst = bass.AP(tails.ap().tensor, 96, [[102, 128], [1, 6]])
            t = ve(lambda v, a=tb_dst, b=tb_src: v.tensor_copy(a, b),
                   R("bcr") + Wd("tails"))
            sl["bcr"].read(t)
            sl["tails"].wrote(t)

            for (srct, sn, dn, dstt) in ((dtfm, "dtfm", "dttm", dttm),
                                         (dafm, "dafm", "datm", datm)):
                t = te((lambda srct: lambda tt: tt.transpose(
                    ps_eb[0][:, 0:32], srct[:, :], idf_s[:, :]))(srct),
                    R(sn) + R("idf") + Wd("ps_eb0"))
                sl["ps_eb0"].wrote(t)
                sl[sn].read(t)
                t = ve((lambda dstt: lambda v: v.tensor_copy(dstt[:, :], ps_eb[0][:, 0:32]))(dstt),
                       R("ps_eb0") + Wd(dn))
                sl["ps_eb0"].read(t)
                sl[dn].wrote(t)

            for s_ in range(2):
                t = te((lambda s_: lambda tt: tt.matmul(
                    ps_eb[1][s_ * 32:(s_ + 1) * 32, 0:64],
                    datm[s_ * 64:(s_ + 1) * 64, :],
                    tril_s[s_ * 64:(s_ + 1) * 64, :],
                    start=True, stop=True))(s_),
                    R("datm") + R("tril") + (Wd("ps_eb1") if s_ == 0 else []))
            sl["ps_eb1"].wrote(t)
            sl["datm"].read(t)
            t = se(lambda s: s.activation(eacshl[:, :], ps_eb[1][0:64, 0:64], ACT.Exp),
                   R("ps_eb1") + Wd("eacshl"))
            sl["ps_eb1"].read(t)
            sl["eacshl"].wrote(t)

            fl = flats2[p]
            ftoks = []
            for s_ in range(2):
                src = bass.AP(edafm.ap().tensor, s_ * 64, [[128, 32], [1, 64]])
                dst = bass.AP(fl.ap().tensor, s_ * 2048, [[2048, 1], [64, 32], [1, 64]])
                t = g_dma(dst, src, R("edafm") + (sl["flats"][p].wd() if s_ == 0 else []))
                sl["edafm"].read(t)
                ftoks.append(t)
            for s_ in range(2):
                src = bass.AP(eacshl.ap().tensor, s_ * 32 * 64, [[64, 32], [1, 64]])
                dst = bass.AP(fl.ap().tensor, (32 + s_) * 2048,
                              [[2048, 1], [64, 32], [1, 64]])
                t = g_dma(dst, src, R("eacshl"))
                sl["eacshl"].read(t)
                ftoks.append(t)
            sl["flats"][p].wrote_many(ftoks)

            t = te(lambda tt: tt.transpose(ps_tb2[:, 0:64], bcc[0:64, 0:128], idb_s[0:64, 0:64]),
                   R("bcc") + R("idb") + Wd("ps_tb2"))
            sl["ps_tb2"].wrote(t)
            t = te(lambda tt: tt.transpose(ps_tb2[:, 64:128], bcc[64:128, 0:128],
                                           idb_s[64:128, 64:128]),
                   R("bcc") + Wd("ps_tb2"))
            sl["ps_tb2"].wrote(t)
            sl["bcc"].read(t)
            t = ve(lambda v: v.tensor_copy(btm[:, :], ps_tb2[:, 0:64]),
                   R("ps_tb2") + Wd("btm"))
            sl["ps_tb2"].read(t)
            sl["btm"].wrote(t)
            t = ve(lambda v: v.tensor_copy(cttm[:, :], ps_tb2[:, 64:128]),
                   R("ps_tb2") + Wd("cttm"))
            sl["ps_tb2"].read(t)
            sl["cttm"].wrote(t)
            for (srct, sn, dn, dstt) in ((btm, "btm", "bfm2", bfm2), (cttm, "cttm", "cfm2", cfm2)):
                for s_ in range(2):
                    t = te((lambda srct, s_: lambda tt: tt.transpose(
                        ps_tb2[s_ * 64:(s_ + 1) * 64, 128:192],
                        srct[s_ * 64:(s_ + 1) * 64, 0:64],
                        idb_s[s_ * 64:(s_ + 1) * 64, s_ * 64:(s_ + 1) * 64]))(srct, s_),
                        R(sn) + (Wd("ps_tb2") if s_ == 0 else []))
                sl["ps_tb2"].wrote(t)
                sl[sn].read(t)
                t = ve((lambda dstt: lambda v: v.tensor_copy(dstt[:, :], ps_tb2[:, 128:192]))(dstt),
                       R("ps_tb2") + Wd(dn))
                sl["ps_tb2"].read(t)
                sl[dn].wrote(t)
            for s_ in range(2):
                t = te((lambda s_: lambda tt: tt.matmul(
                    ps_ip[0][s_ * 64:(s_ + 1) * 64, 0:64],
                    bfm2[s_ * 64:(s_ + 1) * 64, :],
                    cfm2[s_ * 64:(s_ + 1) * 64, :],
                    start=True, stop=True))(s_),
                    R("bfm2") + R("cfm2") + (Wd("ps_ip0") if s_ == 0 else []))
            sl["ps_ip0"].wrote(t)
            sl["bfm2"].read(t)
            sl["cfm2"].read(t)
            t = ve(lambda v: v.tensor_tensor(scoresm[:, :], ps_ip[0][:, 0:64],
                                             triu_sb[:, :], ALU.mult),
                   R("ps_ip0") + R("triu") + Wd("scoresm"))
            sl["ps_ip0"].read(t)
            sl["scoresm"].wrote(t)

            for hhalf in range(2):
                pst = ps_tb if hhalf == 0 else ps_tb2
                pn = "ps_tb" if hhalf == 0 else "ps_tb2"
                for j in range(8):
                    m = hhalf * 8 + j
                    t = te((lambda m, j, pst: lambda tt: tt.transpose(
                        pst[:, j * 128:(j + 1) * 128],
                        xsconv[:, m * 128:(m + 1) * 128], idb_s[:, :]))(m, j, pst),
                        R("xsconv") + (Wd(pn) if j == 0 else []))
                sl[pn].wrote(t)
                sl["xsconv"].read(t)
                t = ve((lambda hhalf, pst: lambda v: v.tensor_copy(
                    xstm[:, hhalf * 1024:(hhalf + 1) * 1024], pst[:, :]))(hhalf, pst),
                    R(pn) + (Wd("xstm") if hhalf == 0 else []))
                sl[pn].read(t)
                if hhalf == 0:
                    sl["xstm"].wrote(t)
                dtv = bass.AP(dttm.ap().tensor, hhalf * 16, [[32, 128], [1, 16], [0, 64]])
                t = ve((lambda hhalf, pst, dtv: lambda v: v.tensor_tensor(
                    xdtm[:, hhalf * 1024:(hhalf + 1) * 1024], pst[:, :], dtv, ALU.mult))(hhalf, pst, dtv),
                    R(pn) + R("dttm") + (Wd("xdtm") if hhalf == 0 else []))
                sl[pn].read(t)
                sl["dttm"].read(t)
                if hhalf == 0:
                    sl["xdtm"].wrote(t)
            sl["xstm"].wrote(("VE", P.cnt["VE"]))
            sl["xdtm"].wrote(("VE", P.cnt["VE"]))

        def body(c):
            p = c % 2
            q_ = 1 - p
            if c > 0:
                edvv = bass.AP(edec[q_].ap().tensor, 0, [[32, 128], [1, 32], [0, 64]])
                t = ve(lambda v, edvv=edvv: v.tensor_tensor(yn[:, :], tstate[:, :], edvv, ALU.mult),
                       R("tstate") + sl["edec"][q_].rd() + Wd("yn"))
                sl["tstate"].read(t)
                sl["edec"][q_].read(t)
                sl["yn"].wrote(t)
                t = ve(lambda v: v.tensor_tensor(tstate[:, :], yn[:, :], stsb[:, :], ALU.add),
                       R("yn") + R("stsb") + Wd("tstate"))
                sl["yn"].read(t)
                sl["stsb"].read(t)
                sl["tstate"].wrote(t)
            for gz in range(4):
                bank = gz % 2
                for k in range(8):
                    t = te((lambda gz, k, bank: lambda tt: tt.matmul(
                        ps_ip[bank][:, 0:512],
                        xfm[:, k * 128:(k + 1) * 128],
                        W1s[k][:, gz * 512:(gz + 1) * 512],
                        start=(k == 0), stop=(k == 7)))(gz, k, bank),
                        R("xfm") + R("w1") + (Wd(f"ps_ip{bank}") if k == 0 else []))
                sl[f"ps_ip{bank}"].wrote(t)
                sl["xfm"].read(t)
                t = se((lambda gz, bank: lambda s: s.activation(
                    zact[:, gz * 512:(gz + 1) * 512], ps_ip[bank][:, 0:512], ACT.Silu))(gz, bank),
                    R(f"ps_ip{bank}") + (Wd("zact") if gz == 0 else []))
                sl[f"ps_ip{bank}"].read(t)
            sl["zact"].wrote(t)

            fl = flats2[p]
            prev = None
            for gq_ in range(4):
                bank = gq_ % 2
                t = te((lambda gq_, bank, fl: lambda tt: tt.matmul(
                    ps_eb[bank][:, 0:512], seqm_s[0:2, :],
                    fl[0:2, gq_ * 512:(gq_ + 1) * 512],
                    start=True, stop=True))(gq_, bank, fl),
                    sl["flats"][p].rd() + R("seqm") + Wd(f"ps_eb{bank}"))
                sl[f"ps_eb{bank}"].wrote(t)
                sl["flats"][p].read(t)
                t = ve((lambda gq_, bank: lambda v: v.tensor_tensor_scan(
                    escan[:, gq_ * 512:(gq_ + 1) * 512], ps_eb[bank][:, 0:512],
                    i2r_s[:, 0:512], 0.0, ALU.mult, ALU.max))(gq_, bank),
                    R(f"ps_eb{bank}") + R("i2r") + Wd("escan"))
                sl[f"ps_eb{bank}"].read(t)
                sl["escan"].wrote(t)
                prev = t
            sl["escan"].wrote(prev)

            scv = bass.AP(scoresm.ap().tensor, 0, [[64, 128], [0, 32], [1, 64]])
            t = ve(lambda v, scv=scv: v.tensor_tensor(slb[:, :], escan[:, :], scv, ALU.mult),
                   R("escan") + R("scoresm") + Wd("slb"))
            sl["escan"].read(t)
            sl["scoresm"].read(t)
            sl["slb"].wrote(t)

            for gq_ in range(4):
                bank = gq_ % 2
                t = te((lambda gq_, bank, fl: lambda tt: tt.matmul(
                    ps_eb[bank][:, 0:512], seqm_s[32:34, :],
                    fl[32:34, gq_ * 512:(gq_ + 1) * 512],
                    start=True, stop=True))(gq_, bank, fl),
                    sl["flats"][p].rd() + R("seqm") + Wd(f"ps_eb{bank}"))
                sl[f"ps_eb{bank}"].wrote(t)
                sl["flats"][p].read(t)
                t = ve((lambda gq_, bank: lambda v: v.tensor_copy(
                    expacsbc[:, gq_ * 512:(gq_ + 1) * 512], ps_eb[bank][:, 0:512]))(gq_, bank),
                    R(f"ps_eb{bank}") + (Wd("expacsbc") if gq_ == 0 else []))
                sl[f"ps_eb{bank}"].read(t)
            sl["expacsbc"].wrote(t)
            edv = bass.AP(expacsbc.ap().tensor, 63, [[2048, 128], [64, 32]])
            t = ve((lambda p, edv: lambda v: v.tensor_copy(edec[p][:, :], edv))(p, edv),
                   R("expacsbc") + sl["edec"][p].wd())
            sl["expacsbc"].read(t)
            sl["edec"][p].wrote(t)
            cdv = bass.AP(cfm2.ap().tensor, 0, [[64, 128], [0, 32], [1, 64]])
            t = ve(lambda v, cdv=cdv: v.tensor_tensor(cdt[:, :], expacsbc[:, :], cdv, ALU.mult),
                   R("expacsbc") + R("cfm2") + Wd("cdt"))
            sl["expacsbc"].read(t)
            sl["cfm2"].read(t)
            sl["cdt"].wrote(t)

            b0 = bass.AP(btm.ap().tensor, 0, [[64, 128], [0, 32], [1, 64]])
            ecol = bass.AP(escan.ap().tensor, 63, [[2048, 128], [64, 32], [0, 64]])
            t = ve(lambda v, b0=b0, ecol=ecol: v.tensor_tensor(bdec[:, :], b0, ecol, ALU.mult),
                   R("btm") + R("escan") + Wd("bdec"))
            sl["btm"].read(t)
            sl["escan"].read(t)
            sl["bdec"].wrote(t)

            for quarter in range(4):
                ybank = quarter % 2
                first = True
                for hh in range(quarter * 8, quarter * 8 + 8):
                    for s_ in range(2):
                        r0 = s_ * 64
                        fo = (hh % 8) * 64
                        deps = (R("slb") + R("xdtm") + R("tstate") + R("cdt") +
                                R("xstm") + R("di") + (Wd(f"ps_y{ybank}") if first else []))
                        te((lambda hh, r0, fo, ybank: lambda tt: tt.matmul(
                            ps_y[ybank][r0:r0 + 64, fo:fo + 64],
                            slb[r0:r0 + 64, hh * 64:(hh + 1) * 64],
                            xdtm[r0:r0 + 64, hh * 64:(hh + 1) * 64],
                            start=True, stop=False))(hh, r0, fo, ybank), deps)
                        first = False
                        te((lambda hh, r0, fo, ybank: lambda tt: tt.matmul(
                            ps_y[ybank][r0:r0 + 64, fo:fo + 64],
                            cdt[r0:r0 + 64, hh * 64:(hh + 1) * 64],
                            tstate[r0:r0 + 64, hh * 64:(hh + 1) * 64],
                            start=False, stop=False))(hh, r0, fo, ybank), [])
                        te((lambda hh, r0, fo, ybank: lambda tt: tt.matmul(
                            ps_y[ybank][r0:r0 + 64, fo:fo + 64],
                            di_s[r0:r0 + 64, hh * 64:(hh + 1) * 64],
                            xstm[r0:r0 + 64, hh * 64:(hh + 1) * 64],
                            start=False, stop=True))(hh, r0, fo, ybank), [])
                tok = ("TE", P.cnt["TE"])
                sl[f"ps_y{ybank}"].wrote(tok)
                for n in ["slb", "xdtm", "tstate", "cdt", "xstm"]:
                    sl[n].read(tok)
                t = ve((lambda quarter, ybank: lambda v: v.tensor_tensor(
                    slb[:, quarter * 512:(quarter + 1) * 512], ps_y[ybank][:, 0:512],
                    zact[:, quarter * 512:(quarter + 1) * 512], ALU.mult))(quarter, ybank),
                    R(f"ps_y{ybank}") + R("zact") + Wd("slb"))
                sl[f"ps_y{ybank}"].read(t)
                sl["zact"].read(t)
                sl["slb"].wrote(t)

            first_st = True
            for hhalf in range(2):
                for bank in range(2):
                    first = True
                    for hh in range(hhalf * 16 + bank * 8, hhalf * 16 + bank * 8 + 8):
                        for s_ in range(2):
                            r0 = s_ * 64
                            fo = (hh % 8) * 64
                            te((lambda hh, r0, fo, bank: lambda tt: tt.matmul(
                                ps_st[bank][r0:r0 + 64, fo:fo + 64],
                                bdec[r0:r0 + 64, hh * 64:(hh + 1) * 64],
                                xdtm[r0:r0 + 64, hh * 64:(hh + 1) * 64],
                                start=True, stop=True))(hh, r0, fo, bank),
                                R("bdec") + R("xdtm") + (Wd(f"ps_st{bank}") if first else []))
                            first = False
                    tok = ("TE", P.cnt["TE"])
                    sl[f"ps_st{bank}"].wrote(tok)
                    sl["bdec"].read(tok)
                    sl["xdtm"].read(tok)
                    t = ve((lambda hhalf, bank: lambda v: v.tensor_copy(
                        stsb[:, hhalf * 1024 + bank * 512:hhalf * 1024 + bank * 512 + 512],
                        ps_st[bank][:, 0:512]))(hhalf, bank),
                        R(f"ps_st{bank}") + (Wd("stsb") if first_st else []))
                    first_st = False
                    sl[f"ps_st{bank}"].read(t)
            sl["stsb"].wrote(t)

            if c > 0:
                tail_ln(c - 1)
            if c + 1 < nch:
                load_x(c + 1)
                xprep(c + 1)

            for hhalf in range(2):
                t = se((lambda hhalf: lambda s: s.activation(
                    escan[:, 0:DM], slb[:, hhalf * 1024:(hhalf + 1) * 1024], ACT.Square,
                    accum_out=(cols["ssq"] if hhalf == 0 else cols["ssq2"])[:, 0:1]))(hhalf),
                    Wd("escan") + R("slb") + Wd("ssq" if hhalf == 0 else "ssq2"))
                sl["slb"].read(t)
                sl["escan"].wrote(t)
                sl["ssq" if hhalf == 0 else "ssq2"].wrote(t)
            t = ve(lambda v: v.tensor_tensor(cols["ssq"][:, 0:1], cols["ssq"][:, 0:1],
                                             cols["ssq2"][:, 0:1], ALU.add),
                   Wd("ssq") + R("ssq2"))
            sl["ssq"].wrote(t)
            t = se(lambda s: s.activation(cols["lnv"][:, 0:1], cols["ssq"][:, 0:1], ACT.Ln,
                                          bias=cols["epsc"][:, 0:1], scale=cols["invdic"][:, 0:1]),
                   R("ssq") + R("epsc") + R("invdic") + Wd("lnv"))
            sl["lnv"].wrote(t)
            t = se(lambda s: s.activation(cols["rsc"][:, 0:1], cols["lnv"][:, 0:1], ACT.Exp,
                                          scale=cols["nhalfc"][:, 0:1]),
                   R("lnv") + R("nhalfc") + Wd("rsc"))
            sl["rsc"].wrote(t)
            t = se(lambda s: s.mul(yn[:, :], slb[:, :], cols["rsc"][:, 0:1]),
                   R("slb") + R("rsc") + Wd("yn"))
            sl["slb"].read(t)
            sl["yn"].wrote(t)

        def tail_mm(c):
            p = c % 2
            for hhalf in range(2):
                pst = ps_tb if hhalf == 0 else ps_tb2
                pn = "ps_tb" if hhalf == 0 else "ps_tb2"
                for j in range(8):
                    t = te((lambda hhalf, j, pst: lambda tt: tt.transpose(
                        pst[:, j * 128:(j + 1) * 128],
                        yn[:, (hhalf * 8 + j) * 128:(hhalf * 8 + j + 1) * 128],
                        idb_s[:, :]))(hhalf, j, pst),
                        R("yn") + (Wd(pn) if j == 0 else []))
                sl[pn].wrote(t)
                sl["yn"].read(t)
                t = ve((lambda hhalf, pst: lambda v: v.tensor_copy(
                    cdt[:, hhalf * 1024:(hhalf + 1) * 1024], pst[:, :]))(hhalf, pst),
                    R(pn) + Wd("cdt"))
                sl[pn].read(t)
                sl["cdt"].wrote(t)

            t_h = None
            for nh in range(2):
                bank = nh
                for k in range(16):
                    t = te((lambda nh, k, bank: lambda tt: tt.matmul(
                        ps_y[bank][:, 0:512],
                        cdt[:, k * 128:(k + 1) * 128],
                        W2s[k][:, nh * 512:(nh + 1) * 512],
                        start=(k == 0), stop=(k == 15)))(nh, k, bank),
                        R("cdt") + R("w2") + (Wd(f"ps_y{bank}") if k == 0 else []))
                sl[f"ps_y{bank}"].wrote(t)
                sl["cdt"].read(t)
                t_h = ve((lambda nh, bank, p: lambda v: v.tensor_tensor(
                    h1[:, nh * 512:(nh + 1) * 512], ps_y[bank][:, 0:512],
                    xtm[p][:, nh * 512:(nh + 1) * 512], ALU.add))(nh, bank, p),
                    R(f"ps_y{bank}") + sl["xtm"][p].rd() + (Wd("h1") if nh == 0 else []))
                sl[f"ps_y{bank}"].read(t_h)
                sl["xtm"][p].read(t_h)
            sl["h1"].wrote(t_h)

        def tail_ln(c):
            p = c % 2
            t = se(lambda s: s.activation(escan[:, 0:DM], h1[:, :], ACT.Identity,
                                          accum_out=cols["hsum"][:, 0:1]),
                   R("h1") + Wd("escan") + Wd("hsum"))
            sl["h1"].read(t)
            sl["escan"].wrote(t)
            sl["hsum"].wrote(t)
            t = se(lambda s: s.activation(escan[:, 0:DM], h1[:, :], ACT.Square,
                                          accum_out=cols["hsq"][:, 0:1]),
                   R("h1") + Wd("escan") + Wd("hsq"))
            sl["h1"].read(t)
            sl["escan"].wrote(t)
            sl["hsq"].wrote(t)
            t = P.emit_gc(lambda g_: g_.tensor_scalar(cols["muc"][:, 0:1], cols["hsum"][:, 0:1],
                                               1.0 / DM, None, ALU.mult),
                   R("hsum") + Wd("muc"))
            sl["muc"].wrote(t)
            t = P.emit_gc(lambda g_: g_.tensor_tensor(cols["varc"][:, 0:1], cols["muc"][:, 0:1],
                                               cols["muc"][:, 0:1], ALU.mult),
                   R("muc") + Wd("varc"))
            sl["varc"].wrote(t)
            t = P.emit_gc(lambda g_: g_.tensor_scalar(cols["hsqn"][:, 0:1], cols["hsq"][:, 0:1],
                                               1.0 / DM, None, ALU.mult),
                   R("hsq") + Wd("hsqn"))
            sl["hsqn"].wrote(t)
            t = P.emit_gc(lambda g_: g_.tensor_tensor(cols["varc"][:, 0:1], cols["hsqn"][:, 0:1],
                                               cols["varc"][:, 0:1], ALU.subtract),
                   R("hsqn") + Wd("varc"))
            sl["varc"].wrote(t)
            t = se(lambda s: s.activation(cols["lnv"][:, 0:1], cols["varc"][:, 0:1],
                                          ACT.Ln, bias=cols["epsc"][:, 0:1]),
                   R("varc") + R("epsc") + Wd("lnv"))
            sl["lnv"].wrote(t)
            t = se(lambda s: s.activation(cols["rsc"][:, 0:1], cols["lnv"][:, 0:1],
                                          ACT.Exp, scale=cols["nhalfc"][:, 0:1]),
                   R("lnv") + R("nhalfc") + Wd("rsc"))
            sl["rsc"].wrote(t)
            t = P.emit_gc(lambda g_: g_.tensor_scalar(cols["negmu"][:, 0:1], cols["muc"][:, 0:1],
                                               -1.0, None, ALU.mult),
                   R("muc") + Wd("negmu"))
            sl["negmu"].wrote(t)
            t = P.emit_gc(lambda g_: g_.tensor_tensor(cols["nmr"][:, 0:1], cols["negmu"][:, 0:1],
                                               cols["rsc"][:, 0:1], ALU.mult),
                   R("negmu") + R("rsc") + Wd("nmr"))
            sl["nmr"].wrote(t)
            t = se(lambda s: s.activation(a2[:, :], h1[:, :], ACT.Identity,
                                          bias=cols["nmr"][:, 0:1], scale=cols["rsc"][:, 0:1]),
                   R("h1") + R("nmr") + R("rsc") + Wd("a2"))
            sl["h1"].read(t)
            sl["a2"].wrote(t)
            t = ve(lambda v: v.tensor_tensor(a2[:, :], a2[:, :], gbc[:, :], ALU.mult),
                   Wd("a2") + R("gbc"))
            sl["gbc"].read(t)
            sl["a2"].wrote(t)
            t = ve(lambda v: v.tensor_tensor(a2[:, :], a2[:, :], bbc[:, :], ALU.add),
                   Wd("a2") + R("bbc"))
            sl["bbc"].read(t)
            sl["a2"].wrote(t)
            t = ve((lambda p: lambda v: v.scalar_tensor_tensor(
                vv[:, :], a2[:, :], 0.0, xtm[p][:, :], ALU.add, ALU.add,
                accum_out=cols["vsum"][:, 0:1]))(p),
                R("a2") + sl["xtm"][p].rd() + Wd("vv") + Wd("vsum"))
            sl["a2"].read(t)
            sl["xtm"][p].read(t)
            sl["vv"].wrote(t)
            sl["vsum"].wrote(t)
            t = se(lambda s: s.activation(escan[:, 0:DM], vv[:, :], ACT.Square,
                                          accum_out=cols["vsq"][:, 0:1]),
                   R("vv") + Wd("escan") + Wd("vsq"))
            sl["vv"].read(t)
            sl["escan"].wrote(t)
            sl["vsq"].wrote(t)
            t = P.emit_gc(lambda g_: g_.tensor_scalar(cols["mu2"][:, 0:1], cols["vsum"][:, 0:1],
                                               1.0 / DM, None, ALU.mult),
                   R("vsum") + Wd("mu2"))
            sl["mu2"].wrote(t)
            t = P.emit_gc(lambda g_: g_.tensor_tensor(cols["var2"][:, 0:1], cols["mu2"][:, 0:1],
                                               cols["mu2"][:, 0:1], ALU.mult),
                   R("mu2") + Wd("var2"))
            sl["var2"].wrote(t)
            t = P.emit_gc(lambda g_: g_.tensor_scalar(cols["hsqn"][:, 0:1], cols["vsq"][:, 0:1],
                                               1.0 / DM, None, ALU.mult),
                   R("vsq") + Wd("hsqn"))
            sl["hsqn"].wrote(t)
            t = P.emit_gc(lambda g_: g_.tensor_tensor(cols["var2"][:, 0:1], cols["hsqn"][:, 0:1],
                                               cols["var2"][:, 0:1], ALU.subtract),
                   R("hsqn") + Wd("var2"))
            sl["var2"].wrote(t)
            t = se(lambda s: s.activation(cols["sdc"][:, 0:1], cols["var2"][:, 0:1],
                                          ACT.Sqrt, bias=cols["epsc"][:, 0:1]),
                   R("var2") + R("epsc") + Wd("sdc"))
            sl["sdc"].wrote(t)
            t = ve(lambda v: v.reciprocal(cols["rs2"][:, 0:1], cols["sdc"][:, 0:1]),
                   R("sdc") + Wd("rs2"))
            sl["rs2"].wrote(t)
            t = P.emit_gc(lambda g_: g_.tensor_scalar(cols["negmu"][:, 0:1], cols["mu2"][:, 0:1],
                                               -1.0, None, ALU.mult),
                   R("mu2") + Wd("negmu"))
            sl["negmu"].wrote(t)
            t = P.emit_gc(lambda g_: g_.tensor_tensor(cols["nmr2"][:, 0:1], cols["negmu"][:, 0:1],
                                               cols["rs2"][:, 0:1], ALU.mult),
                   R("negmu") + R("rs2") + Wd("nmr2"))
            sl["nmr2"].wrote(t)
            t = se(lambda s: s.activation(x1t[:, :], vv[:, :], ACT.Identity,
                                          bias=cols["nmr2"][:, 0:1], scale=cols["rs2"][:, 0:1]),
                   R("vv") + R("nmr2") + R("rs2") + Wd("x1t"))
            sl["vv"].read(t)
            sl["x1t"].wrote(t)
            t = ve(lambda v: v.reduce_max(amtmp[:, 0:1], x1t[:, :], axis=AX.X,
                                          apply_absolute_value=True),
                   R("x1t") + Wd("amtmp"))
            sl["x1t"].read(t)
            sl["amtmp"].wrote(t)
            ai, aj = c % 2, 1 - (c % 2)
            t = ve((lambda ai, aj: lambda v: v.tensor_tensor(
                amaxc[ai][:, 0:1], amaxc[aj][:, 0:1], amtmp[:, 0:1], ALU.max))(ai, aj),
                sl["amaxc"][aj].rd() + R("amtmp") + sl["amaxc"][ai].wd())
            sl["amaxc"][aj].read(t)
            sl["amtmp"].read(t)
            sl["amaxc"][ai].wrote(t)
            t = g_dma(x1_hbm[c * 128:(c + 1) * 128, :], x1t[:, :], R("x1t"))
            sl["x1t"].read(t)

        load_xbf(0)
        load_x(0)
        xprep(0)
        head(0)
        for c in range(nch):
            body(c)
            if c + 1 < nch:
                head(c + 1)
            tail_mm(c)
        tail_ln(nch - 1)

        lastp = (nch - 1) % 2
        t = g_dma(cc_in.ap(), amaxc[lastp][:, 0:1], sl["amaxc"][lastp].rd())
        alldone = ([("TE", P.cnt["TE"]), ("VE", P.cnt["VE"]), ("SE", P.cnt["SE"]), t]
                   + P.qdone("SQ") + P.qdone("GQ"))
        tokcc = P.emit_cc(lambda g: g.collective_compute(
            "AllReduce", ALU.max, replica_groups=[list(range(8))],
            ins=[cc_in.ap()], outs=[cc_out.ap()]), [t])
        t = g_dma(bass.AP(amrow.ap().tensor, 0, [[128, 1], [1, 128]]),
                  bass.AP(cc_out.ap().tensor, 0, [[128, 1], [1, 128]]), [tokcc])
        sl["amrow"].wrote(t)
        bar = [tokcc, t] + alldone

        es1.close()

        sqd = [sb(es, f"sqd{i}", [128, DM], BF16) for i in range(2)]
        c2 = [{}, {}]
        for pgi in range(2):
            for n in ["vsum", "vsq", "mu2", "var2", "rs2", "nmr2", "sdc"]:
                c2[pgi][n] = sb(es, f"p2_{n}{pgi}", [128, 1], F32)
        s2c = [{n: Slot() for n in c2[0]} for _ in range(2)]
        s2q = [Slot(), Slot()]
        w3full = [sb(es, f"w3f{k}", [128, DM], F32) for k in range(8)]
        w3q = [sb(es, f"w3q{k}", [128, DM], BF16) for k in range(8)]
        x1ld = [sb(es, f"x1ld{i}", [128, DM], F32) for i in range(4)]
        qq = [sb(es, f"qq{i}", [128, DM], F32) for i in range(4)]
        uq = [sb(es, f"uq{i}", [128, DM], F32) for i in range(4)]
        x1q = [sb(es, f"x1q{i}", [128, DM], BF16) for i in range(4)]
        x1qt = [sb(es, f"x1qt{i}", [128, DM], BF16) for i in range(4)]
        f1 = [sb(es, f"f1_{i}", [128, DM], F32) for i in range(2)]
        v2 = [sb(es, f"v2_{i}", [128, DM], F32) for i in range(2)]
        x2 = [sb(es, f"x2_{i}", [128, DM], F32) for i in range(2)]
        onesb = sb(es, "onesb", [1, 128], BF16)
        fbq = sb(es, "fbq", [1, DM], BF16)
        svvc = sb(es, "svvc", [1, 1], F32)
        fbrow = sb(es, "fbrow", [1, DM], F32)
        amg = sb(es, "amg", [1, 1], F32)
        wmrow = sb(es, "wmrow", [1, 128], F32)
        wamax = sb(es, "wamax", [1, 1], F32)
        sxv = sb(es, "sxv", [1, 1], F32)
        swv = sb(es, "swv", [1, 1], F32)
        sxi = sb(es, "sxi", [1, 1], F32)
        swi = sb(es, "swi", [1, 1], F32)
        sxw = sb(es, "sxw", [1, 1], F32)
        onec = sb(es, "onec", [1, 1], F32)
        c448 = sb(es, "c448", [1, 1], F32)
        sxcol = sb(es, "sxcol", [128, 1], F32)
        swcol = sb(es, "swcol", [128, 1], F32)
        sxwcol = sb(es, "sxwcol", [128, 1], F32)
        wmxc = [sb(es, f"wmxc{i}", [128, 1], F32) for i in range(2)]
        for n in ["onesb", "fbq", "svvc", "fbrow", "amg", "wmrow", "wamax",
                  "sxv", "swv", "sxi", "swi", "sxw", "onec", "c448",
                  "sxcol", "swcol", "sxwcol", "w3q"]:
            S(n)
        for n in ["wmxc", "x2", "f1", "v2"]:
            sl[n] = [Slot(), Slot()]
        for n in ["x1ld", "qq", "uq", "x1q", "x1qt"]:
            sl[n] = [Slot() for _ in range(4)]
        sl["w3full"] = [Slot() for _ in range(8)]

        t = ve(lambda v: v.memset(onec[0:1, 0:1], 1.0), bar + Wd("onec"))
        sl["onec"].wrote(t)
        t = ve(lambda v: v.memset(c448[0:1, 0:1], 448.0), bar + Wd("c448"))
        sl["c448"].wrote(t)

        for k in range(8):
            t = q_dma(w3full[k][:, :], w3t[k * 128:(k + 1) * 128, :],
                      bar + sl["w3full"][k].wd())
            sl["w3full"][k].wrote(t)
            t = ve((lambda k: lambda v: v.reduce_max(amtmp[:, 0:1], w3full[k][:, :],
                                                     axis=AX.X, apply_absolute_value=True))(k),
                   sl["w3full"][k].rd() + Wd("amtmp"))
            sl["amtmp"].wrote(t)
            sl["w3full"][k].read(t)
            if k == 0:
                t = ve(lambda v: v.tensor_copy(wmxc[0][:, 0:1], amtmp[:, 0:1]),
                       R("amtmp") + sl["wmxc"][0].wd())
            else:
                ai, aj = k % 2, 1 - k % 2
                t = ve((lambda ai, aj: lambda v: v.tensor_tensor(
                    wmxc[ai][:, 0:1], wmxc[aj][:, 0:1], amtmp[:, 0:1], ALU.max))(ai, aj),
                    sl["wmxc"][aj].rd() + R("amtmp") + sl["wmxc"][ai].wd())
                sl["wmxc"][aj].read(t)
            sl["wmxc"][k % 2].wrote(t)
            sl["amtmp"].read(t)
        t = g_dma(wmx_hbm.ap(), wmxc[1][:, 0:1], sl["wmxc"][1].rd() + bar)
        t = g_dma(bass.AP(wmrow.ap().tensor, 0, [[128, 1], [1, 128]]),
                  bass.AP(wmx_hbm.ap().tensor, 0, [[128, 1], [1, 128]]), [t])
        sl["wmrow"].wrote(t)
        t = ve(lambda v: v.reduce_max(wamax[0:1, 0:1], wmrow[0:1, :], axis=AX.X),
               R("wmrow") + Wd("wamax"))
        sl["wamax"].wrote(t)
        t = ve(lambda v: v.tensor_scalar(wamax[0:1, 0:1], wamax[0:1, 0:1], 1e-12, None, ALU.max),
               Wd("wamax"))
        sl["wamax"].wrote(t)
        t = ve(lambda v: v.reciprocal(swv[0:1, 0:1], wamax[0:1, 0:1]),
               R("wamax") + Wd("swv"))
        sl["swv"].wrote(t)
        t = ve(lambda v: v.tensor_scalar(swv[0:1, 0:1], swv[0:1, 0:1], 448.0, None, ALU.mult),
               Wd("swv"))
        sl["swv"].wrote(t)
        t = ve(lambda v: v.tensor_scalar(swv[0:1, 0:1], swv[0:1, 0:1], 448.0, None, ALU.min),
               Wd("swv"))
        sl["swv"].wrote(t)
        t = ve(lambda v: v.reciprocal(swi[0:1, 0:1], swv[0:1, 0:1]),
               R("swv") + Wd("swi"))
        sl["swi"].wrote(t)
        for (src, sn, dst, dn) in ((swv, "swv", swcol, "swcol"),):
            t = te((lambda src: lambda tt: tt.matmul(
                ps_ip[0][:, 0:1], ones_s[0:1, :], src[0:1, 0:1],
                start=True, stop=True))(src),
                bar + R(sn) + R("ones") + Wd("ps_ip0"))
            sl["ps_ip0"].wrote(t)
            t = ve((lambda dst: lambda v: v.tensor_copy(dst[:, 0:1], ps_ip[0][:, 0:1]))(dst),
                   R("ps_ip0") + Wd(dn))
            sl["ps_ip0"].read(t)
            sl[dn].wrote(t)

        for k in range(8):
            pk = k % 4
            t = se((lambda k, pk: lambda s: s.activation(
                qq[pk][:, :], w3full[k][:, :], ACT.Identity,
                scale=swcol[:, 0:1]))(k, pk),
                sl["w3full"][k].rd() + R("swcol") + sl["qq"][pk].wd())
            sl["w3full"][k].read(t)
            sl["qq"][pk].wrote(t)
            t = ve((lambda pk: lambda v: v.scalar_tensor_tensor(
                uq[pk][:, :], qq[pk][:, :], C21, qq[pk][:, :],
                ALU.mult, ALU.subtract))(pk),
                sl["qq"][pk].rd() + sl["uq"][pk].wd())
            sl["uq"][pk].wrote(t)
            t = ve((lambda pk, k: lambda v: v.scalar_tensor_tensor(
                w3q[k][:, :], qq[pk][:, :], C21, uq[pk][:, :],
                ALU.mult, ALU.subtract))(pk, k),
                sl["qq"][pk].rd() + sl["uq"][pk].rd() + Wd("w3q"))
            sl["qq"][pk].read(t)
            sl["uq"][pk].read(t)
        sl["w3q"].wrote(t)
        t = ve(lambda v: v.reduce_max(amg[0:1, 0:1], amrow[0:1, :], axis=AX.X),
               bar + R("amrow") + Wd("amg"))
        sl["amg"].wrote(t)
        t = ve(lambda v: v.tensor_scalar(amg[0:1, 0:1], amg[0:1, 0:1], 1e-12, None, ALU.max),
               Wd("amg"))
        sl["amg"].wrote(t)
        t = ve(lambda v: v.reciprocal(sxv[0:1, 0:1], amg[0:1, 0:1]),
               R("amg") + Wd("sxv"))
        sl["sxv"].wrote(t)
        t = ve(lambda v: v.tensor_scalar(sxv[0:1, 0:1], sxv[0:1, 0:1], 448.0, None, ALU.mult),
               Wd("sxv"))
        sl["sxv"].wrote(t)
        t = ve(lambda v: v.tensor_scalar(sxv[0:1, 0:1], sxv[0:1, 0:1], 448.0, None, ALU.min),
               Wd("sxv"))
        sl["sxv"].wrote(t)
        t = ve(lambda v: v.reciprocal(sxi[0:1, 0:1], sxv[0:1, 0:1]),
               R("sxv") + Wd("sxi"))
        sl["sxi"].wrote(t)
        t = ve(lambda v: v.tensor_tensor(sxw[0:1, 0:1], sxi[0:1, 0:1], swi[0:1, 0:1], ALU.mult),
               R("sxi") + R("swi") + Wd("sxw"))
        sl["sxw"].wrote(t)
        for (src, sn, dst, dn) in ((sxv, "sxv", sxcol, "sxcol"),
                                   (sxw, "sxw", sxwcol, "sxwcol")):
            t = te((lambda src: lambda tt: tt.matmul(
                ps_ip[0][:, 0:1], ones_s[0:1, :], src[0:1, 0:1],
                start=True, stop=True))(src),
                bar + R(sn) + R("ones") + Wd("ps_ip0"))
            sl["ps_ip0"].wrote(t)
            t = ve((lambda dst: lambda v: v.tensor_copy(dst[:, 0:1], ps_ip[0][:, 0:1]))(dst),
                   R("ps_ip0") + Wd(dn))
            sl["ps_ip0"].read(t)
            sl[dn].wrote(t)
        t = g_dma(fbrow[:, :], fbiasd[:, :], bar + Wd("fbrow"))
        sl["fbrow"].wrote(t)
        t = ve(lambda v: v.memset(onesb[0:1, :], 1.0), bar + Wd("onesb"))
        sl["onesb"].wrote(t)
        t = ve(lambda v: v.tensor_tensor(svvc[0:1, 0:1], sxv[0:1, 0:1], swv[0:1, 0:1],
                                         ALU.mult),
               R("sxv") + R("swv") + Wd("svvc"))
        sl["svvc"].wrote(t)
        t = ve(lambda v: v.tensor_scalar(fbq[0:1, :], fbrow[0:1, :], svvc[0:1, 0:1],
                                         None, ALU.mult),
               R("fbrow") + R("svvc") + Wd("fbq"))
        sl["fbq"].wrote(t)

        def x1_load(g):
            pg = g % 4
            t = q_dma(x1ld[pg][:, :], x1_hbm[g * 128:(g + 1) * 128, :],
                      bar + sl["x1ld"][pg].wd())
            sl["x1ld"][pg].wrote(t)

        for g0 in range(4):
            x1_load(g0)

        def quant_a(g):
            pg = g % 4
            t = se((lambda pg: lambda s: s.activation(
                qq[pg][:, :], x1ld[pg][:, :], ACT.Identity,
                scale=sxcol[:, 0:1]))(pg),
                sl["x1ld"][pg].rd() + R("sxcol") + sl["qq"][pg].wd())
            sl["x1ld"][pg].read(t)
            sl["qq"][pg].wrote(t)

        def quant_b(g):
            pg = g % 4
            t = ve((lambda pg: lambda v: v.scalar_tensor_tensor(
                uq[pg][:, :], qq[pg][:, :], C21, qq[pg][:, :],
                ALU.mult, ALU.subtract))(pg),
                sl["qq"][pg].rd() + sl["uq"][pg].wd())
            sl["uq"][pg].wrote(t)
            t = ve((lambda pg: lambda v: v.scalar_tensor_tensor(
                x1q[pg][:, :], qq[pg][:, :], C21, uq[pg][:, :],
                ALU.mult, ALU.subtract))(pg),
                sl["qq"][pg].rd() + sl["uq"][pg].rd() + sl["x1q"][pg].wd())
            sl["qq"][pg].read(t)
            sl["uq"][pg].read(t)
            sl["x1q"][pg].wrote(t)
            pst = ps_tb if (pg % 2) == 0 else ps_tb2
            pn = "ps_tb" if (pg % 2) == 0 else "ps_tb2"
            for j in range(8):
                t = te((lambda j, pg, pst: lambda tt: tt.transpose(
                    pst[:, j * 128:(j + 1) * 128], x1q[pg][:, j * 128:(j + 1) * 128],
                    idb_s[:, :]))(j, pg, pst),
                    sl["x1q"][pg].rd() + (bar if (g == 0 and j == 0) else []) +
                    (Wd(pn) if j == 0 else []))
            sl[pn].wrote(t)
            sl["x1q"][pg].read(t)
            t = se((lambda pg, pst: lambda s: s.copy(x1qt[pg][:, :], pst[:, :]))(pg, pst),
                   sl[pn].rd() + sl["x1qt"][pg].wd())
            sl[pn].read(t)
            sl["x1qt"][pg].wrote(t)

        def ffn_mm(g):
            pg = g % 4
            ph = g % 2
            for nh in range(2):
                bank = nh
                t = te((lambda nh, bank: lambda tt: tt.matmul(
                    ps_y[bank][:, 0:512], onesb[0:1, :],
                    fbq[0:1, nh * 512:(nh + 1) * 512],
                    start=True, stop=False))(nh, bank),
                    R("onesb") + R("fbq") + Wd(f"ps_y{bank}"))
                for k in range(8):
                    t = te((lambda nh, k, bank, pg: lambda tt: tt.matmul(
                        ps_y[bank][:, 0:512],
                        x1qt[pg][:, k * 128:(k + 1) * 128],
                        w3q[k][:, nh * 512:(nh + 1) * 512],
                        start=False, stop=(k == 7)))(nh, k, bank, pg),
                        sl["x1qt"][pg].rd() + R("w3q"))
                sl[f"ps_y{bank}"].wrote(t)
                sl["x1qt"][pg].read(t)
                t = se((lambda nh, bank, ph: lambda s: s.activation(
                    f1[ph][:, nh * 512:(nh + 1) * 512], ps_y[bank][:, 0:512], ACT.Identity,
                    scale=sxwcol[:, 0:1]))(nh, bank, ph),
                    R(f"ps_y{bank}") + R("sxwcol") + (sl["f1"][ph].wd() if nh == 0 else []))
                sl[f"ps_y{bank}"].read(t)
            sl["f1"][ph].wrote(t)

        def tail(g):
            pg = g % 4
            ph = g % 2
            cc_ = c2[ph]
            sc_ = s2c[ph]
            t = ve((lambda pg, ph, cc_: lambda v: v.scalar_tensor_tensor(
                v2[ph][:, :], x1ld[pg][:, :], 0.0, f1[ph][:, :], ALU.add, ALU.add,
                accum_out=cc_["vsum"][:, 0:1]))(pg, ph, cc_),
                sl["x1ld"][pg].rd() + sl["f1"][ph].rd() + sl["v2"][ph].wd() + sc_["vsum"].wd())
            sl["x1ld"][pg].read(t)
            sl["f1"][ph].read(t)
            sl["v2"][ph].wrote(t)
            sc_["vsum"].wrote(t)
            if g + 4 < nch:
                x1_load(g + 4)
            t = se((lambda ph, cc_: lambda s: s.activation(
                sqd[ph][:, 0:DM], v2[ph][:, :], ACT.Square,
                accum_out=cc_["vsq"][:, 0:1]))(ph, cc_),
                sl["v2"][ph].rd() + s2q[ph].wd() + sc_["vsq"].wd())
            sl["v2"][ph].read(t)
            s2q[ph].wrote(t)
            sc_["vsq"].wrote(t)
            t = ve((lambda pg, cc_: lambda v: v.tensor_scalar(
                cc_["mu2"][:, 0:1], cc_["vsum"][:, 0:1], 1.0 / DM, None, ALU.mult))(pg, cc_),
                sc_["vsum"].rd() + sc_["mu2"].wd())
            sc_["mu2"].wrote(t)
            t = ve((lambda pg, cc_: lambda v: v.tensor_tensor(
                cc_["var2"][:, 0:1], cc_["mu2"][:, 0:1], cc_["mu2"][:, 0:1], ALU.mult))(pg, cc_),
                sc_["mu2"].rd() + sc_["var2"].wd())
            sc_["var2"].wrote(t)
            t = ve((lambda pg, cc_: lambda v: v.scalar_tensor_tensor(
                cc_["var2"][:, 0:1], cc_["vsq"][:, 0:1], 1.0 / DM, cc_["var2"][:, 0:1],
                ALU.mult, ALU.subtract))(pg, cc_),
                sc_["vsq"].rd() + sc_["var2"].wd())
            sc_["var2"].wrote(t)
            t = se((lambda pg, cc_: lambda s: s.activation(
                cc_["sdc"][:, 0:1], cc_["var2"][:, 0:1], ACT.Sqrt,
                bias=cols["epsc"][:, 0:1]))(pg, cc_),
                sc_["var2"].rd() + R("epsc") + sc_["sdc"].wd())
            sc_["sdc"].wrote(t)
            t = ve((lambda pg, cc_: lambda v: v.reciprocal(
                cc_["rs2"][:, 0:1], cc_["sdc"][:, 0:1]))(pg, cc_),
                sc_["sdc"].rd() + sc_["rs2"].wd())
            sc_["rs2"].wrote(t)
            t = ve((lambda pg, cc_: lambda v: v.scalar_tensor_tensor(
                cc_["nmr2"][:, 0:1], cc_["mu2"][:, 0:1], -1.0, cc_["rs2"][:, 0:1],
                ALU.mult, ALU.mult))(pg, cc_),
                sc_["mu2"].rd() + sc_["rs2"].rd() + sc_["nmr2"].wd())
            sc_["nmr2"].wrote(t)
            t = se((lambda ph, cc_: lambda s: s.activation(
                x2[ph][:, :], v2[ph][:, :], ACT.Identity,
                bias=cc_["nmr2"][:, 0:1], scale=cc_["rs2"][:, 0:1]))(ph, cc_),
                sl["v2"][ph].rd() + sc_["nmr2"].rd() + sc_["rs2"].rd() + sl["x2"][ph].wd())
            sl["v2"][ph].read(t)
            sl["x2"][ph].wrote(t)
            for s_ in range(2):
                dst = bass.AP(yout.ap().tensor, s_ * L * DM + g * CH * DM, [[DM, 64], [1, DM]])
                t = q_dma(dst, x2[ph][s_ * 64:(s_ + 1) * 64, :], sl["x2"][ph].rd())
                sl["x2"][ph].read(t)

        quant_a(0)
        quant_b(0)
        for g in range(nch):
            if g + 1 < nch:
                quant_a(g + 1)
            ffn_mm(g)
            if g + 1 < nch:
                quant_b(g + 1)
            if g > 0:
                tail(g - 1)
        tail(nch - 1)

        P.emit("SQ", lambda s: s.nop(), P.qdone("SQ"))

        with nc.Block() as block:
            P.realize(block)

    return nc


def _prep_inputs(inputs):
    W = inputs
    pr = {}
    pr["w1t"] = np.ascontiguousarray(np.asarray(W["L0_attn_in_proj_w"], np.float32).T)
    pr["w2t"] = np.ascontiguousarray(np.asarray(W["L0_attn_out_proj_w"], np.float32).T)
    pr["w3t"] = np.ascontiguousarray(np.asarray(W["L0_ffn_weight"], np.float32).T)
    cw = np.asarray(W["L0_attn_conv_w"], np.float32)[:, 0, :]
    pr["convw"] = np.ascontiguousarray(cw.reshape(17, 128, 4).transpose(1, 0, 2).reshape(128, 68))
    cb = np.asarray(W["L0_attn_conv_b"], np.float32)
    pr["convb"] = np.ascontiguousarray(cb.reshape(17, 128).T)
    pr["dtb"] = np.asarray(W["L0_attn_dt_bias"], np.float32).reshape(32, 1)
    pr["alog"] = np.asarray(W["L0_attn_A_log"], np.float32).reshape(32, 1)
    pr["drep"] = np.ascontiguousarray(np.tile(np.asarray(W["L0_attn_D"], np.float32)[None, :], (128, 1)))
    pr["rmsw16"] = np.ascontiguousarray(np.asarray(W["L0_attn_rms_w"], np.float32).reshape(16, 128).T)
    pr["gamma"] = np.asarray(W["L0_attn_ln_gamma"], np.float32).reshape(1, DM)
    pr["beta"] = np.asarray(W["L0_attn_ln_beta"], np.float32).reshape(1, DM)
    pr["fbias"] = np.asarray(W["L0_ffn_bias"], np.float32).reshape(1, DM)
    pr.update(host_constants())
    return pr


def kernel(**inputs):
    x = np.asarray(inputs["L0_attn_x"], np.float32)
    pr = _prep_inputs(inputs)
    nc = build(nch=32, debug=False)
    in_maps = []
    for core in range(8):
        m = dict(pr)
        m["xin"] = np.ascontiguousarray(x[2 * core:2 * core + 2])
        in_maps.append(m)
    res = run_bass_kernel_spmd(nc, in_maps, core_ids=list(range(8)))
    out = np.empty((16, 2048, 1024), np.float32)
    for core in range(8):
        out[2 * core:2 * core + 2] = np.asarray(res.results[core]["yout"]).reshape(2, 2048, 1024)
    return out


# version 25
# speedup vs baseline: 1.0383x; 1.0383x over previous
import numpy as np
from contextlib import ExitStack

import concourse.bass as bass
import concourse.mybir as mybir
from concourse.bass_utils import run_bass_kernel_spmd

F32 = mybir.dt.float32
F32R = mybir.dt.float32r
F16 = mybir.dt.float16
BF16 = mybir.dt.bfloat16
ALU = mybir.AluOpType
ACT = mybir.ActivationFunctionType
AX = mybir.AxisListType

CH = 64
H = 32
DM = 1024
DI = 2048
EPS = 1e-5
C21 = float(2 ** 20 + 1)


class Plan:
    ENGINES = ("TE", "VE", "SE")
    QUEUES = ("SQ", "GQ")
    NSLOT = 8

    def __init__(self, nc, es):
        self.nc = nc
        self.sems = {e: es.enter_context(nc.semaphore("sem_" + e)) for e in self.ENGINES}
        for q in self.QUEUES:
            for i in range(self.NSLOT):
                self.sems[f"{q}{i}"] = es.enter_context(nc.semaphore(f"sem_{q}{i}"))
        self.sems["CC"] = es.enter_context(nc.semaphore("sem_CC"))
        self.sems["GC"] = es.enter_context(nc.semaphore("sem_GC"))
        self.cnt = {e: 0 for e in self.sems}
        self.qidx = {q: 0 for q in self.QUEUES}
        self.ops = {e: [] for e in list(self.ENGINES) + list(self.QUEUES)}
        self.waited = {e: {} for e in list(self.ENGINES) + list(self.QUEUES)}

    def _waits(self, eng, deps, extra=()):
        need = {}
        for d in list(deps) + list(extra):
            if d is None:
                continue
            te_, tc = d
            if tc > need.get(te_, 0):
                need[te_] = tc
        waits = []
        for te_, tc in need.items():
            if self.waited[eng].get(te_, 0) < tc:
                self.waited[eng][te_] = tc
                waits.append((te_, tc))
        return waits

    def emit(self, eng, fn, deps=()):
        if eng in self.QUEUES:
            slot = self.qidx[eng] % self.NSLOT
            self.qidx[eng] += 1
            name = f"{eng}{slot}"
            extra = []
            if self.cnt[name] > 0:
                extra = [(name, self.cnt[name])]
            waits = self._waits(eng, deps, extra)
            self.cnt[name] += 16
            self.ops[eng].append((waits, fn, name, 16))
            return (name, self.cnt[name])
        waits = self._waits(eng, deps)
        self.cnt[eng] += 1
        self.ops[eng].append((waits, fn, eng, 1))
        return (eng, self.cnt[eng])

    def emit_cc(self, fn, deps=()):
        waits = self._waits("GQ", deps)
        self.cnt["CC"] += 1
        self.ops["GQ"].append((waits, fn, "CC", 1))
        return ("CC", self.cnt["CC"])

    def emit_gc(self, fn, deps=()):
        waits = self._waits("GQ", deps)
        self.cnt["GC"] += 1
        self.ops["GQ"].append((waits, fn, "GC", 1))
        return ("GC", self.cnt["GC"])

    def qdone(self, q):
        return [(f"{q}{i}", self.cnt[f"{q}{i}"]) for i in range(self.NSLOT)]

    def realize(self, block):
        secs = {"TE": block.tensor, "VE": block.vector, "SE": block.scalar,
                "SQ": block.sync, "GQ": block.gpsimd}
        sems = self.sems
        for ename in list(self.ENGINES) + list(self.QUEUES):
            ops = self.ops[ename]

            def make(ops=ops):
                def body(eng):
                    for waits, fn, sname, step in ops:
                        for te_, tc in waits:
                            eng.wait_ge(sems[te_], tc)
                        fn(eng).then_inc(sems[sname], step)
                return body
            secs[ename](make())


class Slot:
    __slots__ = ("w", "r")

    def __init__(self):
        self.w = []
        self.r = []

    def rd(self):
        return list(self.w)

    def wd(self):
        return list(self.w) + self.r

    def wrote(self, tok):
        self.w = [tok]
        self.r = []

    def wrote_many(self, toks):
        self.w = list(toks)
        self.r = []

    def read(self, tok):
        self.r.append(tok)


def host_constants():
    c = {}
    sm4 = np.zeros((128, 128), np.float32)
    for b in (0, 32, 64, 96):
        sm4[b, :64] = 1.0
        sm4[b + 1, 64:] = 1.0
    c["cseqm"] = sm4
    trl = np.tril(np.ones((64, 64), np.float32)).T
    c["ctril"] = np.concatenate([trl, trl], axis=0)
    i2 = np.zeros((128, 64), np.float32)
    for r in range(128):
        i2[r, r % 64] = 1.0
    c["ci2"] = i2
    c["ci2r"] = np.ascontiguousarray(np.tile(i2, (1, 8)))
    c["cident"] = np.eye(128, dtype=np.float32)
    c["cones"] = np.ones((1, 128), np.float32)
    return c


def build(nch=32, debug=False):
    L = nch * CH
    nc = bass.Bass("TRN2", target_bir_lowering=False, debug=False)

    xin = nc.dram_tensor("xin", [2, L, DM], F32, kind="ExternalInput")
    w1t = nc.dram_tensor("w1t", [DM, 4256], F32, kind="ExternalInput")
    w2t = nc.dram_tensor("w2t", [DI, DM], F32, kind="ExternalInput")
    w3t = nc.dram_tensor("w3t", [DM, DM], F32, kind="ExternalInput")
    convw = nc.dram_tensor("convw", [128, 68], F32, kind="ExternalInput")
    convb = nc.dram_tensor("convb", [128, 17], F32, kind="ExternalInput")
    dtbd = nc.dram_tensor("dtb", [32, 1], F32, kind="ExternalInput")
    alogd = nc.dram_tensor("alog", [32, 1], F32, kind="ExternalInput")
    drepd = nc.dram_tensor("drep", [128, 32], F32, kind="ExternalInput")
    rmsw16 = nc.dram_tensor("rmsw16", [128, 16], F32, kind="ExternalInput")
    gammad = nc.dram_tensor("gamma", [1, DM], F32, kind="ExternalInput")
    betad = nc.dram_tensor("beta", [1, DM], F32, kind="ExternalInput")
    fbiasd = nc.dram_tensor("fbias", [1, DM], F32, kind="ExternalInput")
    cseqm = nc.dram_tensor("cseqm", [128, 128], F32, kind="ExternalInput")
    ctrild = nc.dram_tensor("ctril", [128, 64], F32, kind="ExternalInput")
    ci2d = nc.dram_tensor("ci2", [128, 64], F32, kind="ExternalInput")
    ci2rd = nc.dram_tensor("ci2r", [128, 512], F32, kind="ExternalInput")
    cidentd = nc.dram_tensor("cident", [128, 128], F32, kind="ExternalInput")
    conesd = nc.dram_tensor("cones", [1, 128], F32, kind="ExternalInput")

    yout = nc.dram_tensor("yout", [2, L, DM], F32, kind="ExternalOutput")
    x1_hbm = nc.dram_tensor("x1s", [nch * 128, DM], F32,
                            kind=("ExternalOutput" if debug else "Internal"))
    cc_in = nc.dram_tensor("cc_in", [128, 1], F32)
    cc_out = nc.dram_tensor("cc_out", [128, 1], F32)
    wmx_hbm = nc.dram_tensor("wmx", [128, 1], F32)

    es = ExitStack()
    with es:
        P = Plan(nc, es)

        def sb(stk, name, shape, dtype):
            return stk.enter_context(nc.sbuf_tensor(name, shape, dtype))

        def pt(name, shape, dtype=F32):
            return es.enter_context(nc.psum_tensor(name, shape, dtype))

        idb_s = sb(es, "idb_s", [128, 128], BF16)
        ones_s = sb(es, "ones_s", [1, 128], F32)
        amtmp = sb(es, "amtmp", [128, 1], F32)
        amrow = sb(es, "amrow", [1, 128], F32)
        cols = {}
        for n in ["ssq", "ssq2", "sdc", "rsc", "hsum", "hsq", "muc", "varc", "nmr",
                  "vsum", "vsq", "mu2", "var2", "rs2", "nmr2", "epsc", "invdic", "onecol",
                  "nhalfc", "lnv", "hsqn", "negmu"]:
            cols[n] = sb(es, n, [128, 1], F32)

        ps_ip = [pt(f"ps_ip{i}", [128, 512]) for i in range(2)]
        ps_eb = [pt(f"ps_eb{i}", [128, 512]) for i in range(2)]
        ps_y = [pt(f"ps_y{i}", [128, 512]) for i in range(2)]
        ps_st = ps_eb
        ps_tb = pt("ps_tb", [128, 1024], BF16)
        ps_tb2 = pt("ps_tb2", [128, 1024], BF16)

        sl = {}

        def S(name):
            sl[name] = Slot()
            return sl[name]

        for n in ["idb", "ones", "amtmp", "amrow"] + list(cols):
            S(n)
        for n in ["ps_ip0", "ps_ip1", "ps_eb0", "ps_eb1", "ps_y0", "ps_y1",
                  "ps_tb", "ps_tb2"]:
            S(n)
        sl["ps_st0"] = sl["ps_eb0"]
        sl["ps_st1"] = sl["ps_eb1"]

        es1 = ExitStack()
        W1s = [sb(es1, f"w1_{k}", [128, 4256], BF16) for k in range(8)]
        W2s = [sb(es1, f"w2_{k}", [128, DM], BF16) for k in range(16)]
        rmsw_s = sb(es1, "rmsw_s", [128, 16], F32)
        cw_s = sb(es1, "cw_s", [128, 68], F32)
        cb_s = sb(es1, "cb_s", [128, 17], F32)
        dtb_s = sb(es1, "dtb_s", [32, 1], F32)
        alog_s = sb(es1, "alog_s", [32, 1], F32)
        acol = sb(es1, "acol", [32, 1], F32)
        drep_s = sb(es1, "drep_s", [128, 32], F32)
        seqm_s = sb(es1, "seqm_s", [128, 128], F16)
        tril_s = sb(es1, "tril_s", [128, 64], F32)
        triu_sb = sb(es1, "triu_sb", [128, 64], BF16)
        i2r_s = sb(es1, "i2r_s", [128, 512], BF16)
        idf_s = sb(es1, "idf_s", [32, 32], F32)
        di_s = sb(es1, "di_s", [128, 2048], BF16)
        gbc = sb(es1, "gbc", [128, DM], BF16)
        bbc = sb(es1, "bbc", [128, DM], BF16)

        xtm = [sb(es1, f"xtm{i}", [128, DM], F32) for i in range(2)]
        xbf1 = sb(es1, "xbf1", [128, DM], BF16)
        xfm = sb(es1, "xfm", [128, 1024], BF16)
        zact = sb(es1, "zact", [128, DI], BF16)
        xbcr = sb(es1, "xbcr", [128, 16 * 134], BF16)
        bcr = sb(es1, "bcr", [128, 134], BF16)
        tails = sb(es1, "tails", [128, 102], BF16)
        xsconv = sb(es1, "xsconv", [128, DI], BF16)
        bcc = sb(es1, "bcc", [128, 128], BF16)
        cwdg = sb(es1, "cwdg", [128, 68 * 128], BF16)
        dtfm = sb(es1, "dtfm", [32, 128], F32)
        dafm = sb(es1, "dafm", [32, 128], F32)
        edafm = sb(es1, "edafm", [32, 128], F32)
        dttm = sb(es1, "dttm", [128, 32], F32)
        datm = sb(es1, "datm", [128, 32], F32)
        eacshl = sb(es1, "eacshl", [64, 64], F32)
        flats1 = sb(es1, "flats1", [64, 2048], F16)
        flats2 = [flats1, flats1]
        btm = sb(es1, "btm", [128, 64], BF16)
        cttm = sb(es1, "cttm", [128, 64], BF16)
        bfm2 = sb(es1, "bfm2", [128, 64], BF16)
        cfm2 = sb(es1, "cfm2", [128, 64], BF16)
        scoresm = sb(es1, "scoresm", [128, 64], BF16)
        escan = sb(es1, "escan", [128, 2048], BF16)
        slb = sb(es1, "slb", [128, 2048], BF16)
        cdt = sb(es1, "cdt", [128, 2048], BF16)
        bdec = sb(es1, "bdec", [128, 2048], BF16)
        xstm = sb(es1, "xstm", [128, 2048], BF16)
        xdtm = sb(es1, "xdtm", [128, 2048], BF16)
        expacsbc = sb(es1, "expacsbc", [128, 2048], BF16)
        edec = [sb(es1, f"edec{i}", [128, 32], BF16) for i in range(2)]
        tstate = sb(es1, "tstate", [128, 2048], BF16)
        yn = sb(es1, "yn", [128, DI], BF16)
        stsb = sb(es1, "stsb", [128, 2048], BF16)
        h1 = sb(es1, "h1", [128, DM], F32)
        a2 = sb(es1, "a2", [128, DM], F32)
        vv = h1
        x1t = a2
        amaxc = [sb(es1, f"amaxc{i}", [128, 1], F32) for i in range(2)]
        import os as _os
        if _os.environ.get("KSLACK"):
            print("phase1 sbuf remaining:", nc.sbuf_bytes_remaining)

        for n in ["w1", "w2", "rmsw", "cw", "cb", "dtb", "alog", "acol", "drep",
                  "seqm", "tril", "triu", "i2", "idf", "di", "gbc", "bbc",
                  "xfm", "zact", "xbcr", "bcr", "tails", "xsconv", "bcc",
                  "cwdg", "dtfm", "dafm", "edafm", "dttm", "datm",
                  "eacshl", "i2r", "btm", "cttm", "bfm2", "cfm2", "scoresm",
                  "escan", "slb", "cdt", "bdec", "xstm", "xdtm", "expacsbc",
                  "tstate", "yn", "stsb", "h1", "a2", "vv", "x1t"]:
            S(n)
        sl["xtm"] = [Slot(), Slot()]
        S("xbf1")
        sl["vv"] = sl["h1"]
        sl["x1t"] = sl["a2"]
        _fsl = Slot()
        sl["flats"] = [_fsl, _fsl]
        sl["edec"] = [Slot(), Slot()]
        sl["amaxc"] = [Slot(), Slot()]

        def R(n):
            return sl[n].rd()

        def Wd(n):
            return sl[n].wd()

        def ve(fn, deps):
            return P.emit("VE", fn, deps)

        def se(fn, deps):
            return P.emit("SE", fn, deps)

        def te(fn, deps):
            return P.emit("TE", fn, deps)

        def gq(fn, deps):
            return P.emit("GQ", fn, deps)

        def sq(fn, deps):
            return P.emit("SQ", fn, deps)

        def g_dma(dst, src, deps):
            return gq(lambda g, dst=dst, src=src: g.dma_start(out=dst, in_=src), deps)

        def q_dma(dst, src, deps):
            return sq(lambda s, dst=dst, src=src: s.dma_start(out=dst, in_=src), deps)

        w1toks = []
        for k in range(8):
            w1toks.append(g_dma(W1s[k][:, :], w1t[k * 128:(k + 1) * 128, :], Wd("w1")))
        w2toks = []
        for k in range(16):
            w2toks.append(g_dma(W2s[k][:, :], w2t[k * 128:(k + 1) * 128, :], Wd("w2")))
        for (dst, src, n) in [(rmsw_s, rmsw16, "rmsw"), (cw_s, convw, "cw"),
                              (cb_s, convb, "cb"), (dtb_s, dtbd, "dtb"),
                              (alog_s, alogd, "alog"), (drep_s, drepd, "drep"),
                              (seqm_s, cseqm, "seqm"), (tril_s, ctrild, "tril"),
                              (triu_sb, ctrild, "triu"),
                              (i2r_s, ci2rd, "i2r"),
                              (idb_s, cidentd, "idb"),
                              (ones_s, conesd, "ones")]:
            t = g_dma(dst[:, :], src[:, :], Wd(n))
            sl[n].wrote(t)
        sl["w1"].wrote_many(w1toks)
        w2stoks = []
        for k in range(16):
            w2stoks.append(ve((lambda k: lambda v: v.tensor_scalar(
                W2s[k][:, :], W2s[k][:, :], rmsw_s[:, k:k + 1], None, ALU.mult))(k),
                [w2toks[k]] + R("rmsw")))
        sl["w2"].wrote_many(w2stoks)
        t = se(lambda s: s.activation(acol[:, 0:1], alog_s[:, 0:1], ACT.Exp),
               R("alog") + Wd("acol"))
        sl["acol"].wrote(t)
        t = ve(lambda v: v.tensor_scalar(acol[:, 0:1], acol[:, 0:1], -1.0, None, ALU.mult),
               Wd("acol"))
        sl["acol"].wrote(t)
        div0 = bass.AP(i2r_s.ap().tensor, 0, [[512, 128], [0, 32], [1, 64]])
        div1 = bass.AP(drep_s.ap().tensor, 0, [[32, 128], [1, 32], [0, 64]])
        t = ve(lambda v: v.tensor_tensor(di_s[:, :], div0, div1, ALU.mult),
               R("i2r") + R("drep") + Wd("di"))
        sl["di"].wrote(t)
        grow = h1
        brow = a2
        sg, sbr = sl["h1"], sl["a2"]
        t = g_dma(grow[0:1, :], gammad[:, :], sg.wd())
        sg.wrote(t)
        t = g_dma(brow[0:1, :], betad[:, :], sbr.wd())
        sbr.wrote(t)
        for (row, rs_, dst, dn) in ((grow, sg, gbc, "gbc"), (brow, sbr, bbc, "bbc")):
            for hhalf in range(2):
                t = te((lambda row, hhalf: lambda tt: tt.matmul(
                    ps_ip[hhalf][:, 0:512], ones_s[0:1, :],
                    row[0:1, hhalf * 512:(hhalf + 1) * 512],
                    start=True, stop=True))(row, hhalf),
                    rs_.rd() + R("ones") + Wd(f"ps_ip{hhalf}"))
                sl[f"ps_ip{hhalf}"].wrote(t)
                rs_.read(t)
                t = ve((lambda dst, hhalf: lambda v: v.tensor_copy(
                    dst[:, hhalf * 512:(hhalf + 1) * 512], ps_ip[hhalf][:, 0:512]))(dst, hhalf),
                    R(f"ps_ip{hhalf}") + Wd(dn))
                sl[f"ps_ip{hhalf}"].read(t)
            sl[dn].wrote(t)
        t = ve(lambda v: v.memset(tstate[:, :], 0.0), Wd("tstate"))
        sl["tstate"].wrote(t)
        for i in range(2):
            t = ve((lambda i: lambda v: v.memset(amaxc[i][:, :], 0.0))(i),
                   sl["amaxc"][i].wd())
            sl["amaxc"][i].wrote(t)
        t = ve(lambda v: v.memset(tails[:, :], 0.0), Wd("tails"))
        sl["tails"].wrote(t)
        t = ve(lambda v: v.memset(cols["epsc"][:, :], float(EPS)), Wd("epsc"))
        sl["epsc"].wrote(t)
        t = ve(lambda v: v.memset(cols["invdic"][:, :], 1.0 / DI), Wd("invdic"))
        sl["invdic"].wrote(t)
        t = ve(lambda v: v.memset(cols["onecol"][:, :], 1.0), Wd("onecol"))
        sl["onecol"].wrote(t)
        t = ve(lambda v: v.memset(cols["nhalfc"][:, :], -0.5), Wd("nhalfc"))
        sl["nhalfc"].wrote(t)
        t = g_dma(idf_s[:, :], cidentd[0:32, 0:32], Wd("idf"))
        sl["idf"].wrote(t)
        for mk in range(68):
            t = ve((lambda mk: lambda v: v.tensor_scalar(
                cwdg[:, mk * 128:(mk + 1) * 128], idb_s[:, :], cw_s[:, mk:mk + 1],
                None, ALU.mult))(mk),
                R("idb") + R("cw") + (Wd("cwdg") if mk == 0 else []))
        sl["cwdg"].wrote(t)

        def load_xbf(c):
            src_ = bass.AP(xin.ap().tensor, c * CH * DM, [[DM, 64], [1, DM]])
            t1 = g_dma(xbf1[0:64, :], src_, sl["xbf1"].wd())
            src_ = bass.AP(xin.ap().tensor, L * DM + c * CH * DM, [[DM, 64], [1, DM]])
            t2 = g_dma(xbf1[64:128, :], src_, [])
            sl["xbf1"].wrote_many([t1, t2])

        def load_x(c):
            p2 = c % 2
            tl = []
            for s_ in range(2):
                src = bass.AP(xin.ap().tensor, s_ * L * DM + c * CH * DM, [[DM, 64], [1, DM]])
                tl.append(q_dma(xtm[p2][s_ * 64:(s_ + 1) * 64, :], src,
                                sl["xtm"][p2].wd() if s_ == 0 else []))
            sl["xtm"][p2].wrote_many(tl)


        def xprep(c):
            for k in range(8):
                t = te((lambda k: lambda tt: tt.transpose(
                    ps_tb2[:, k * 128:(k + 1) * 128], xbf1[:, k * 128:(k + 1) * 128],
                    idb_s[:, :]))(k),
                    R("xbf1") + R("idb") + (Wd("ps_tb2") if k == 0 else []))
            sl["ps_tb2"].wrote(t)
            sl["xbf1"].read(t)
            t = ve(lambda v: v.tensor_copy(xfm[:, :], ps_tb2[:, :]),
                   R("ps_tb2") + Wd("xfm"))
            sl["ps_tb2"].read(t)
            sl["xfm"].wrote(t)

        def head(c):
            p = c % 2
            if c + 1 < nch:
                load_xbf(c + 1)

            for mg in range(5):
                bank = mg % 2
                mlist = (list(range((mg - 1) * 4, mg * 4)) if mg > 0 else [16, 17])
                first = True
                for m in mlist:
                    rows = 32 if m == 17 else 128
                    for k in range(8):
                        t = te((lambda m, k, bank, rows: lambda tt: tt.matmul(
                            ps_ip[bank][0:rows, (m % 4) * 128:(m % 4) * 128 + 128],
                            W1s[k][:, DI + m * 128: DI + m * 128 + rows],
                            xfm[:, k * 128:(k + 1) * 128],
                            start=(k == 0), stop=(k == 7)))(m, k, bank, rows),
                            R("xfm") + R("w1") + (Wd(f"ps_ip{bank}") if first else []))
                        first = False
                sl[f"ps_ip{bank}"].wrote(t)
                sl["xfm"].read(t)
                if mlist[-1] <= 15:
                    dst = bass.AP(xbcr.ap().tensor, mlist[0] * 134 + 3,
                                  [[16 * 134, 128], [134, len(mlist)], [67, 2], [1, 64]])
                    src = bass.AP(ps_ip[bank].ap().tensor, 0,
                                  [[512, 128], [128, len(mlist)], [64, 2], [1, 64]])
                    t = ve((lambda dst, src: lambda v: v.tensor_copy(dst, src))(dst, src),
                           R(f"ps_ip{bank}") + Wd("xbcr"))
                    sl[f"ps_ip{bank}"].read(t)
                    sl["xbcr"].wrote(t)
                else:
                    bdst = bass.AP(bcr.ap().tensor, 3, [[134, 128], [67, 2], [1, 64]])
                    bsrc = bass.AP(ps_ip[bank].ap().tensor, 0, [[512, 128], [64, 2], [1, 64]])
                    tse = se((lambda bank: lambda s: s.activation(
                        dtfm[:, :], ps_ip[bank][0:32, 128:256], ACT.Exp,
                        bias=dtb_s[:, 0:1]))(bank),
                        R(f"ps_ip{bank}") + R("dtb") + Wd("dtfm"))
                    sl[f"ps_ip{bank}"].read(tse)
                    sl["dtfm"].wrote(tse)
                    tb = se((lambda bdst, bsrc: lambda s: s.copy(bdst, bsrc))(bdst, bsrc),
                            R(f"ps_ip{bank}") + [tse] + Wd("bcr"))
                    sl[f"ps_ip{bank}"].read(tb)
                    sl["bcr"].wrote(tb)
                    t = se(lambda s: s.activation(dtfm[:, :], dtfm[:, :], ACT.Ln,
                                                  bias=cols["onecol"][0:32, 0:1]),
                           Wd("dtfm") + R("onecol"))
                    sl["dtfm"].wrote(t)

            t = ve(lambda v: v.tensor_scalar(dafm[:, :], dtfm[:, :], acol[:, 0:1], None, ALU.mult),
                   R("dtfm") + R("acol") + Wd("dafm"))
            sl["dafm"].wrote(t)
            t = se(lambda s: s.activation(edafm[:, :], dafm[:, :], ACT.Exp),
                   R("dafm") + Wd("edafm"))
            sl["edafm"].wrote(t)

            hd = bass.AP(xbcr.ap().tensor, 0, [[16 * 134, 128], [134, 16], [67, 2], [1, 3]])
            hd2 = bass.AP(bcr.ap().tensor, 0, [[134, 128], [67, 2], [1, 3]])
            if c > 0:
                hs = bass.AP(tails.ap().tensor, 0, [[102, 128], [1, 96]])
                t = ve(lambda v, hd=hd, hs=hs: v.tensor_copy(hd, hs),
                       R("tails") + Wd("xbcr"))
                sl["tails"].read(t)
                sl["xbcr"].wrote(t)
                hs2 = bass.AP(tails.ap().tensor, 96, [[102, 128], [1, 6]])
                t = ve(lambda v, hd2=hd2, hs2=hs2: v.tensor_copy(hd2, hs2),
                       R("tails") + Wd("bcr"))
                sl["tails"].read(t)
                sl["bcr"].wrote(t)
            else:
                t = ve(lambda v, hd=hd: v.memset(hd, 0.0), Wd("xbcr"))
                sl["xbcr"].wrote(t)
                t = ve(lambda v, hd2=hd2: v.memset(hd2, 0.0), Wd("bcr"))
                sl["bcr"].wrote(t)

            for cg in range(5):
                bank = (cg + 1) % 2
                mlist = list(range(cg * 4, min(17, cg * 4 + 4)))
                first = True
                for m in mlist:
                    srcb = bcr if m == 16 else xbcr
                    off = 0 if m == 16 else m * 134
                    sn = "bcr" if m == 16 else "xbcr"
                    pstep = 134 if m == 16 else 16 * 134
                    for k in range(4):
                        win = bass.AP(srcb.ap().tensor, off + k, [[pstep, 128], [67, 2], [1, 64]])
                        t = te((lambda m, k, bank, win: lambda tt: tt.matmul(
                            ps_ip[bank][:, (m % 4) * 128:(m % 4) * 128 + 128],
                            cwdg[:, (4 * m + k) * 128:(4 * m + k + 1) * 128], win,
                            start=(k == 0), stop=(k == 3)))(m, k, bank, win),
                            R(sn) + R("cwdg") + (Wd(f"ps_ip{bank}") if first else []))
                        first = False
                sl[f"ps_ip{bank}"].wrote(t)
                sl["xbcr"].read(t)
                sl["bcr"].read(t)
                if mlist[-1] != 16:
                    cbv = cb_s[:, mlist[0]:mlist[0] + 1]
                    for m in mlist:
                        t = se((lambda m, bank: lambda s: s.activation(
                            xsconv[:, m * 128:(m + 1) * 128],
                            ps_ip[bank][:, (m % 4) * 128:(m % 4) * 128 + 128],
                            ACT.Silu, bias=cb_s[:, m:m + 1]))(m, bank),
                            R(f"ps_ip{bank}") + R("cb") + (Wd("xsconv") if m == 0 else []))
                        if m == 0:
                            sl["xsconv"].wrote(t)
                        sl[f"ps_ip{bank}"].read(t)
                else:
                    for m in mlist:
                        dstt, dn = (bcc, "bcc") if m == 16 else (xsconv, "xsconv")
                        t = se((lambda m, bank, dstt: lambda s: s.activation(
                            dstt[:, 0:128] if dstt is bcc else dstt[:, m * 128:(m + 1) * 128],
                            ps_ip[bank][:, (m % 4) * 128:(m % 4) * 128 + 128],
                            ACT.Silu, bias=cb_s[:, m:m + 1]))(m, bank, dstt),
                            R(f"ps_ip{bank}") + R("cb") + (Wd(dn) if dstt is bcc else []))
                        if dstt is bcc:
                            sl["bcc"].wrote(t)
                        sl[f"ps_ip{bank}"].read(t)
            sl["xsconv"].wrote(("SE", P.cnt["SE"]))
            ts_src = bass.AP(xbcr.ap().tensor, 64, [[16 * 134, 128], [134, 16], [67, 2], [1, 3]])
            ts_dst = bass.AP(tails.ap().tensor, 0, [[102, 128], [1, 96]])
            t = ve(lambda v, a=ts_dst, b=ts_src: v.tensor_copy(a, b),
                   R("xbcr") + Wd("tails"))
            sl["xbcr"].read(t)
            sl["tails"].wrote(t)
            tb_src = bass.AP(bcr.ap().tensor, 64, [[134, 128], [67, 2], [1, 3]])
            tb_dst = bass.AP(tails.ap().tensor, 96, [[102, 128], [1, 6]])
            t = ve(lambda v, a=tb_dst, b=tb_src: v.tensor_copy(a, b),
                   R("bcr") + Wd("tails"))
            sl["bcr"].read(t)
            sl["tails"].wrote(t)

            for (srct, sn, dn, dstt) in ((dtfm, "dtfm", "dttm", dttm),
                                         (dafm, "dafm", "datm", datm)):
                t = te((lambda srct: lambda tt: tt.transpose(
                    ps_eb[0][:, 0:32], srct[:, :], idf_s[:, :]))(srct),
                    R(sn) + R("idf") + Wd("ps_eb0"))
                sl["ps_eb0"].wrote(t)
                sl[sn].read(t)
                t = ve((lambda dstt: lambda v: v.tensor_copy(dstt[:, :], ps_eb[0][:, 0:32]))(dstt),
                       R("ps_eb0") + Wd(dn))
                sl["ps_eb0"].read(t)
                sl[dn].wrote(t)

            for s_ in range(2):
                t = te((lambda s_: lambda tt: tt.matmul(
                    ps_eb[1][s_ * 32:(s_ + 1) * 32, 0:64],
                    datm[s_ * 64:(s_ + 1) * 64, :],
                    tril_s[s_ * 64:(s_ + 1) * 64, :],
                    start=True, stop=True))(s_),
                    R("datm") + R("tril") + (Wd("ps_eb1") if s_ == 0 else []))
            sl["ps_eb1"].wrote(t)
            sl["datm"].read(t)
            t = se(lambda s: s.activation(eacshl[:, :], ps_eb[1][0:64, 0:64], ACT.Exp),
                   R("ps_eb1") + Wd("eacshl"))
            sl["ps_eb1"].read(t)
            sl["eacshl"].wrote(t)

            fl = flats2[p]
            ftoks = []
            for s_ in range(2):
                src = bass.AP(edafm.ap().tensor, s_ * 64, [[128, 32], [1, 64]])
                dst = bass.AP(fl.ap().tensor, s_ * 2048, [[2048, 1], [64, 32], [1, 64]])
                t = g_dma(dst, src, R("edafm") + (sl["flats"][p].wd() if s_ == 0 else []))
                sl["edafm"].read(t)
                ftoks.append(t)
            for s_ in range(2):
                src = bass.AP(eacshl.ap().tensor, s_ * 32 * 64, [[64, 32], [1, 64]])
                dst = bass.AP(fl.ap().tensor, (32 + s_) * 2048,
                              [[2048, 1], [64, 32], [1, 64]])
                t = g_dma(dst, src, R("eacshl"))
                sl["eacshl"].read(t)
                ftoks.append(t)
            sl["flats"][p].wrote_many(ftoks)

            t = te(lambda tt: tt.transpose(ps_tb2[:, 0:64], bcc[0:64, 0:128], idb_s[0:64, 0:64]),
                   R("bcc") + R("idb") + Wd("ps_tb2"))
            sl["ps_tb2"].wrote(t)
            t = te(lambda tt: tt.transpose(ps_tb2[:, 64:128], bcc[64:128, 0:128],
                                           idb_s[64:128, 64:128]),
                   R("bcc") + Wd("ps_tb2"))
            sl["ps_tb2"].wrote(t)
            sl["bcc"].read(t)
            t = ve(lambda v: v.tensor_copy(btm[:, :], ps_tb2[:, 0:64]),
                   R("ps_tb2") + Wd("btm"))
            sl["ps_tb2"].read(t)
            sl["btm"].wrote(t)
            t = ve(lambda v: v.tensor_copy(cttm[:, :], ps_tb2[:, 64:128]),
                   R("ps_tb2") + Wd("cttm"))
            sl["ps_tb2"].read(t)
            sl["cttm"].wrote(t)
            for (srct, sn, dn, dstt) in ((btm, "btm", "bfm2", bfm2), (cttm, "cttm", "cfm2", cfm2)):
                for s_ in range(2):
                    t = te((lambda srct, s_: lambda tt: tt.transpose(
                        ps_tb2[s_ * 64:(s_ + 1) * 64, 128:192],
                        srct[s_ * 64:(s_ + 1) * 64, 0:64],
                        idb_s[s_ * 64:(s_ + 1) * 64, s_ * 64:(s_ + 1) * 64]))(srct, s_),
                        R(sn) + (Wd("ps_tb2") if s_ == 0 else []))
                sl["ps_tb2"].wrote(t)
                sl[sn].read(t)
                t = ve((lambda dstt: lambda v: v.tensor_copy(dstt[:, :], ps_tb2[:, 128:192]))(dstt),
                       R("ps_tb2") + Wd(dn))
                sl["ps_tb2"].read(t)
                sl[dn].wrote(t)
            for s_ in range(2):
                t = te((lambda s_: lambda tt: tt.matmul(
                    ps_ip[0][s_ * 64:(s_ + 1) * 64, 0:64],
                    bfm2[s_ * 64:(s_ + 1) * 64, :],
                    cfm2[s_ * 64:(s_ + 1) * 64, :],
                    start=True, stop=True))(s_),
                    R("bfm2") + R("cfm2") + (Wd("ps_ip0") if s_ == 0 else []))
            sl["ps_ip0"].wrote(t)
            sl["bfm2"].read(t)
            sl["cfm2"].read(t)
            t = ve(lambda v: v.tensor_tensor(scoresm[:, :], ps_ip[0][:, 0:64],
                                             triu_sb[:, :], ALU.mult),
                   R("ps_ip0") + R("triu") + Wd("scoresm"))
            sl["ps_ip0"].read(t)
            sl["scoresm"].wrote(t)

            for hhalf in range(2):
                pst = ps_tb if hhalf == 0 else ps_tb2
                pn = "ps_tb" if hhalf == 0 else "ps_tb2"
                for j in range(8):
                    m = hhalf * 8 + j
                    t = te((lambda m, j, pst: lambda tt: tt.transpose(
                        pst[:, j * 128:(j + 1) * 128],
                        xsconv[:, m * 128:(m + 1) * 128], idb_s[:, :]))(m, j, pst),
                        R("xsconv") + (Wd(pn) if j == 0 else []))
                sl[pn].wrote(t)
                sl["xsconv"].read(t)
                t = ve((lambda hhalf, pst: lambda v: v.tensor_copy(
                    xstm[:, hhalf * 1024:(hhalf + 1) * 1024], pst[:, :]))(hhalf, pst),
                    R(pn) + (Wd("xstm") if hhalf == 0 else []))
                sl[pn].read(t)
                if hhalf == 0:
                    sl["xstm"].wrote(t)
                dtv = bass.AP(dttm.ap().tensor, hhalf * 16, [[32, 128], [1, 16], [0, 64]])
                t = ve((lambda hhalf, pst, dtv: lambda v: v.tensor_tensor(
                    xdtm[:, hhalf * 1024:(hhalf + 1) * 1024], pst[:, :], dtv, ALU.mult))(hhalf, pst, dtv),
                    R(pn) + R("dttm") + (Wd("xdtm") if hhalf == 0 else []))
                sl[pn].read(t)
                sl["dttm"].read(t)
                if hhalf == 0:
                    sl["xdtm"].wrote(t)
            sl["xstm"].wrote(("VE", P.cnt["VE"]))
            sl["xdtm"].wrote(("VE", P.cnt["VE"]))

        def body(c):
            p = c % 2
            q_ = 1 - p
            if c > 0:
                edvv = bass.AP(edec[q_].ap().tensor, 0, [[32, 128], [1, 32], [0, 64]])
                t = ve(lambda v, edvv=edvv: v.tensor_tensor(yn[:, :], tstate[:, :], edvv, ALU.mult),
                       R("tstate") + sl["edec"][q_].rd() + Wd("yn"))
                sl["tstate"].read(t)
                sl["edec"][q_].read(t)
                sl["yn"].wrote(t)
                t = ve(lambda v: v.tensor_tensor(tstate[:, :], yn[:, :], stsb[:, :], ALU.add),
                       R("yn") + R("stsb") + Wd("tstate"))
                sl["yn"].read(t)
                sl["stsb"].read(t)
                sl["tstate"].wrote(t)
            for gz in range(4):
                bank = gz % 2
                for k in range(8):
                    t = te((lambda gz, k, bank: lambda tt: tt.matmul(
                        ps_ip[bank][:, 0:512],
                        xfm[:, k * 128:(k + 1) * 128],
                        W1s[k][:, gz * 512:(gz + 1) * 512],
                        start=(k == 0), stop=(k == 7)))(gz, k, bank),
                        R("xfm") + R("w1") + (Wd(f"ps_ip{bank}") if k == 0 else []))
                sl[f"ps_ip{bank}"].wrote(t)
                sl["xfm"].read(t)
                t = se((lambda gz, bank: lambda s: s.activation(
                    zact[:, gz * 512:(gz + 1) * 512], ps_ip[bank][:, 0:512], ACT.Silu))(gz, bank),
                    R(f"ps_ip{bank}") + (Wd("zact") if gz == 0 else []))
                sl[f"ps_ip{bank}"].read(t)
            sl["zact"].wrote(t)

            fl = flats2[p]
            prev = None
            for gq_ in range(4):
                bank = gq_ % 2
                t = te((lambda gq_, bank, fl: lambda tt: tt.matmul(
                    ps_eb[bank][:, 0:512], seqm_s[0:2, :],
                    fl[0:2, gq_ * 512:(gq_ + 1) * 512],
                    start=True, stop=True))(gq_, bank, fl),
                    sl["flats"][p].rd() + R("seqm") + Wd(f"ps_eb{bank}"))
                sl[f"ps_eb{bank}"].wrote(t)
                sl["flats"][p].read(t)
                t = ve((lambda gq_, bank: lambda v: v.tensor_tensor_scan(
                    escan[:, gq_ * 512:(gq_ + 1) * 512], ps_eb[bank][:, 0:512],
                    i2r_s[:, 0:512], 0.0, ALU.mult, ALU.max))(gq_, bank),
                    R(f"ps_eb{bank}") + R("i2r") + Wd("escan"))
                sl[f"ps_eb{bank}"].read(t)
                sl["escan"].wrote(t)
                prev = t
            sl["escan"].wrote(prev)

            scv = bass.AP(scoresm.ap().tensor, 0, [[64, 128], [0, 32], [1, 64]])
            t = ve(lambda v, scv=scv: v.tensor_tensor(slb[:, :], escan[:, :], scv, ALU.mult),
                   R("escan") + R("scoresm") + Wd("slb"))
            sl["escan"].read(t)
            sl["scoresm"].read(t)
            sl["slb"].wrote(t)

            for gq_ in range(4):
                bank = gq_ % 2
                t = te((lambda gq_, bank, fl: lambda tt: tt.matmul(
                    ps_eb[bank][:, 0:512], seqm_s[32:34, :],
                    fl[32:34, gq_ * 512:(gq_ + 1) * 512],
                    start=True, stop=True))(gq_, bank, fl),
                    sl["flats"][p].rd() + R("seqm") + Wd(f"ps_eb{bank}"))
                sl[f"ps_eb{bank}"].wrote(t)
                sl["flats"][p].read(t)
                t = ve((lambda gq_, bank: lambda v: v.tensor_copy(
                    expacsbc[:, gq_ * 512:(gq_ + 1) * 512], ps_eb[bank][:, 0:512]))(gq_, bank),
                    R(f"ps_eb{bank}") + (Wd("expacsbc") if gq_ == 0 else []))
                sl[f"ps_eb{bank}"].read(t)
            sl["expacsbc"].wrote(t)
            edv = bass.AP(expacsbc.ap().tensor, 63, [[2048, 128], [64, 32]])
            t = ve((lambda p, edv: lambda v: v.tensor_copy(edec[p][:, :], edv))(p, edv),
                   R("expacsbc") + sl["edec"][p].wd())
            sl["expacsbc"].read(t)
            sl["edec"][p].wrote(t)
            cdv = bass.AP(cfm2.ap().tensor, 0, [[64, 128], [0, 32], [1, 64]])
            t = ve(lambda v, cdv=cdv: v.tensor_tensor(cdt[:, :], expacsbc[:, :], cdv, ALU.mult),
                   R("expacsbc") + R("cfm2") + Wd("cdt"))
            sl["expacsbc"].read(t)
            sl["cfm2"].read(t)
            sl["cdt"].wrote(t)

            b0 = bass.AP(btm.ap().tensor, 0, [[64, 128], [0, 32], [1, 64]])
            ecol = bass.AP(escan.ap().tensor, 63, [[2048, 128], [64, 32], [0, 64]])
            t = ve(lambda v, b0=b0, ecol=ecol: v.tensor_tensor(bdec[:, :], b0, ecol, ALU.mult),
                   R("btm") + R("escan") + Wd("bdec"))
            sl["btm"].read(t)
            sl["escan"].read(t)
            sl["bdec"].wrote(t)

            for quarter in range(4):
                ybank = quarter % 2
                first = True
                for hh in range(quarter * 8, quarter * 8 + 8):
                    for s_ in range(2):
                        r0 = s_ * 64
                        fo = (hh % 8) * 64
                        deps = (R("slb") + R("xdtm") + R("tstate") + R("cdt") +
                                R("xstm") + R("di") + (Wd(f"ps_y{ybank}") if first else []))
                        te((lambda hh, r0, fo, ybank: lambda tt: tt.matmul(
                            ps_y[ybank][r0:r0 + 64, fo:fo + 64],
                            slb[r0:r0 + 64, hh * 64:(hh + 1) * 64],
                            xdtm[r0:r0 + 64, hh * 64:(hh + 1) * 64],
                            start=True, stop=False))(hh, r0, fo, ybank), deps)
                        first = False
                        te((lambda hh, r0, fo, ybank: lambda tt: tt.matmul(
                            ps_y[ybank][r0:r0 + 64, fo:fo + 64],
                            cdt[r0:r0 + 64, hh * 64:(hh + 1) * 64],
                            tstate[r0:r0 + 64, hh * 64:(hh + 1) * 64],
                            start=False, stop=False))(hh, r0, fo, ybank), [])
                        te((lambda hh, r0, fo, ybank: lambda tt: tt.matmul(
                            ps_y[ybank][r0:r0 + 64, fo:fo + 64],
                            di_s[r0:r0 + 64, hh * 64:(hh + 1) * 64],
                            xstm[r0:r0 + 64, hh * 64:(hh + 1) * 64],
                            start=False, stop=True))(hh, r0, fo, ybank), [])
                tok = ("TE", P.cnt["TE"])
                sl[f"ps_y{ybank}"].wrote(tok)
                for n in ["slb", "xdtm", "tstate", "cdt", "xstm"]:
                    sl[n].read(tok)
                t = ve((lambda quarter, ybank: lambda v: v.tensor_tensor(
                    slb[:, quarter * 512:(quarter + 1) * 512], ps_y[ybank][:, 0:512],
                    zact[:, quarter * 512:(quarter + 1) * 512], ALU.mult))(quarter, ybank),
                    R(f"ps_y{ybank}") + R("zact") + Wd("slb"))
                sl[f"ps_y{ybank}"].read(t)
                sl["zact"].read(t)
                sl["slb"].wrote(t)

            first_st = True
            for hhalf in range(2):
                for bank in range(2):
                    first = True
                    for hh in range(hhalf * 16 + bank * 8, hhalf * 16 + bank * 8 + 8):
                        for s_ in range(2):
                            r0 = s_ * 64
                            fo = (hh % 8) * 64
                            te((lambda hh, r0, fo, bank: lambda tt: tt.matmul(
                                ps_st[bank][r0:r0 + 64, fo:fo + 64],
                                bdec[r0:r0 + 64, hh * 64:(hh + 1) * 64],
                                xdtm[r0:r0 + 64, hh * 64:(hh + 1) * 64],
                                start=True, stop=True))(hh, r0, fo, bank),
                                R("bdec") + R("xdtm") + (Wd(f"ps_st{bank}") if first else []))
                            first = False
                    tok = ("TE", P.cnt["TE"])
                    sl[f"ps_st{bank}"].wrote(tok)
                    sl["bdec"].read(tok)
                    sl["xdtm"].read(tok)
                    t = ve((lambda hhalf, bank: lambda v: v.tensor_copy(
                        stsb[:, hhalf * 1024 + bank * 512:hhalf * 1024 + bank * 512 + 512],
                        ps_st[bank][:, 0:512]))(hhalf, bank),
                        R(f"ps_st{bank}") + (Wd("stsb") if first_st else []))
                    first_st = False
                    sl[f"ps_st{bank}"].read(t)
            sl["stsb"].wrote(t)

            if c > 0:
                tail_ln(c - 1)
            if c + 1 < nch:
                load_x(c + 1)
                xprep(c + 1)

            for hhalf in range(2):
                t = se((lambda hhalf: lambda s: s.activation(
                    escan[:, 0:DM], slb[:, hhalf * 1024:(hhalf + 1) * 1024], ACT.Square,
                    accum_out=(cols["ssq"] if hhalf == 0 else cols["ssq2"])[:, 0:1]))(hhalf),
                    Wd("escan") + R("slb") + Wd("ssq" if hhalf == 0 else "ssq2"))
                sl["slb"].read(t)
                sl["escan"].wrote(t)
                sl["ssq" if hhalf == 0 else "ssq2"].wrote(t)
            t = ve(lambda v: v.tensor_tensor(cols["ssq"][:, 0:1], cols["ssq"][:, 0:1],
                                             cols["ssq2"][:, 0:1], ALU.add),
                   Wd("ssq") + R("ssq2"))
            sl["ssq"].wrote(t)
            t = se(lambda s: s.activation(cols["lnv"][:, 0:1], cols["ssq"][:, 0:1], ACT.Ln,
                                          bias=cols["epsc"][:, 0:1], scale=cols["invdic"][:, 0:1]),
                   R("ssq") + R("epsc") + R("invdic") + Wd("lnv"))
            sl["lnv"].wrote(t)
            t = se(lambda s: s.activation(cols["rsc"][:, 0:1], cols["lnv"][:, 0:1], ACT.Exp,
                                          scale=cols["nhalfc"][:, 0:1]),
                   R("lnv") + R("nhalfc") + Wd("rsc"))
            sl["rsc"].wrote(t)
            t = se(lambda s: s.mul(yn[:, :], slb[:, :], cols["rsc"][:, 0:1]),
                   R("slb") + R("rsc") + Wd("yn"))
            sl["slb"].read(t)
            sl["yn"].wrote(t)

        def tail_mm(c):
            p = c % 2
            for hhalf in range(2):
                pst = ps_tb if hhalf == 0 else ps_tb2
                pn = "ps_tb" if hhalf == 0 else "ps_tb2"
                for j in range(8):
                    t = te((lambda hhalf, j, pst: lambda tt: tt.transpose(
                        pst[:, j * 128:(j + 1) * 128],
                        yn[:, (hhalf * 8 + j) * 128:(hhalf * 8 + j + 1) * 128],
                        idb_s[:, :]))(hhalf, j, pst),
                        R("yn") + (Wd(pn) if j == 0 else []))
                sl[pn].wrote(t)
                sl["yn"].read(t)
                t = ve((lambda hhalf, pst: lambda v: v.tensor_copy(
                    cdt[:, hhalf * 1024:(hhalf + 1) * 1024], pst[:, :]))(hhalf, pst),
                    R(pn) + Wd("cdt"))
                sl[pn].read(t)
                sl["cdt"].wrote(t)

            t_h = None
            for nh in range(2):
                bank = nh
                for k in range(16):
                    t = te((lambda nh, k, bank: lambda tt: tt.matmul(
                        ps_y[bank][:, 0:512],
                        cdt[:, k * 128:(k + 1) * 128],
                        W2s[k][:, nh * 512:(nh + 1) * 512],
                        start=(k == 0), stop=(k == 15)))(nh, k, bank),
                        R("cdt") + R("w2") + (Wd(f"ps_y{bank}") if k == 0 else []))
                sl[f"ps_y{bank}"].wrote(t)
                sl["cdt"].read(t)
                t_h = ve((lambda nh, bank, p: lambda v: v.tensor_tensor(
                    h1[:, nh * 512:(nh + 1) * 512], ps_y[bank][:, 0:512],
                    xtm[p][:, nh * 512:(nh + 1) * 512], ALU.add))(nh, bank, p),
                    R(f"ps_y{bank}") + sl["xtm"][p].rd() + (Wd("h1") if nh == 0 else []))
                sl[f"ps_y{bank}"].read(t_h)
                sl["xtm"][p].read(t_h)
            sl["h1"].wrote(t_h)

        def tail_ln(c):
            p = c % 2
            t = se(lambda s: s.activation(escan[:, 0:DM], h1[:, :], ACT.Identity,
                                          accum_out=cols["hsum"][:, 0:1]),
                   R("h1") + Wd("escan") + Wd("hsum"))
            sl["h1"].read(t)
            sl["escan"].wrote(t)
            sl["hsum"].wrote(t)
            t = se(lambda s: s.activation(escan[:, 0:DM], h1[:, :], ACT.Square,
                                          accum_out=cols["hsq"][:, 0:1]),
                   R("h1") + Wd("escan") + Wd("hsq"))
            sl["h1"].read(t)
            sl["escan"].wrote(t)
            sl["hsq"].wrote(t)
            t = P.emit_gc(lambda g_: g_.tensor_scalar(cols["muc"][:, 0:1], cols["hsum"][:, 0:1],
                                               1.0 / DM, None, ALU.mult),
                   R("hsum") + Wd("muc"))
            sl["muc"].wrote(t)
            t = P.emit_gc(lambda g_: g_.tensor_tensor(cols["varc"][:, 0:1], cols["muc"][:, 0:1],
                                               cols["muc"][:, 0:1], ALU.mult),
                   R("muc") + Wd("varc"))
            sl["varc"].wrote(t)
            t = P.emit_gc(lambda g_: g_.tensor_scalar(cols["hsqn"][:, 0:1], cols["hsq"][:, 0:1],
                                               1.0 / DM, None, ALU.mult),
                   R("hsq") + Wd("hsqn"))
            sl["hsqn"].wrote(t)
            t = P.emit_gc(lambda g_: g_.tensor_tensor(cols["varc"][:, 0:1], cols["hsqn"][:, 0:1],
                                               cols["varc"][:, 0:1], ALU.subtract),
                   R("hsqn") + Wd("varc"))
            sl["varc"].wrote(t)
            t = se(lambda s: s.activation(cols["lnv"][:, 0:1], cols["varc"][:, 0:1],
                                          ACT.Ln, bias=cols["epsc"][:, 0:1]),
                   R("varc") + R("epsc") + Wd("lnv"))
            sl["lnv"].wrote(t)
            t = se(lambda s: s.activation(cols["rsc"][:, 0:1], cols["lnv"][:, 0:1],
                                          ACT.Exp, scale=cols["nhalfc"][:, 0:1]),
                   R("lnv") + R("nhalfc") + Wd("rsc"))
            sl["rsc"].wrote(t)
            t = P.emit_gc(lambda g_: g_.tensor_scalar(cols["negmu"][:, 0:1], cols["muc"][:, 0:1],
                                               -1.0, None, ALU.mult),
                   R("muc") + Wd("negmu"))
            sl["negmu"].wrote(t)
            t = P.emit_gc(lambda g_: g_.tensor_tensor(cols["nmr"][:, 0:1], cols["negmu"][:, 0:1],
                                               cols["rsc"][:, 0:1], ALU.mult),
                   R("negmu") + R("rsc") + Wd("nmr"))
            sl["nmr"].wrote(t)
            t = se(lambda s: s.activation(a2[:, :], h1[:, :], ACT.Identity,
                                          bias=cols["nmr"][:, 0:1], scale=cols["rsc"][:, 0:1]),
                   R("h1") + R("nmr") + R("rsc") + Wd("a2"))
            sl["h1"].read(t)
            sl["a2"].wrote(t)
            t = ve(lambda v: v.tensor_tensor(a2[:, :], a2[:, :], gbc[:, :], ALU.mult),
                   Wd("a2") + R("gbc"))
            sl["gbc"].read(t)
            sl["a2"].wrote(t)
            t = ve(lambda v: v.tensor_tensor(a2[:, :], a2[:, :], bbc[:, :], ALU.add),
                   Wd("a2") + R("bbc"))
            sl["bbc"].read(t)
            sl["a2"].wrote(t)
            t = ve((lambda p: lambda v: v.scalar_tensor_tensor(
                vv[:, :], a2[:, :], 0.0, xtm[p][:, :], ALU.add, ALU.add,
                accum_out=cols["vsum"][:, 0:1]))(p),
                R("a2") + sl["xtm"][p].rd() + Wd("vv") + Wd("vsum"))
            sl["a2"].read(t)
            sl["xtm"][p].read(t)
            sl["vv"].wrote(t)
            sl["vsum"].wrote(t)
            t = se(lambda s: s.activation(escan[:, 0:DM], vv[:, :], ACT.Square,
                                          accum_out=cols["vsq"][:, 0:1]),
                   R("vv") + Wd("escan") + Wd("vsq"))
            sl["vv"].read(t)
            sl["escan"].wrote(t)
            sl["vsq"].wrote(t)
            t = P.emit_gc(lambda g_: g_.tensor_scalar(cols["mu2"][:, 0:1], cols["vsum"][:, 0:1],
                                               1.0 / DM, None, ALU.mult),
                   R("vsum") + Wd("mu2"))
            sl["mu2"].wrote(t)
            t = P.emit_gc(lambda g_: g_.tensor_tensor(cols["var2"][:, 0:1], cols["mu2"][:, 0:1],
                                               cols["mu2"][:, 0:1], ALU.mult),
                   R("mu2") + Wd("var2"))
            sl["var2"].wrote(t)
            t = P.emit_gc(lambda g_: g_.tensor_scalar(cols["hsqn"][:, 0:1], cols["vsq"][:, 0:1],
                                               1.0 / DM, None, ALU.mult),
                   R("vsq") + Wd("hsqn"))
            sl["hsqn"].wrote(t)
            t = P.emit_gc(lambda g_: g_.tensor_tensor(cols["var2"][:, 0:1], cols["hsqn"][:, 0:1],
                                               cols["var2"][:, 0:1], ALU.subtract),
                   R("hsqn") + Wd("var2"))
            sl["var2"].wrote(t)
            t = se(lambda s: s.activation(cols["sdc"][:, 0:1], cols["var2"][:, 0:1],
                                          ACT.Sqrt, bias=cols["epsc"][:, 0:1]),
                   R("var2") + R("epsc") + Wd("sdc"))
            sl["sdc"].wrote(t)
            t = ve(lambda v: v.reciprocal(cols["rs2"][:, 0:1], cols["sdc"][:, 0:1]),
                   R("sdc") + Wd("rs2"))
            sl["rs2"].wrote(t)
            t = P.emit_gc(lambda g_: g_.tensor_scalar(cols["negmu"][:, 0:1], cols["mu2"][:, 0:1],
                                               -1.0, None, ALU.mult),
                   R("mu2") + Wd("negmu"))
            sl["negmu"].wrote(t)
            t = P.emit_gc(lambda g_: g_.tensor_tensor(cols["nmr2"][:, 0:1], cols["negmu"][:, 0:1],
                                               cols["rs2"][:, 0:1], ALU.mult),
                   R("negmu") + R("rs2") + Wd("nmr2"))
            sl["nmr2"].wrote(t)
            t = se(lambda s: s.activation(x1t[:, :], vv[:, :], ACT.Identity,
                                          bias=cols["nmr2"][:, 0:1], scale=cols["rs2"][:, 0:1]),
                   R("vv") + R("nmr2") + R("rs2") + Wd("x1t"))
            sl["vv"].read(t)
            sl["x1t"].wrote(t)
            t = ve(lambda v: v.reduce_max(amtmp[:, 0:1], x1t[:, :], axis=AX.X,
                                          apply_absolute_value=True),
                   R("x1t") + Wd("amtmp"))
            sl["x1t"].read(t)
            sl["amtmp"].wrote(t)
            ai, aj = c % 2, 1 - (c % 2)
            t = ve((lambda ai, aj: lambda v: v.tensor_tensor(
                amaxc[ai][:, 0:1], amaxc[aj][:, 0:1], amtmp[:, 0:1], ALU.max))(ai, aj),
                sl["amaxc"][aj].rd() + R("amtmp") + sl["amaxc"][ai].wd())
            sl["amaxc"][aj].read(t)
            sl["amtmp"].read(t)
            sl["amaxc"][ai].wrote(t)
            t = g_dma(x1_hbm[c * 128:(c + 1) * 128, :], x1t[:, :], R("x1t"))
            sl["x1t"].read(t)

        load_xbf(0)
        load_x(0)
        xprep(0)
        head(0)
        for c in range(nch):
            body(c)
            if c + 1 < nch:
                head(c + 1)
            tail_mm(c)
        tail_ln(nch - 1)

        lastp = (nch - 1) % 2
        t = g_dma(cc_in.ap(), amaxc[lastp][:, 0:1], sl["amaxc"][lastp].rd())
        alldone = ([("TE", P.cnt["TE"]), ("VE", P.cnt["VE"]), ("SE", P.cnt["SE"]), t]
                   + P.qdone("SQ") + P.qdone("GQ"))
        tokcc = P.emit_cc(lambda g: g.collective_compute(
            "AllReduce", ALU.max, replica_groups=[list(range(8))],
            ins=[cc_in.ap()], outs=[cc_out.ap()]), [t])
        t = g_dma(bass.AP(amrow.ap().tensor, 0, [[128, 1], [1, 128]]),
                  bass.AP(cc_out.ap().tensor, 0, [[128, 1], [1, 128]]), [tokcc])
        sl["amrow"].wrote(t)
        bar = [tokcc, t] + alldone

        es1.close()

        sqd = [sb(es, f"sqd{i}", [128, DM], BF16) for i in range(2)]
        c2 = [{}, {}]
        for pgi in range(2):
            for n in ["vsum", "vsq", "mu2", "var2", "rs2", "nmr2", "sdc"]:
                c2[pgi][n] = sb(es, f"p2_{n}{pgi}", [128, 1], F32)
        s2c = [{n: Slot() for n in c2[0]} for _ in range(2)]
        s2q = [Slot(), Slot()]
        w3full = [sb(es, f"w3f{k}", [128, DM], F32) for k in range(8)]
        w3q = [sb(es, f"w3q{k}", [128, DM], BF16) for k in range(8)]
        x1ld = [sb(es, f"x1ld{i}", [128, DM], F32) for i in range(4)]
        qq = [sb(es, f"qq{i}", [128, DM], F32) for i in range(4)]
        uq = [sb(es, f"uq{i}", [128, DM], F32) for i in range(4)]
        x1q = [sb(es, f"x1q{i}", [128, DM], BF16) for i in range(4)]
        x1qt = [sb(es, f"x1qt{i}", [128, DM], BF16) for i in range(4)]
        f1 = [sb(es, f"f1_{i}", [128, DM], F32) for i in range(2)]
        v2 = [sb(es, f"v2_{i}", [128, DM], F32) for i in range(2)]
        x2 = [sb(es, f"x2_{i}", [128, DM], F32) for i in range(2)]
        onesb = sb(es, "onesb", [1, 128], BF16)
        fbq = sb(es, "fbq", [1, DM], BF16)
        svvc = sb(es, "svvc", [1, 1], F32)
        fbrow = sb(es, "fbrow", [1, DM], F32)
        amg = sb(es, "amg", [1, 1], F32)
        wmrow = sb(es, "wmrow", [1, 128], F32)
        wamax = sb(es, "wamax", [1, 1], F32)
        sxv = sb(es, "sxv", [1, 1], F32)
        swv = sb(es, "swv", [1, 1], F32)
        sxi = sb(es, "sxi", [1, 1], F32)
        swi = sb(es, "swi", [1, 1], F32)
        sxw = sb(es, "sxw", [1, 1], F32)
        onec = sb(es, "onec", [1, 1], F32)
        c448 = sb(es, "c448", [1, 1], F32)
        sxcol = sb(es, "sxcol", [128, 1], F32)
        swcol = sb(es, "swcol", [128, 1], F32)
        sxwcol = sb(es, "sxwcol", [128, 1], F32)
        wmxc = [sb(es, f"wmxc{i}", [128, 1], F32) for i in range(2)]
        for n in ["onesb", "fbq", "svvc", "fbrow", "amg", "wmrow", "wamax",
                  "sxv", "swv", "sxi", "swi", "sxw", "onec", "c448",
                  "sxcol", "swcol", "sxwcol", "w3q"]:
            S(n)
        for n in ["wmxc", "x2", "f1", "v2"]:
            sl[n] = [Slot(), Slot()]
        for n in ["x1ld", "qq", "uq", "x1q", "x1qt"]:
            sl[n] = [Slot() for _ in range(4)]
        sl["w3full"] = [Slot() for _ in range(8)]

        t = ve(lambda v: v.reduce_max(amg[0:1, 0:1], amrow[0:1, :], axis=AX.X),
               bar + R("amrow") + Wd("amg"))
        sl["amg"].wrote(t)
        t = ve(lambda v: v.memset(onec[0:1, 0:1], 1.0), bar + Wd("onec"))
        sl["onec"].wrote(t)
        t = ve(lambda v: v.memset(c448[0:1, 0:1], 448.0), bar + Wd("c448"))
        sl["c448"].wrote(t)
        t = ve(lambda v: v.tensor_scalar(amg[0:1, 0:1], amg[0:1, 0:1], 1e-12, None, ALU.max),
               Wd("amg"))
        sl["amg"].wrote(t)
        t = ve(lambda v: v.reciprocal(sxv[0:1, 0:1], amg[0:1, 0:1]),
               R("amg") + Wd("sxv"))
        sl["sxv"].wrote(t)
        t = ve(lambda v: v.tensor_scalar(sxv[0:1, 0:1], sxv[0:1, 0:1], 448.0, None, ALU.mult),
               Wd("sxv"))
        sl["sxv"].wrote(t)
        t = ve(lambda v: v.tensor_scalar(sxv[0:1, 0:1], sxv[0:1, 0:1], 448.0, None, ALU.min),
               Wd("sxv"))
        sl["sxv"].wrote(t)
        t = ve(lambda v: v.reciprocal(sxi[0:1, 0:1], sxv[0:1, 0:1]),
               R("sxv") + Wd("sxi"))
        sl["sxi"].wrote(t)

        for k in range(8):
            t = q_dma(w3full[k][:, :], w3t[k * 128:(k + 1) * 128, :],
                      bar + sl["w3full"][k].wd())
            sl["w3full"][k].wrote(t)
            t = ve((lambda k: lambda v: v.reduce_max(amtmp[:, 0:1], w3full[k][:, :],
                                                     axis=AX.X, apply_absolute_value=True))(k),
                   sl["w3full"][k].rd() + Wd("amtmp"))
            sl["amtmp"].wrote(t)
            sl["w3full"][k].read(t)
            if k == 0:
                t = ve(lambda v: v.tensor_copy(wmxc[0][:, 0:1], amtmp[:, 0:1]),
                       R("amtmp") + sl["wmxc"][0].wd())
            else:
                ai, aj = k % 2, 1 - k % 2
                t = ve((lambda ai, aj: lambda v: v.tensor_tensor(
                    wmxc[ai][:, 0:1], wmxc[aj][:, 0:1], amtmp[:, 0:1], ALU.max))(ai, aj),
                    sl["wmxc"][aj].rd() + R("amtmp") + sl["wmxc"][ai].wd())
                sl["wmxc"][aj].read(t)
            sl["wmxc"][k % 2].wrote(t)
            sl["amtmp"].read(t)
        t = g_dma(wmx_hbm.ap(), wmxc[1][:, 0:1], sl["wmxc"][1].rd() + bar)
        t = g_dma(bass.AP(wmrow.ap().tensor, 0, [[128, 1], [1, 128]]),
                  bass.AP(wmx_hbm.ap().tensor, 0, [[128, 1], [1, 128]]), [t])
        sl["wmrow"].wrote(t)
        t = ve(lambda v: v.reduce_max(wamax[0:1, 0:1], wmrow[0:1, :], axis=AX.X),
               R("wmrow") + Wd("wamax"))
        sl["wamax"].wrote(t)
        t = ve(lambda v: v.tensor_scalar(wamax[0:1, 0:1], wamax[0:1, 0:1], 1e-12, None, ALU.max),
               Wd("wamax"))
        sl["wamax"].wrote(t)
        t = ve(lambda v: v.reciprocal(swv[0:1, 0:1], wamax[0:1, 0:1]),
               R("wamax") + Wd("swv"))
        sl["swv"].wrote(t)
        t = ve(lambda v: v.tensor_scalar(swv[0:1, 0:1], swv[0:1, 0:1], 448.0, None, ALU.mult),
               Wd("swv"))
        sl["swv"].wrote(t)
        t = ve(lambda v: v.tensor_scalar(swv[0:1, 0:1], swv[0:1, 0:1], 448.0, None, ALU.min),
               Wd("swv"))
        sl["swv"].wrote(t)
        t = ve(lambda v: v.reciprocal(swi[0:1, 0:1], swv[0:1, 0:1]),
               R("swv") + Wd("swi"))
        sl["swi"].wrote(t)
        t = ve(lambda v: v.tensor_tensor(sxw[0:1, 0:1], sxi[0:1, 0:1], swi[0:1, 0:1], ALU.mult),
               R("sxi") + R("swi") + Wd("sxw"))
        sl["sxw"].wrote(t)
        for (src, sn, dst, dn) in ((sxv, "sxv", sxcol, "sxcol"),
                                   (swv, "swv", swcol, "swcol"),
                                   (sxw, "sxw", sxwcol, "sxwcol")):
            t = te((lambda src: lambda tt: tt.matmul(
                ps_ip[0][:, 0:1], ones_s[0:1, :], src[0:1, 0:1],
                start=True, stop=True))(src),
                bar + R(sn) + R("ones") + Wd("ps_ip0"))
            sl["ps_ip0"].wrote(t)
            t = ve((lambda dst: lambda v: v.tensor_copy(dst[:, 0:1], ps_ip[0][:, 0:1]))(dst),
                   R("ps_ip0") + Wd(dn))
            sl["ps_ip0"].read(t)
            sl[dn].wrote(t)

        for k in range(8):
            pk = k % 4
            t = se((lambda k, pk: lambda s: s.activation(
                qq[pk][:, :], w3full[k][:, :], ACT.Identity,
                scale=swcol[:, 0:1]))(k, pk),
                sl["w3full"][k].rd() + R("swcol") + sl["qq"][pk].wd())
            sl["w3full"][k].read(t)
            sl["qq"][pk].wrote(t)
            t = ve((lambda pk: lambda v: v.scalar_tensor_tensor(
                uq[pk][:, :], qq[pk][:, :], C21, qq[pk][:, :],
                ALU.mult, ALU.subtract))(pk),
                sl["qq"][pk].rd() + sl["uq"][pk].wd())
            sl["uq"][pk].wrote(t)
            t = ve((lambda pk, k: lambda v: v.scalar_tensor_tensor(
                w3q[k][:, :], qq[pk][:, :], C21, uq[pk][:, :],
                ALU.mult, ALU.subtract))(pk, k),
                sl["qq"][pk].rd() + sl["uq"][pk].rd() + Wd("w3q"))
            sl["qq"][pk].read(t)
            sl["uq"][pk].read(t)
        sl["w3q"].wrote(t)
        t = g_dma(fbrow[:, :], fbiasd[:, :], bar + Wd("fbrow"))
        sl["fbrow"].wrote(t)
        t = ve(lambda v: v.memset(onesb[0:1, :], 1.0), bar + Wd("onesb"))
        sl["onesb"].wrote(t)
        t = ve(lambda v: v.tensor_tensor(svvc[0:1, 0:1], sxv[0:1, 0:1], swv[0:1, 0:1],
                                         ALU.mult),
               R("sxv") + R("swv") + Wd("svvc"))
        sl["svvc"].wrote(t)
        t = ve(lambda v: v.tensor_scalar(fbq[0:1, :], fbrow[0:1, :], svvc[0:1, 0:1],
                                         None, ALU.mult),
               R("fbrow") + R("svvc") + Wd("fbq"))
        sl["fbq"].wrote(t)

        def x1_load(g):
            pg = g % 4
            t = q_dma(x1ld[pg][:, :], x1_hbm[g * 128:(g + 1) * 128, :],
                      bar + sl["x1ld"][pg].wd())
            sl["x1ld"][pg].wrote(t)

        for g0 in range(4):
            x1_load(g0)

        def quant_a(g):
            pg = g % 4
            t = se((lambda pg: lambda s: s.activation(
                qq[pg][:, :], x1ld[pg][:, :], ACT.Identity,
                scale=sxcol[:, 0:1]))(pg),
                sl["x1ld"][pg].rd() + R("sxcol") + sl["qq"][pg].wd())
            sl["x1ld"][pg].read(t)
            sl["qq"][pg].wrote(t)

        def quant_b(g):
            pg = g % 4
            t = ve((lambda pg: lambda v: v.scalar_tensor_tensor(
                uq[pg][:, :], qq[pg][:, :], C21, qq[pg][:, :],
                ALU.mult, ALU.subtract))(pg),
                sl["qq"][pg].rd() + sl["uq"][pg].wd())
            sl["uq"][pg].wrote(t)
            t = ve((lambda pg: lambda v: v.scalar_tensor_tensor(
                x1q[pg][:, :], qq[pg][:, :], C21, uq[pg][:, :],
                ALU.mult, ALU.subtract))(pg),
                sl["qq"][pg].rd() + sl["uq"][pg].rd() + sl["x1q"][pg].wd())
            sl["qq"][pg].read(t)
            sl["uq"][pg].read(t)
            sl["x1q"][pg].wrote(t)
            pst = ps_tb if (pg % 2) == 0 else ps_tb2
            pn = "ps_tb" if (pg % 2) == 0 else "ps_tb2"
            for j in range(8):
                t = te((lambda j, pg, pst: lambda tt: tt.transpose(
                    pst[:, j * 128:(j + 1) * 128], x1q[pg][:, j * 128:(j + 1) * 128],
                    idb_s[:, :]))(j, pg, pst),
                    sl["x1q"][pg].rd() + (bar if (g == 0 and j == 0) else []) +
                    (Wd(pn) if j == 0 else []))
            sl[pn].wrote(t)
            sl["x1q"][pg].read(t)
            t = se((lambda pg, pst: lambda s: s.copy(x1qt[pg][:, :], pst[:, :]))(pg, pst),
                   sl[pn].rd() + sl["x1qt"][pg].wd())
            sl[pn].read(t)
            sl["x1qt"][pg].wrote(t)

        def ffn_mm(g):
            pg = g % 4
            ph = g % 2
            for nh in range(2):
                bank = nh
                t = te((lambda nh, bank: lambda tt: tt.matmul(
                    ps_y[bank][:, 0:512], onesb[0:1, :],
                    fbq[0:1, nh * 512:(nh + 1) * 512],
                    start=True, stop=False))(nh, bank),
                    R("onesb") + R("fbq") + Wd(f"ps_y{bank}"))
                for k in range(8):
                    t = te((lambda nh, k, bank, pg: lambda tt: tt.matmul(
                        ps_y[bank][:, 0:512],
                        x1qt[pg][:, k * 128:(k + 1) * 128],
                        w3q[k][:, nh * 512:(nh + 1) * 512],
                        start=False, stop=(k == 7)))(nh, k, bank, pg),
                        sl["x1qt"][pg].rd() + R("w3q"))
                sl[f"ps_y{bank}"].wrote(t)
                sl["x1qt"][pg].read(t)
                t = se((lambda nh, bank, ph: lambda s: s.activation(
                    f1[ph][:, nh * 512:(nh + 1) * 512], ps_y[bank][:, 0:512], ACT.Identity,
                    scale=sxwcol[:, 0:1]))(nh, bank, ph),
                    R(f"ps_y{bank}") + R("sxwcol") + (sl["f1"][ph].wd() if nh == 0 else []))
                sl[f"ps_y{bank}"].read(t)
            sl["f1"][ph].wrote(t)

        def tail(g):
            pg = g % 4
            ph = g % 2
            cc_ = c2[ph]
            sc_ = s2c[ph]
            t = ve((lambda pg, ph, cc_: lambda v: v.scalar_tensor_tensor(
                v2[ph][:, :], x1ld[pg][:, :], 0.0, f1[ph][:, :], ALU.add, ALU.add,
                accum_out=cc_["vsum"][:, 0:1]))(pg, ph, cc_),
                sl["x1ld"][pg].rd() + sl["f1"][ph].rd() + sl["v2"][ph].wd() + sc_["vsum"].wd())
            sl["x1ld"][pg].read(t)
            sl["f1"][ph].read(t)
            sl["v2"][ph].wrote(t)
            sc_["vsum"].wrote(t)
            if g + 4 < nch:
                x1_load(g + 4)
            t = se((lambda ph, cc_: lambda s: s.activation(
                sqd[ph][:, 0:DM], v2[ph][:, :], ACT.Square,
                accum_out=cc_["vsq"][:, 0:1]))(ph, cc_),
                sl["v2"][ph].rd() + s2q[ph].wd() + sc_["vsq"].wd())
            sl["v2"][ph].read(t)
            s2q[ph].wrote(t)
            sc_["vsq"].wrote(t)
            t = ve((lambda pg, cc_: lambda v: v.tensor_scalar(
                cc_["mu2"][:, 0:1], cc_["vsum"][:, 0:1], 1.0 / DM, None, ALU.mult))(pg, cc_),
                sc_["vsum"].rd() + sc_["mu2"].wd())
            sc_["mu2"].wrote(t)
            t = ve((lambda pg, cc_: lambda v: v.tensor_tensor(
                cc_["var2"][:, 0:1], cc_["mu2"][:, 0:1], cc_["mu2"][:, 0:1], ALU.mult))(pg, cc_),
                sc_["mu2"].rd() + sc_["var2"].wd())
            sc_["var2"].wrote(t)
            t = ve((lambda pg, cc_: lambda v: v.scalar_tensor_tensor(
                cc_["var2"][:, 0:1], cc_["vsq"][:, 0:1], 1.0 / DM, cc_["var2"][:, 0:1],
                ALU.mult, ALU.subtract))(pg, cc_),
                sc_["vsq"].rd() + sc_["var2"].wd())
            sc_["var2"].wrote(t)
            t = se((lambda pg, cc_: lambda s: s.activation(
                cc_["sdc"][:, 0:1], cc_["var2"][:, 0:1], ACT.Sqrt,
                bias=cols["epsc"][:, 0:1]))(pg, cc_),
                sc_["var2"].rd() + R("epsc") + sc_["sdc"].wd())
            sc_["sdc"].wrote(t)
            t = ve((lambda pg, cc_: lambda v: v.reciprocal(
                cc_["rs2"][:, 0:1], cc_["sdc"][:, 0:1]))(pg, cc_),
                sc_["sdc"].rd() + sc_["rs2"].wd())
            sc_["rs2"].wrote(t)
            t = ve((lambda pg, cc_: lambda v: v.scalar_tensor_tensor(
                cc_["nmr2"][:, 0:1], cc_["mu2"][:, 0:1], -1.0, cc_["rs2"][:, 0:1],
                ALU.mult, ALU.mult))(pg, cc_),
                sc_["mu2"].rd() + sc_["rs2"].rd() + sc_["nmr2"].wd())
            sc_["nmr2"].wrote(t)
            t = se((lambda ph, cc_: lambda s: s.activation(
                x2[ph][:, :], v2[ph][:, :], ACT.Identity,
                bias=cc_["nmr2"][:, 0:1], scale=cc_["rs2"][:, 0:1]))(ph, cc_),
                sl["v2"][ph].rd() + sc_["nmr2"].rd() + sc_["rs2"].rd() + sl["x2"][ph].wd())
            sl["v2"][ph].read(t)
            sl["x2"][ph].wrote(t)
            for s_ in range(2):
                dst = bass.AP(yout.ap().tensor, s_ * L * DM + g * CH * DM, [[DM, 64], [1, DM]])
                t = q_dma(dst, x2[ph][s_ * 64:(s_ + 1) * 64, :], sl["x2"][ph].rd())
                sl["x2"][ph].read(t)

        quant_a(0)
        quant_b(0)
        for g in range(nch):
            if g + 1 < nch:
                quant_a(g + 1)
            ffn_mm(g)
            if g + 1 < nch:
                quant_b(g + 1)
            if g > 0:
                tail(g - 1)
        tail(nch - 1)

        P.emit("SQ", lambda s: s.nop(), P.qdone("SQ"))

        with nc.Block() as block:
            P.realize(block)

    return nc


def _prep_inputs(inputs):
    W = inputs
    pr = {}
    pr["w1t"] = np.ascontiguousarray(np.asarray(W["L0_attn_in_proj_w"], np.float32).T)
    pr["w2t"] = np.ascontiguousarray(np.asarray(W["L0_attn_out_proj_w"], np.float32).T)
    pr["w3t"] = np.ascontiguousarray(np.asarray(W["L0_ffn_weight"], np.float32).T)
    cw = np.asarray(W["L0_attn_conv_w"], np.float32)[:, 0, :]
    pr["convw"] = np.ascontiguousarray(cw.reshape(17, 128, 4).transpose(1, 0, 2).reshape(128, 68))
    cb = np.asarray(W["L0_attn_conv_b"], np.float32)
    pr["convb"] = np.ascontiguousarray(cb.reshape(17, 128).T)
    pr["dtb"] = np.asarray(W["L0_attn_dt_bias"], np.float32).reshape(32, 1)
    pr["alog"] = np.asarray(W["L0_attn_A_log"], np.float32).reshape(32, 1)
    pr["drep"] = np.ascontiguousarray(np.tile(np.asarray(W["L0_attn_D"], np.float32)[None, :], (128, 1)))
    pr["rmsw16"] = np.ascontiguousarray(np.asarray(W["L0_attn_rms_w"], np.float32).reshape(16, 128).T)
    pr["gamma"] = np.asarray(W["L0_attn_ln_gamma"], np.float32).reshape(1, DM)
    pr["beta"] = np.asarray(W["L0_attn_ln_beta"], np.float32).reshape(1, DM)
    pr["fbias"] = np.asarray(W["L0_ffn_bias"], np.float32).reshape(1, DM)
    pr.update(host_constants())
    return pr


def kernel(**inputs):
    x = np.asarray(inputs["L0_attn_x"], np.float32)
    pr = _prep_inputs(inputs)
    nc = build(nch=32, debug=False)
    in_maps = []
    for core in range(8):
        m = dict(pr)
        m["xin"] = np.ascontiguousarray(x[2 * core:2 * core + 2])
        in_maps.append(m)
    res = run_bass_kernel_spmd(nc, in_maps, core_ids=list(range(8)))
    out = np.empty((16, 2048, 1024), np.float32)
    for core in range(8):
        out[2 * core:2 * core + 2] = np.asarray(res.results[core]["yout"]).reshape(2, 2048, 1024)
    return out


# version 26 (final)
# speedup vs baseline: 1.0602x; 1.0211x over previous
import numpy as np
from contextlib import ExitStack

import concourse.bass as bass
import concourse.mybir as mybir
from concourse.bass_utils import run_bass_kernel_spmd

F32 = mybir.dt.float32
F32R = mybir.dt.float32r
F16 = mybir.dt.float16
BF16 = mybir.dt.bfloat16
ALU = mybir.AluOpType
ACT = mybir.ActivationFunctionType
AX = mybir.AxisListType

CH = 64
H = 32
DM = 1024
DI = 2048
EPS = 1e-5
C21 = float(2 ** 20 + 1)


class Plan:
    ENGINES = ("TE", "VE", "SE")
    QUEUES = ("SQ", "GQ")
    NSLOT = 8

    def __init__(self, nc, es):
        self.nc = nc
        self.sems = {e: es.enter_context(nc.semaphore("sem_" + e)) for e in self.ENGINES}
        for q in self.QUEUES:
            for i in range(self.NSLOT):
                self.sems[f"{q}{i}"] = es.enter_context(nc.semaphore(f"sem_{q}{i}"))
        self.sems["CC"] = es.enter_context(nc.semaphore("sem_CC"))
        self.sems["GC"] = es.enter_context(nc.semaphore("sem_GC"))
        self.cnt = {e: 0 for e in self.sems}
        self.qidx = {q: 0 for q in self.QUEUES}
        self.ops = {e: [] for e in list(self.ENGINES) + list(self.QUEUES)}
        self.waited = {e: {} for e in list(self.ENGINES) + list(self.QUEUES)}

    def _waits(self, eng, deps, extra=()):
        need = {}
        for d in list(deps) + list(extra):
            if d is None:
                continue
            te_, tc = d
            if tc > need.get(te_, 0):
                need[te_] = tc
        waits = []
        for te_, tc in need.items():
            if self.waited[eng].get(te_, 0) < tc:
                self.waited[eng][te_] = tc
                waits.append((te_, tc))
        return waits

    def emit(self, eng, fn, deps=()):
        if eng in self.QUEUES:
            slot = self.qidx[eng] % self.NSLOT
            self.qidx[eng] += 1
            name = f"{eng}{slot}"
            extra = []
            if self.cnt[name] > 0:
                extra = [(name, self.cnt[name])]
            waits = self._waits(eng, deps, extra)
            self.cnt[name] += 16
            self.ops[eng].append((waits, fn, name, 16))
            return (name, self.cnt[name])
        waits = self._waits(eng, deps)
        self.cnt[eng] += 1
        self.ops[eng].append((waits, fn, eng, 1))
        return (eng, self.cnt[eng])

    def emit_cc(self, fn, deps=()):
        waits = self._waits("GQ", deps)
        self.cnt["CC"] += 1
        self.ops["GQ"].append((waits, fn, "CC", 1))
        return ("CC", self.cnt["CC"])

    def emit_gc(self, fn, deps=()):
        waits = self._waits("GQ", deps)
        self.cnt["GC"] += 1
        self.ops["GQ"].append((waits, fn, "GC", 1))
        return ("GC", self.cnt["GC"])

    def qdone(self, q):
        return [(f"{q}{i}", self.cnt[f"{q}{i}"]) for i in range(self.NSLOT)]

    def realize(self, block):
        secs = {"TE": block.tensor, "VE": block.vector, "SE": block.scalar,
                "SQ": block.sync, "GQ": block.gpsimd}
        sems = self.sems
        for ename in list(self.ENGINES) + list(self.QUEUES):
            ops = self.ops[ename]

            def make(ops=ops):
                def body(eng):
                    for waits, fn, sname, step in ops:
                        for te_, tc in waits:
                            eng.wait_ge(sems[te_], tc)
                        fn(eng).then_inc(sems[sname], step)
                return body
            secs[ename](make())


class Slot:
    __slots__ = ("w", "r")

    def __init__(self):
        self.w = []
        self.r = []

    def rd(self):
        return list(self.w)

    def wd(self):
        return list(self.w) + self.r

    def wrote(self, tok):
        self.w = [tok]
        self.r = []

    def wrote_many(self, toks):
        self.w = list(toks)
        self.r = []

    def read(self, tok):
        self.r.append(tok)


def host_constants():
    c = {}
    sm4 = np.zeros((128, 128), np.float32)
    for b in (0, 32, 64, 96):
        sm4[b, :64] = 1.0
        sm4[b + 1, 64:] = 1.0
    c["cseqm"] = sm4
    trl = np.tril(np.ones((64, 64), np.float32)).T
    c["ctril"] = np.concatenate([trl, trl], axis=0)
    i2 = np.zeros((128, 64), np.float32)
    for r in range(128):
        i2[r, r % 64] = 1.0
    c["ci2"] = i2
    c["ci2r"] = np.ascontiguousarray(np.tile(i2, (1, 8)))
    c["cident"] = np.eye(128, dtype=np.float32)
    c["cones"] = np.ones((1, 128), np.float32)
    return c


def build(nch=32, debug=False):
    L = nch * CH
    nc = bass.Bass("TRN2", target_bir_lowering=False, debug=False)

    xin = nc.dram_tensor("xin", [2, L, DM], F32, kind="ExternalInput")
    w1t = nc.dram_tensor("w1t", [DM, 4256], F32, kind="ExternalInput")
    w2t = nc.dram_tensor("w2t", [DI, DM], F32, kind="ExternalInput")
    w3t = nc.dram_tensor("w3t", [DM, DM], F32, kind="ExternalInput")
    convw = nc.dram_tensor("convw", [128, 68], F32, kind="ExternalInput")
    convb = nc.dram_tensor("convb", [128, 17], F32, kind="ExternalInput")
    dtbd = nc.dram_tensor("dtb", [32, 1], F32, kind="ExternalInput")
    alogd = nc.dram_tensor("alog", [32, 1], F32, kind="ExternalInput")
    drepd = nc.dram_tensor("drep", [128, 32], F32, kind="ExternalInput")
    rmsw16 = nc.dram_tensor("rmsw16", [128, 16], F32, kind="ExternalInput")
    gammad = nc.dram_tensor("gamma", [1, DM], F32, kind="ExternalInput")
    betad = nc.dram_tensor("beta", [1, DM], F32, kind="ExternalInput")
    fbiasd = nc.dram_tensor("fbias", [1, DM], F32, kind="ExternalInput")
    cseqm = nc.dram_tensor("cseqm", [128, 128], F32, kind="ExternalInput")
    ctrild = nc.dram_tensor("ctril", [128, 64], F32, kind="ExternalInput")
    ci2d = nc.dram_tensor("ci2", [128, 64], F32, kind="ExternalInput")
    ci2rd = nc.dram_tensor("ci2r", [128, 512], F32, kind="ExternalInput")
    cidentd = nc.dram_tensor("cident", [128, 128], F32, kind="ExternalInput")
    conesd = nc.dram_tensor("cones", [1, 128], F32, kind="ExternalInput")

    yout = nc.dram_tensor("yout", [2, L, DM], F32, kind="ExternalOutput")
    x1_hbm = nc.dram_tensor("x1s", [nch * 128, DM], F32,
                            kind=("ExternalOutput" if debug else "Internal"))
    cc_in = nc.dram_tensor("cc_in", [128, 1], F32)
    cc_out = nc.dram_tensor("cc_out", [128, 1], F32)
    wmx_hbm = nc.dram_tensor("wmx", [128, 1], F32)

    es = ExitStack()
    with es:
        P = Plan(nc, es)

        def sb(stk, name, shape, dtype):
            return stk.enter_context(nc.sbuf_tensor(name, shape, dtype))

        def pt(name, shape, dtype=F32):
            return es.enter_context(nc.psum_tensor(name, shape, dtype))

        idb_s = sb(es, "idb_s", [128, 128], BF16)
        ones_s = sb(es, "ones_s", [1, 128], F32)
        amtmp = sb(es, "amtmp", [128, 1], F32)
        amrow = sb(es, "amrow", [1, 128], F32)
        cols = {}
        for n in ["ssq", "ssq2", "sdc", "rsc", "hsum", "hsq", "muc", "varc", "nmr",
                  "vsum", "vsq", "mu2", "var2", "rs2", "nmr2", "epsc", "invdic", "onecol",
                  "nhalfc", "lnv", "hsqn", "negmu"]:
            cols[n] = sb(es, n, [128, 1], F32)

        ps_ip = [pt(f"ps_ip{i}", [128, 512]) for i in range(2)]
        ps_eb = [pt(f"ps_eb{i}", [128, 512]) for i in range(2)]
        ps_y = [pt(f"ps_y{i}", [128, 512]) for i in range(2)]
        ps_st = ps_eb
        ps_tb = pt("ps_tb", [128, 1024], BF16)
        ps_tb2 = pt("ps_tb2", [128, 1024], BF16)

        sl = {}

        def S(name):
            sl[name] = Slot()
            return sl[name]

        for n in ["idb", "ones", "amtmp", "amrow"] + list(cols):
            S(n)
        for n in ["ps_ip0", "ps_ip1", "ps_eb0", "ps_eb1", "ps_y0", "ps_y1",
                  "ps_tb", "ps_tb2"]:
            S(n)
        sl["ps_st0"] = sl["ps_eb0"]
        sl["ps_st1"] = sl["ps_eb1"]

        es1 = ExitStack()
        W1s = [sb(es1, f"w1_{k}", [128, 4256], BF16) for k in range(8)]
        W2s = [sb(es1, f"w2_{k}", [128, DM], BF16) for k in range(16)]
        rmsw_s = sb(es1, "rmsw_s", [128, 16], F32)
        cw_s = sb(es1, "cw_s", [128, 68], F32)
        cb_s = sb(es1, "cb_s", [128, 17], F32)
        dtb_s = sb(es1, "dtb_s", [32, 1], F32)
        alog_s = sb(es1, "alog_s", [32, 1], F32)
        acol = sb(es1, "acol", [32, 1], F32)
        drep_s = sb(es1, "drep_s", [128, 32], F32)
        seqm_s = sb(es1, "seqm_s", [128, 128], F16)
        tril_s = sb(es1, "tril_s", [128, 64], F32)
        triu_sb = sb(es1, "triu_sb", [128, 64], BF16)
        i2r_s = sb(es1, "i2r_s", [128, 512], BF16)
        idf_s = sb(es1, "idf_s", [32, 32], F32)
        di_s = sb(es1, "di_s", [128, 2048], BF16)
        gbc = sb(es1, "gbc", [128, DM], BF16)
        bbc = sb(es1, "bbc", [128, DM], BF16)

        xtm = [sb(es1, f"xtm{i}", [128, DM], F32) for i in range(2)]
        xbf1 = sb(es1, "xbf1", [128, DM], BF16)
        xfm = sb(es1, "xfm", [128, 1024], BF16)
        zact = sb(es1, "zact", [128, DI], BF16)
        xbcr = sb(es1, "xbcr", [128, 16 * 134], BF16)
        bcr = sb(es1, "bcr", [128, 134], BF16)
        tails = sb(es1, "tails", [128, 102], BF16)
        xsconv = sb(es1, "xsconv", [128, DI], BF16)
        bcc = sb(es1, "bcc", [128, 128], BF16)
        cwdg = sb(es1, "cwdg", [128, 68 * 128], BF16)
        dtfm = sb(es1, "dtfm", [32, 128], F32)
        dafm = sb(es1, "dafm", [32, 128], F32)
        edafm = sb(es1, "edafm", [32, 128], F32)
        dttm = sb(es1, "dttm", [128, 32], F32)
        datm = sb(es1, "datm", [128, 32], F32)
        eacshl = sb(es1, "eacshl", [64, 64], F32)
        flats1 = sb(es1, "flats1", [64, 2048], F16)
        flats2 = [flats1, flats1]
        btm = sb(es1, "btm", [128, 64], BF16)
        cttm = sb(es1, "cttm", [128, 64], BF16)
        bfm2 = sb(es1, "bfm2", [128, 64], BF16)
        cfm2 = sb(es1, "cfm2", [128, 64], BF16)
        scoresm = sb(es1, "scoresm", [128, 64], BF16)
        escan = sb(es1, "escan", [128, 2048], BF16)
        slb = sb(es1, "slb", [128, 2048], BF16)
        cdt = sb(es1, "cdt", [128, 2048], BF16)
        bdec = sb(es1, "bdec", [128, 2048], BF16)
        xstm = sb(es1, "xstm", [128, 2048], BF16)
        xdtm = sb(es1, "xdtm", [128, 2048], BF16)
        expacsbc = sb(es1, "expacsbc", [128, 2048], BF16)
        edec = [sb(es1, f"edec{i}", [128, 32], BF16) for i in range(2)]
        tstate = sb(es1, "tstate", [128, 2048], BF16)
        yn = sb(es1, "yn", [128, DI], BF16)
        stsb = sb(es1, "stsb", [128, 2048], BF16)
        h1 = sb(es1, "h1", [128, DM], F32)
        a2 = sb(es1, "a2", [128, DM], F32)
        vv = h1
        x1t = a2
        amaxc = [sb(es1, f"amaxc{i}", [128, 1], F32) for i in range(2)]
        import os as _os
        if _os.environ.get("KSLACK"):
            print("phase1 sbuf remaining:", nc.sbuf_bytes_remaining)

        for n in ["w1", "w2", "rmsw", "cw", "cb", "dtb", "alog", "acol", "drep",
                  "seqm", "tril", "triu", "i2", "idf", "di", "gbc", "bbc",
                  "xfm", "zact", "xbcr", "bcr", "tails", "xsconv", "bcc",
                  "cwdg", "dtfm", "dafm", "edafm", "dttm", "datm",
                  "eacshl", "i2r", "btm", "cttm", "bfm2", "cfm2", "scoresm",
                  "escan", "slb", "cdt", "bdec", "xstm", "xdtm", "expacsbc",
                  "tstate", "yn", "stsb", "h1", "a2", "vv", "x1t"]:
            S(n)
        sl["xtm"] = [Slot(), Slot()]
        S("xbf1")
        sl["vv"] = sl["h1"]
        sl["x1t"] = sl["a2"]
        _fsl = Slot()
        sl["flats"] = [_fsl, _fsl]
        sl["edec"] = [Slot(), Slot()]
        sl["amaxc"] = [Slot(), Slot()]

        def R(n):
            return sl[n].rd()

        def Wd(n):
            return sl[n].wd()

        def ve(fn, deps):
            return P.emit("VE", fn, deps)

        def se(fn, deps):
            return P.emit("SE", fn, deps)

        def te(fn, deps):
            return P.emit("TE", fn, deps)

        def gq(fn, deps):
            return P.emit("GQ", fn, deps)

        def sq(fn, deps):
            return P.emit("SQ", fn, deps)

        def g_dma(dst, src, deps):
            return gq(lambda g, dst=dst, src=src: g.dma_start(out=dst, in_=src), deps)

        def q_dma(dst, src, deps):
            return sq(lambda s, dst=dst, src=src: s.dma_start(out=dst, in_=src), deps)

        w1toks = []
        for k in range(8):
            w1toks.append(g_dma(W1s[k][:, :], w1t[k * 128:(k + 1) * 128, :], Wd("w1")))
        w2toks = []
        for k in range(16):
            w2toks.append(g_dma(W2s[k][:, :], w2t[k * 128:(k + 1) * 128, :], Wd("w2")))
        for (dst, src, n) in [(rmsw_s, rmsw16, "rmsw"), (cw_s, convw, "cw"),
                              (cb_s, convb, "cb"), (dtb_s, dtbd, "dtb"),
                              (alog_s, alogd, "alog"), (drep_s, drepd, "drep"),
                              (seqm_s, cseqm, "seqm"), (tril_s, ctrild, "tril"),
                              (triu_sb, ctrild, "triu"),
                              (i2r_s, ci2rd, "i2r"),
                              (idb_s, cidentd, "idb"),
                              (ones_s, conesd, "ones")]:
            t = g_dma(dst[:, :], src[:, :], Wd(n))
            sl[n].wrote(t)
        sl["w1"].wrote_many(w1toks)
        w2stoks = []
        for k in range(16):
            w2stoks.append(ve((lambda k: lambda v: v.tensor_scalar(
                W2s[k][:, :], W2s[k][:, :], rmsw_s[:, k:k + 1], None, ALU.mult))(k),
                [w2toks[k]] + R("rmsw")))
        sl["w2"].wrote_many(w2stoks)
        t = se(lambda s: s.activation(acol[:, 0:1], alog_s[:, 0:1], ACT.Exp),
               R("alog") + Wd("acol"))
        sl["acol"].wrote(t)
        t = ve(lambda v: v.tensor_scalar(acol[:, 0:1], acol[:, 0:1], -1.0, None, ALU.mult),
               Wd("acol"))
        sl["acol"].wrote(t)
        div0 = bass.AP(i2r_s.ap().tensor, 0, [[512, 128], [0, 32], [1, 64]])
        div1 = bass.AP(drep_s.ap().tensor, 0, [[32, 128], [1, 32], [0, 64]])
        t = ve(lambda v: v.tensor_tensor(di_s[:, :], div0, div1, ALU.mult),
               R("i2r") + R("drep") + Wd("di"))
        sl["di"].wrote(t)
        grow = h1
        brow = a2
        sg, sbr = sl["h1"], sl["a2"]
        t = g_dma(grow[0:1, :], gammad[:, :], sg.wd())
        sg.wrote(t)
        t = g_dma(brow[0:1, :], betad[:, :], sbr.wd())
        sbr.wrote(t)
        for (row, rs_, dst, dn) in ((grow, sg, gbc, "gbc"), (brow, sbr, bbc, "bbc")):
            for hhalf in range(2):
                t = te((lambda row, hhalf: lambda tt: tt.matmul(
                    ps_ip[hhalf][:, 0:512], ones_s[0:1, :],
                    row[0:1, hhalf * 512:(hhalf + 1) * 512],
                    start=True, stop=True))(row, hhalf),
                    rs_.rd() + R("ones") + Wd(f"ps_ip{hhalf}"))
                sl[f"ps_ip{hhalf}"].wrote(t)
                rs_.read(t)
                t = ve((lambda dst, hhalf: lambda v: v.tensor_copy(
                    dst[:, hhalf * 512:(hhalf + 1) * 512], ps_ip[hhalf][:, 0:512]))(dst, hhalf),
                    R(f"ps_ip{hhalf}") + Wd(dn))
                sl[f"ps_ip{hhalf}"].read(t)
            sl[dn].wrote(t)
        t = ve(lambda v: v.memset(tstate[:, :], 0.0), Wd("tstate"))
        sl["tstate"].wrote(t)
        for i in range(2):
            t = ve((lambda i: lambda v: v.memset(amaxc[i][:, :], 0.0))(i),
                   sl["amaxc"][i].wd())
            sl["amaxc"][i].wrote(t)
        t = ve(lambda v: v.memset(tails[:, :], 0.0), Wd("tails"))
        sl["tails"].wrote(t)
        t = ve(lambda v: v.memset(cols["epsc"][:, :], float(EPS)), Wd("epsc"))
        sl["epsc"].wrote(t)
        t = ve(lambda v: v.memset(cols["invdic"][:, :], 1.0 / DI), Wd("invdic"))
        sl["invdic"].wrote(t)
        t = ve(lambda v: v.memset(cols["onecol"][:, :], 1.0), Wd("onecol"))
        sl["onecol"].wrote(t)
        t = ve(lambda v: v.memset(cols["nhalfc"][:, :], -0.5), Wd("nhalfc"))
        sl["nhalfc"].wrote(t)
        t = g_dma(idf_s[:, :], cidentd[0:32, 0:32], Wd("idf"))
        sl["idf"].wrote(t)
        for mk in range(68):
            t = ve((lambda mk: lambda v: v.tensor_scalar(
                cwdg[:, mk * 128:(mk + 1) * 128], idb_s[:, :], cw_s[:, mk:mk + 1],
                None, ALU.mult))(mk),
                R("idb") + R("cw") + (Wd("cwdg") if mk == 0 else []))
        sl["cwdg"].wrote(t)

        def load_xbf(c):
            src_ = bass.AP(xin.ap().tensor, c * CH * DM, [[DM, 64], [1, DM]])
            t1 = g_dma(xbf1[0:64, :], src_, sl["xbf1"].wd())
            src_ = bass.AP(xin.ap().tensor, L * DM + c * CH * DM, [[DM, 64], [1, DM]])
            t2 = g_dma(xbf1[64:128, :], src_, [])
            sl["xbf1"].wrote_many([t1, t2])

        def load_x(c):
            p2 = c % 2
            tl = []
            for s_ in range(2):
                src = bass.AP(xin.ap().tensor, s_ * L * DM + c * CH * DM, [[DM, 64], [1, DM]])
                tl.append(q_dma(xtm[p2][s_ * 64:(s_ + 1) * 64, :], src,
                                sl["xtm"][p2].wd() if s_ == 0 else []))
            sl["xtm"][p2].wrote_many(tl)


        def xprep(c):
            for k in range(8):
                t = te((lambda k: lambda tt: tt.transpose(
                    ps_tb2[:, k * 128:(k + 1) * 128], xbf1[:, k * 128:(k + 1) * 128],
                    idb_s[:, :]))(k),
                    R("xbf1") + R("idb") + (Wd("ps_tb2") if k == 0 else []))
            sl["ps_tb2"].wrote(t)
            sl["xbf1"].read(t)
            t = ve(lambda v: v.tensor_copy(xfm[:, :], ps_tb2[:, :]),
                   R("ps_tb2") + Wd("xfm"))
            sl["ps_tb2"].read(t)
            sl["xfm"].wrote(t)

        def head(c):
            p = c % 2
            if c + 1 < nch:
                load_xbf(c + 1)

            for mg in range(5):
                bank = mg % 2
                mlist = (list(range((mg - 1) * 4, mg * 4)) if mg > 0 else [16, 17])
                first = True
                for m in mlist:
                    rows = 32 if m == 17 else 128
                    for k in range(8):
                        t = te((lambda m, k, bank, rows: lambda tt: tt.matmul(
                            ps_ip[bank][0:rows, (m % 4) * 128:(m % 4) * 128 + 128],
                            W1s[k][:, DI + m * 128: DI + m * 128 + rows],
                            xfm[:, k * 128:(k + 1) * 128],
                            start=(k == 0), stop=(k == 7)))(m, k, bank, rows),
                            R("xfm") + R("w1") + (Wd(f"ps_ip{bank}") if first else []))
                        first = False
                sl[f"ps_ip{bank}"].wrote(t)
                sl["xfm"].read(t)
                if mlist[-1] <= 15:
                    dst = bass.AP(xbcr.ap().tensor, mlist[0] * 134 + 3,
                                  [[16 * 134, 128], [134, len(mlist)], [67, 2], [1, 64]])
                    src = bass.AP(ps_ip[bank].ap().tensor, 0,
                                  [[512, 128], [128, len(mlist)], [64, 2], [1, 64]])
                    t = ve((lambda dst, src: lambda v: v.tensor_copy(dst, src))(dst, src),
                           R(f"ps_ip{bank}") + Wd("xbcr"))
                    sl[f"ps_ip{bank}"].read(t)
                    sl["xbcr"].wrote(t)
                else:
                    bdst = bass.AP(bcr.ap().tensor, 3, [[134, 128], [67, 2], [1, 64]])
                    bsrc = bass.AP(ps_ip[bank].ap().tensor, 0, [[512, 128], [64, 2], [1, 64]])
                    tse = se((lambda bank: lambda s: s.activation(
                        dtfm[:, :], ps_ip[bank][0:32, 128:256], ACT.Exp,
                        bias=dtb_s[:, 0:1]))(bank),
                        R(f"ps_ip{bank}") + R("dtb") + Wd("dtfm"))
                    sl[f"ps_ip{bank}"].read(tse)
                    sl["dtfm"].wrote(tse)
                    tb = se((lambda bdst, bsrc: lambda s: s.copy(bdst, bsrc))(bdst, bsrc),
                            R(f"ps_ip{bank}") + [tse] + Wd("bcr"))
                    sl[f"ps_ip{bank}"].read(tb)
                    sl["bcr"].wrote(tb)
                    t = se(lambda s: s.activation(dtfm[:, :], dtfm[:, :], ACT.Ln,
                                                  bias=cols["onecol"][0:32, 0:1]),
                           Wd("dtfm") + R("onecol"))
                    sl["dtfm"].wrote(t)

            t = ve(lambda v: v.tensor_scalar(dafm[:, :], dtfm[:, :], acol[:, 0:1], None, ALU.mult),
                   R("dtfm") + R("acol") + Wd("dafm"))
            sl["dafm"].wrote(t)
            t = se(lambda s: s.activation(edafm[:, :], dafm[:, :], ACT.Exp),
                   R("dafm") + Wd("edafm"))
            sl["edafm"].wrote(t)

            hd = bass.AP(xbcr.ap().tensor, 0, [[16 * 134, 128], [134, 16], [67, 2], [1, 3]])
            hd2 = bass.AP(bcr.ap().tensor, 0, [[134, 128], [67, 2], [1, 3]])
            if c > 0:
                hs = bass.AP(tails.ap().tensor, 0, [[102, 128], [1, 96]])
                t = ve(lambda v, hd=hd, hs=hs: v.tensor_copy(hd, hs),
                       R("tails") + Wd("xbcr"))
                sl["tails"].read(t)
                sl["xbcr"].wrote(t)
                hs2 = bass.AP(tails.ap().tensor, 96, [[102, 128], [1, 6]])
                t = ve(lambda v, hd2=hd2, hs2=hs2: v.tensor_copy(hd2, hs2),
                       R("tails") + Wd("bcr"))
                sl["tails"].read(t)
                sl["bcr"].wrote(t)
            else:
                t = ve(lambda v, hd=hd: v.memset(hd, 0.0), Wd("xbcr"))
                sl["xbcr"].wrote(t)
                t = ve(lambda v, hd2=hd2: v.memset(hd2, 0.0), Wd("bcr"))
                sl["bcr"].wrote(t)

            for cg in range(5):
                bank = (cg + 1) % 2
                mlist = list(range(cg * 4, min(17, cg * 4 + 4)))
                first = True
                for m in mlist:
                    srcb = bcr if m == 16 else xbcr
                    off = 0 if m == 16 else m * 134
                    sn = "bcr" if m == 16 else "xbcr"
                    pstep = 134 if m == 16 else 16 * 134
                    for k in range(4):
                        win = bass.AP(srcb.ap().tensor, off + k, [[pstep, 128], [67, 2], [1, 64]])
                        t = te((lambda m, k, bank, win: lambda tt: tt.matmul(
                            ps_ip[bank][:, (m % 4) * 128:(m % 4) * 128 + 128],
                            cwdg[:, (4 * m + k) * 128:(4 * m + k + 1) * 128], win,
                            start=(k == 0), stop=(k == 3)))(m, k, bank, win),
                            R(sn) + R("cwdg") + (Wd(f"ps_ip{bank}") if first else []))
                        first = False
                sl[f"ps_ip{bank}"].wrote(t)
                sl["xbcr"].read(t)
                sl["bcr"].read(t)
                if mlist[-1] != 16:
                    cbv = cb_s[:, mlist[0]:mlist[0] + 1]
                    for m in mlist:
                        t = se((lambda m, bank: lambda s: s.activation(
                            xsconv[:, m * 128:(m + 1) * 128],
                            ps_ip[bank][:, (m % 4) * 128:(m % 4) * 128 + 128],
                            ACT.Silu, bias=cb_s[:, m:m + 1]))(m, bank),
                            R(f"ps_ip{bank}") + R("cb") + (Wd("xsconv") if m == 0 else []))
                        if m == 0:
                            sl["xsconv"].wrote(t)
                        sl[f"ps_ip{bank}"].read(t)
                else:
                    for m in mlist:
                        dstt, dn = (bcc, "bcc") if m == 16 else (xsconv, "xsconv")
                        t = se((lambda m, bank, dstt: lambda s: s.activation(
                            dstt[:, 0:128] if dstt is bcc else dstt[:, m * 128:(m + 1) * 128],
                            ps_ip[bank][:, (m % 4) * 128:(m % 4) * 128 + 128],
                            ACT.Silu, bias=cb_s[:, m:m + 1]))(m, bank, dstt),
                            R(f"ps_ip{bank}") + R("cb") + (Wd(dn) if dstt is bcc else []))
                        if dstt is bcc:
                            sl["bcc"].wrote(t)
                        sl[f"ps_ip{bank}"].read(t)
            sl["xsconv"].wrote(("SE", P.cnt["SE"]))
            ts_src = bass.AP(xbcr.ap().tensor, 64, [[16 * 134, 128], [134, 16], [67, 2], [1, 3]])
            ts_dst = bass.AP(tails.ap().tensor, 0, [[102, 128], [1, 96]])
            t = ve(lambda v, a=ts_dst, b=ts_src: v.tensor_copy(a, b),
                   R("xbcr") + Wd("tails"))
            sl["xbcr"].read(t)
            sl["tails"].wrote(t)
            tb_src = bass.AP(bcr.ap().tensor, 64, [[134, 128], [67, 2], [1, 3]])
            tb_dst = bass.AP(tails.ap().tensor, 96, [[102, 128], [1, 6]])
            t = ve(lambda v, a=tb_dst, b=tb_src: v.tensor_copy(a, b),
                   R("bcr") + Wd("tails"))
            sl["bcr"].read(t)
            sl["tails"].wrote(t)

            for (srct, sn, dn, dstt) in ((dtfm, "dtfm", "dttm", dttm),
                                         (dafm, "dafm", "datm", datm)):
                t = te((lambda srct: lambda tt: tt.transpose(
                    ps_eb[0][:, 0:32], srct[:, :], idf_s[:, :]))(srct),
                    R(sn) + R("idf") + Wd("ps_eb0"))
                sl["ps_eb0"].wrote(t)
                sl[sn].read(t)
                t = ve((lambda dstt: lambda v: v.tensor_copy(dstt[:, :], ps_eb[0][:, 0:32]))(dstt),
                       R("ps_eb0") + Wd(dn))
                sl["ps_eb0"].read(t)
                sl[dn].wrote(t)

            for s_ in range(2):
                t = te((lambda s_: lambda tt: tt.matmul(
                    ps_eb[1][s_ * 32:(s_ + 1) * 32, 0:64],
                    datm[s_ * 64:(s_ + 1) * 64, :],
                    tril_s[s_ * 64:(s_ + 1) * 64, :],
                    start=True, stop=True))(s_),
                    R("datm") + R("tril") + (Wd("ps_eb1") if s_ == 0 else []))
            sl["ps_eb1"].wrote(t)
            sl["datm"].read(t)
            t = se(lambda s: s.activation(eacshl[:, :], ps_eb[1][0:64, 0:64], ACT.Exp),
                   R("ps_eb1") + Wd("eacshl"))
            sl["ps_eb1"].read(t)
            sl["eacshl"].wrote(t)

            fl = flats2[p]
            ftoks = []
            for s_ in range(2):
                src = bass.AP(edafm.ap().tensor, s_ * 64, [[128, 32], [1, 64]])
                dst = bass.AP(fl.ap().tensor, s_ * 2048, [[2048, 1], [64, 32], [1, 64]])
                t = g_dma(dst, src, R("edafm") + (sl["flats"][p].wd() if s_ == 0 else []))
                sl["edafm"].read(t)
                ftoks.append(t)
            for s_ in range(2):
                src = bass.AP(eacshl.ap().tensor, s_ * 32 * 64, [[64, 32], [1, 64]])
                dst = bass.AP(fl.ap().tensor, (32 + s_) * 2048,
                              [[2048, 1], [64, 32], [1, 64]])
                t = g_dma(dst, src, R("eacshl"))
                sl["eacshl"].read(t)
                ftoks.append(t)
            sl["flats"][p].wrote_many(ftoks)

            t = te(lambda tt: tt.transpose(ps_tb2[:, 0:64], bcc[0:64, 0:128], idb_s[0:64, 0:64]),
                   R("bcc") + R("idb") + Wd("ps_tb2"))
            sl["ps_tb2"].wrote(t)
            t = te(lambda tt: tt.transpose(ps_tb2[:, 64:128], bcc[64:128, 0:128],
                                           idb_s[64:128, 64:128]),
                   R("bcc") + Wd("ps_tb2"))
            sl["ps_tb2"].wrote(t)
            sl["bcc"].read(t)
            t = ve(lambda v: v.tensor_copy(btm[:, :], ps_tb2[:, 0:64]),
                   R("ps_tb2") + Wd("btm"))
            sl["ps_tb2"].read(t)
            sl["btm"].wrote(t)
            t = ve(lambda v: v.tensor_copy(cttm[:, :], ps_tb2[:, 64:128]),
                   R("ps_tb2") + Wd("cttm"))
            sl["ps_tb2"].read(t)
            sl["cttm"].wrote(t)
            for (srct, sn, dn, dstt) in ((btm, "btm", "bfm2", bfm2), (cttm, "cttm", "cfm2", cfm2)):
                for s_ in range(2):
                    t = te((lambda srct, s_: lambda tt: tt.transpose(
                        ps_tb2[s_ * 64:(s_ + 1) * 64, 128:192],
                        srct[s_ * 64:(s_ + 1) * 64, 0:64],
                        idb_s[s_ * 64:(s_ + 1) * 64, s_ * 64:(s_ + 1) * 64]))(srct, s_),
                        R(sn) + (Wd("ps_tb2") if s_ == 0 else []))
                sl["ps_tb2"].wrote(t)
                sl[sn].read(t)
                t = ve((lambda dstt: lambda v: v.tensor_copy(dstt[:, :], ps_tb2[:, 128:192]))(dstt),
                       R("ps_tb2") + Wd(dn))
                sl["ps_tb2"].read(t)
                sl[dn].wrote(t)
            for s_ in range(2):
                t = te((lambda s_: lambda tt: tt.matmul(
                    ps_ip[0][s_ * 64:(s_ + 1) * 64, 0:64],
                    bfm2[s_ * 64:(s_ + 1) * 64, :],
                    cfm2[s_ * 64:(s_ + 1) * 64, :],
                    start=True, stop=True))(s_),
                    R("bfm2") + R("cfm2") + (Wd("ps_ip0") if s_ == 0 else []))
            sl["ps_ip0"].wrote(t)
            sl["bfm2"].read(t)
            sl["cfm2"].read(t)
            t = ve(lambda v: v.tensor_tensor(scoresm[:, :], ps_ip[0][:, 0:64],
                                             triu_sb[:, :], ALU.mult),
                   R("ps_ip0") + R("triu") + Wd("scoresm"))
            sl["ps_ip0"].read(t)
            sl["scoresm"].wrote(t)

            for hhalf in range(2):
                pst = ps_tb if hhalf == 0 else ps_tb2
                pn = "ps_tb" if hhalf == 0 else "ps_tb2"
                for j in range(8):
                    m = hhalf * 8 + j
                    t = te((lambda m, j, pst: lambda tt: tt.transpose(
                        pst[:, j * 128:(j + 1) * 128],
                        xsconv[:, m * 128:(m + 1) * 128], idb_s[:, :]))(m, j, pst),
                        R("xsconv") + (Wd(pn) if j == 0 else []))
                sl[pn].wrote(t)
                sl["xsconv"].read(t)
                t = ve((lambda hhalf, pst: lambda v: v.tensor_copy(
                    xstm[:, hhalf * 1024:(hhalf + 1) * 1024], pst[:, :]))(hhalf, pst),
                    R(pn) + (Wd("xstm") if hhalf == 0 else []))
                sl[pn].read(t)
                if hhalf == 0:
                    sl["xstm"].wrote(t)
                dtv = bass.AP(dttm.ap().tensor, hhalf * 16, [[32, 128], [1, 16], [0, 64]])
                t = ve((lambda hhalf, pst, dtv: lambda v: v.tensor_tensor(
                    xdtm[:, hhalf * 1024:(hhalf + 1) * 1024], pst[:, :], dtv, ALU.mult))(hhalf, pst, dtv),
                    R(pn) + R("dttm") + (Wd("xdtm") if hhalf == 0 else []))
                sl[pn].read(t)
                sl["dttm"].read(t)
                if hhalf == 0:
                    sl["xdtm"].wrote(t)
            sl["xstm"].wrote(("VE", P.cnt["VE"]))
            sl["xdtm"].wrote(("VE", P.cnt["VE"]))

        def body(c):
            p = c % 2
            q_ = 1 - p
            if c > 0:
                edvv = bass.AP(edec[q_].ap().tensor, 0, [[32, 128], [1, 32], [0, 64]])
                t = ve(lambda v, edvv=edvv: v.tensor_tensor(yn[:, :], tstate[:, :], edvv, ALU.mult),
                       R("tstate") + sl["edec"][q_].rd() + Wd("yn"))
                sl["tstate"].read(t)
                sl["edec"][q_].read(t)
                sl["yn"].wrote(t)
                t = ve(lambda v: v.tensor_tensor(tstate[:, :], yn[:, :], stsb[:, :], ALU.add),
                       R("yn") + R("stsb") + Wd("tstate"))
                sl["yn"].read(t)
                sl["stsb"].read(t)
                sl["tstate"].wrote(t)
            for gz in range(4):
                bank = gz % 2
                for k in range(8):
                    t = te((lambda gz, k, bank: lambda tt: tt.matmul(
                        ps_ip[bank][:, 0:512],
                        xfm[:, k * 128:(k + 1) * 128],
                        W1s[k][:, gz * 512:(gz + 1) * 512],
                        start=(k == 0), stop=(k == 7)))(gz, k, bank),
                        R("xfm") + R("w1") + (Wd(f"ps_ip{bank}") if k == 0 else []))
                sl[f"ps_ip{bank}"].wrote(t)
                sl["xfm"].read(t)
                t = se((lambda gz, bank: lambda s: s.activation(
                    zact[:, gz * 512:(gz + 1) * 512], ps_ip[bank][:, 0:512], ACT.Silu))(gz, bank),
                    R(f"ps_ip{bank}") + (Wd("zact") if gz == 0 else []))
                sl[f"ps_ip{bank}"].read(t)
            sl["zact"].wrote(t)

            fl = flats2[p]
            scan_tok = {}
            prev = None
            for gq_ in range(4):
                bank = gq_ % 2
                t = te((lambda gq_, bank, fl: lambda tt: tt.matmul(
                    ps_eb[bank][:, 0:512], seqm_s[0:2, :],
                    fl[0:2, gq_ * 512:(gq_ + 1) * 512],
                    start=True, stop=True))(gq_, bank, fl),
                    sl["flats"][p].rd() + R("seqm") + Wd(f"ps_eb{bank}"))
                sl[f"ps_eb{bank}"].wrote(t)
                sl["flats"][p].read(t)
                t = ve((lambda gq_, bank: lambda v: v.tensor_tensor_scan(
                    escan[:, gq_ * 512:(gq_ + 1) * 512], ps_eb[bank][:, 0:512],
                    i2r_s[:, 0:512], 0.0, ALU.mult, ALU.max))(gq_, bank),
                    R(f"ps_eb{bank}") + R("i2r") + Wd("escan"))
                sl[f"ps_eb{bank}"].read(t)
                sl["escan"].wrote(t)
                scan_tok[gq_] = t
                prev = t
            sl["escan"].wrote(prev)

            scv = bass.AP(scoresm.ap().tensor, 0, [[64, 128], [0, 8], [1, 64]])
            cdv = bass.AP(cfm2.ap().tensor, 0, [[64, 128], [0, 8], [1, 64]])
            slbq_tok = {}
            cdtq_tok = {}
            for gq_ in range(4):
                bank = gq_ % 2
                t = ve((lambda gq_, scv=scv: lambda v: v.tensor_tensor(
                    slb[:, gq_ * 512:(gq_ + 1) * 512],
                    escan[:, gq_ * 512:(gq_ + 1) * 512], scv, ALU.mult))(gq_),
                    [scan_tok[gq_]] + R("scoresm") + (Wd("slb") if gq_ == 0 else []))
                sl["scoresm"].read(t)
                sl["slb"].wrote(t)
                slbq_tok[gq_] = t
                t = te((lambda gq_, bank, fl: lambda tt: tt.matmul(
                    ps_eb[bank][:, 0:512], seqm_s[32:34, :],
                    fl[32:34, gq_ * 512:(gq_ + 1) * 512],
                    start=True, stop=True))(gq_, bank, fl),
                    sl["flats"][p].rd() + R("seqm") + Wd(f"ps_eb{bank}"))
                sl[f"ps_eb{bank}"].wrote(t)
                sl["flats"][p].read(t)
                t = ve((lambda gq_, bank: lambda v: v.tensor_copy(
                    expacsbc[:, gq_ * 512:(gq_ + 1) * 512], ps_eb[bank][:, 0:512]))(gq_, bank),
                    R(f"ps_eb{bank}") + (Wd("expacsbc") if gq_ == 0 else []))
                sl[f"ps_eb{bank}"].read(t)
                sl["expacsbc"].wrote(t)
                t = ve((lambda gq_, t_ev=t, cdv=cdv: lambda v: v.tensor_tensor(
                    cdt[:, gq_ * 512:(gq_ + 1) * 512],
                    expacsbc[:, gq_ * 512:(gq_ + 1) * 512], cdv, ALU.mult))(gq_),
                    [t] + R("cfm2") + (Wd("cdt") if gq_ == 0 else []))
                sl["cfm2"].read(t)
                sl["cdt"].wrote(t)
                cdtq_tok[gq_] = t
            edv = bass.AP(expacsbc.ap().tensor, 63, [[2048, 128], [64, 32]])
            t = ve((lambda p, edv: lambda v: v.tensor_copy(edec[p][:, :], edv))(p, edv),
                   R("expacsbc") + sl["edec"][p].wd())
            sl["expacsbc"].read(t)
            sl["edec"][p].wrote(t)

            b0 = bass.AP(btm.ap().tensor, 0, [[64, 128], [0, 32], [1, 64]])
            ecol = bass.AP(escan.ap().tensor, 63, [[2048, 128], [64, 32], [0, 64]])
            t = ve(lambda v, b0=b0, ecol=ecol: v.tensor_tensor(bdec[:, :], b0, ecol, ALU.mult),
                   R("btm") + R("escan") + Wd("bdec"))
            sl["btm"].read(t)
            sl["escan"].read(t)
            sl["bdec"].wrote(t)

            for quarter in range(4):
                ybank = quarter % 2
                first = True
                for hh in range(quarter * 8, quarter * 8 + 8):
                    for s_ in range(2):
                        r0 = s_ * 64
                        fo = (hh % 8) * 64
                        deps = ([slbq_tok[quarter], cdtq_tok[quarter]] +
                                R("xdtm") + R("tstate") +
                                R("xstm") + R("di") + (Wd(f"ps_y{ybank}") if first else []))
                        te((lambda hh, r0, fo, ybank: lambda tt: tt.matmul(
                            ps_y[ybank][r0:r0 + 64, fo:fo + 64],
                            slb[r0:r0 + 64, hh * 64:(hh + 1) * 64],
                            xdtm[r0:r0 + 64, hh * 64:(hh + 1) * 64],
                            start=True, stop=False))(hh, r0, fo, ybank), deps)
                        first = False
                        te((lambda hh, r0, fo, ybank: lambda tt: tt.matmul(
                            ps_y[ybank][r0:r0 + 64, fo:fo + 64],
                            cdt[r0:r0 + 64, hh * 64:(hh + 1) * 64],
                            tstate[r0:r0 + 64, hh * 64:(hh + 1) * 64],
                            start=False, stop=False))(hh, r0, fo, ybank), [])
                        te((lambda hh, r0, fo, ybank: lambda tt: tt.matmul(
                            ps_y[ybank][r0:r0 + 64, fo:fo + 64],
                            di_s[r0:r0 + 64, hh * 64:(hh + 1) * 64],
                            xstm[r0:r0 + 64, hh * 64:(hh + 1) * 64],
                            start=False, stop=True))(hh, r0, fo, ybank), [])
                tok = ("TE", P.cnt["TE"])
                sl[f"ps_y{ybank}"].wrote(tok)
                for n in ["slb", "xdtm", "tstate", "cdt", "xstm"]:
                    sl[n].read(tok)
                t = ve((lambda quarter, ybank: lambda v: v.tensor_tensor(
                    slb[:, quarter * 512:(quarter + 1) * 512], ps_y[ybank][:, 0:512],
                    zact[:, quarter * 512:(quarter + 1) * 512], ALU.mult))(quarter, ybank),
                    R(f"ps_y{ybank}") + R("zact") + Wd("slb"))
                sl[f"ps_y{ybank}"].read(t)
                sl["zact"].read(t)
                sl["slb"].wrote(t)

            first_st = True
            for hhalf in range(2):
                for bank in range(2):
                    first = True
                    for hh in range(hhalf * 16 + bank * 8, hhalf * 16 + bank * 8 + 8):
                        for s_ in range(2):
                            r0 = s_ * 64
                            fo = (hh % 8) * 64
                            te((lambda hh, r0, fo, bank: lambda tt: tt.matmul(
                                ps_st[bank][r0:r0 + 64, fo:fo + 64],
                                bdec[r0:r0 + 64, hh * 64:(hh + 1) * 64],
                                xdtm[r0:r0 + 64, hh * 64:(hh + 1) * 64],
                                start=True, stop=True))(hh, r0, fo, bank),
                                R("bdec") + R("xdtm") + (Wd(f"ps_st{bank}") if first else []))
                            first = False
                    tok = ("TE", P.cnt["TE"])
                    sl[f"ps_st{bank}"].wrote(tok)
                    sl["bdec"].read(tok)
                    sl["xdtm"].read(tok)
                    t = ve((lambda hhalf, bank: lambda v: v.tensor_copy(
                        stsb[:, hhalf * 1024 + bank * 512:hhalf * 1024 + bank * 512 + 512],
                        ps_st[bank][:, 0:512]))(hhalf, bank),
                        R(f"ps_st{bank}") + (Wd("stsb") if first_st else []))
                    first_st = False
                    sl[f"ps_st{bank}"].read(t)
            sl["stsb"].wrote(t)

            if c > 0:
                tail_ln(c - 1)
            if c + 1 < nch:
                load_x(c + 1)
                xprep(c + 1)

            for hhalf in range(2):
                t = se((lambda hhalf: lambda s: s.activation(
                    escan[:, 0:DM], slb[:, hhalf * 1024:(hhalf + 1) * 1024], ACT.Square,
                    accum_out=(cols["ssq"] if hhalf == 0 else cols["ssq2"])[:, 0:1]))(hhalf),
                    Wd("escan") + R("slb") + Wd("ssq" if hhalf == 0 else "ssq2"))
                sl["slb"].read(t)
                sl["escan"].wrote(t)
                sl["ssq" if hhalf == 0 else "ssq2"].wrote(t)
            t = ve(lambda v: v.tensor_tensor(cols["ssq"][:, 0:1], cols["ssq"][:, 0:1],
                                             cols["ssq2"][:, 0:1], ALU.add),
                   Wd("ssq") + R("ssq2"))
            sl["ssq"].wrote(t)
            t = se(lambda s: s.activation(cols["lnv"][:, 0:1], cols["ssq"][:, 0:1], ACT.Ln,
                                          bias=cols["epsc"][:, 0:1], scale=cols["invdic"][:, 0:1]),
                   R("ssq") + R("epsc") + R("invdic") + Wd("lnv"))
            sl["lnv"].wrote(t)
            t = se(lambda s: s.activation(cols["rsc"][:, 0:1], cols["lnv"][:, 0:1], ACT.Exp,
                                          scale=cols["nhalfc"][:, 0:1]),
                   R("lnv") + R("nhalfc") + Wd("rsc"))
            sl["rsc"].wrote(t)
            t = se(lambda s: s.mul(yn[:, :], slb[:, :], cols["rsc"][:, 0:1]),
                   R("slb") + R("rsc") + Wd("yn"))
            sl["slb"].read(t)
            sl["yn"].wrote(t)

        def tail_mm(c):
            p = c % 2
            for hhalf in range(2):
                pst = ps_tb if hhalf == 0 else ps_tb2
                pn = "ps_tb" if hhalf == 0 else "ps_tb2"
                for j in range(8):
                    t = te((lambda hhalf, j, pst: lambda tt: tt.transpose(
                        pst[:, j * 128:(j + 1) * 128],
                        yn[:, (hhalf * 8 + j) * 128:(hhalf * 8 + j + 1) * 128],
                        idb_s[:, :]))(hhalf, j, pst),
                        R("yn") + (Wd(pn) if j == 0 else []))
                sl[pn].wrote(t)
                sl["yn"].read(t)
                t = ve((lambda hhalf, pst: lambda v: v.tensor_copy(
                    cdt[:, hhalf * 1024:(hhalf + 1) * 1024], pst[:, :]))(hhalf, pst),
                    R(pn) + Wd("cdt"))
                sl[pn].read(t)
                sl["cdt"].wrote(t)

            t_h = None
            for nh in range(2):
                bank = nh
                for k in range(16):
                    t = te((lambda nh, k, bank: lambda tt: tt.matmul(
                        ps_y[bank][:, 0:512],
                        cdt[:, k * 128:(k + 1) * 128],
                        W2s[k][:, nh * 512:(nh + 1) * 512],
                        start=(k == 0), stop=(k == 15)))(nh, k, bank),
                        R("cdt") + R("w2") + (Wd(f"ps_y{bank}") if k == 0 else []))
                sl[f"ps_y{bank}"].wrote(t)
                sl["cdt"].read(t)
                t_h = ve((lambda nh, bank, p: lambda v: v.tensor_tensor(
                    h1[:, nh * 512:(nh + 1) * 512], ps_y[bank][:, 0:512],
                    xtm[p][:, nh * 512:(nh + 1) * 512], ALU.add))(nh, bank, p),
                    R(f"ps_y{bank}") + sl["xtm"][p].rd() + (Wd("h1") if nh == 0 else []))
                sl[f"ps_y{bank}"].read(t_h)
                sl["xtm"][p].read(t_h)
            sl["h1"].wrote(t_h)

        def tail_ln(c):
            p = c % 2
            t = se(lambda s: s.activation(escan[:, 0:DM], h1[:, :], ACT.Identity,
                                          accum_out=cols["hsum"][:, 0:1]),
                   R("h1") + Wd("escan") + Wd("hsum"))
            sl["h1"].read(t)
            sl["escan"].wrote(t)
            sl["hsum"].wrote(t)
            t = se(lambda s: s.activation(escan[:, 0:DM], h1[:, :], ACT.Square,
                                          accum_out=cols["hsq"][:, 0:1]),
                   R("h1") + Wd("escan") + Wd("hsq"))
            sl["h1"].read(t)
            sl["escan"].wrote(t)
            sl["hsq"].wrote(t)
            t = P.emit_gc(lambda g_: g_.tensor_scalar(cols["muc"][:, 0:1], cols["hsum"][:, 0:1],
                                               1.0 / DM, None, ALU.mult),
                   R("hsum") + Wd("muc"))
            sl["muc"].wrote(t)
            t = P.emit_gc(lambda g_: g_.tensor_tensor(cols["varc"][:, 0:1], cols["muc"][:, 0:1],
                                               cols["muc"][:, 0:1], ALU.mult),
                   R("muc") + Wd("varc"))
            sl["varc"].wrote(t)
            t = P.emit_gc(lambda g_: g_.tensor_scalar(cols["hsqn"][:, 0:1], cols["hsq"][:, 0:1],
                                               1.0 / DM, None, ALU.mult),
                   R("hsq") + Wd("hsqn"))
            sl["hsqn"].wrote(t)
            t = P.emit_gc(lambda g_: g_.tensor_tensor(cols["varc"][:, 0:1], cols["hsqn"][:, 0:1],
                                               cols["varc"][:, 0:1], ALU.subtract),
                   R("hsqn") + Wd("varc"))
            sl["varc"].wrote(t)
            t = se(lambda s: s.activation(cols["lnv"][:, 0:1], cols["varc"][:, 0:1],
                                          ACT.Ln, bias=cols["epsc"][:, 0:1]),
                   R("varc") + R("epsc") + Wd("lnv"))
            sl["lnv"].wrote(t)
            t = se(lambda s: s.activation(cols["rsc"][:, 0:1], cols["lnv"][:, 0:1],
                                          ACT.Exp, scale=cols["nhalfc"][:, 0:1]),
                   R("lnv") + R("nhalfc") + Wd("rsc"))
            sl["rsc"].wrote(t)
            t = P.emit_gc(lambda g_: g_.tensor_scalar(cols["negmu"][:, 0:1], cols["muc"][:, 0:1],
                                               -1.0, None, ALU.mult),
                   R("muc") + Wd("negmu"))
            sl["negmu"].wrote(t)
            t = P.emit_gc(lambda g_: g_.tensor_tensor(cols["nmr"][:, 0:1], cols["negmu"][:, 0:1],
                                               cols["rsc"][:, 0:1], ALU.mult),
                   R("negmu") + R("rsc") + Wd("nmr"))
            sl["nmr"].wrote(t)
            t = se(lambda s: s.activation(a2[:, :], h1[:, :], ACT.Identity,
                                          bias=cols["nmr"][:, 0:1], scale=cols["rsc"][:, 0:1]),
                   R("h1") + R("nmr") + R("rsc") + Wd("a2"))
            sl["h1"].read(t)
            sl["a2"].wrote(t)
            t = ve(lambda v: v.tensor_tensor(a2[:, :], a2[:, :], gbc[:, :], ALU.mult),
                   Wd("a2") + R("gbc"))
            sl["gbc"].read(t)
            sl["a2"].wrote(t)
            t = ve(lambda v: v.tensor_tensor(a2[:, :], a2[:, :], bbc[:, :], ALU.add),
                   Wd("a2") + R("bbc"))
            sl["bbc"].read(t)
            sl["a2"].wrote(t)
            t = ve((lambda p: lambda v: v.scalar_tensor_tensor(
                vv[:, :], a2[:, :], 0.0, xtm[p][:, :], ALU.add, ALU.add,
                accum_out=cols["vsum"][:, 0:1]))(p),
                R("a2") + sl["xtm"][p].rd() + Wd("vv") + Wd("vsum"))
            sl["a2"].read(t)
            sl["xtm"][p].read(t)
            sl["vv"].wrote(t)
            sl["vsum"].wrote(t)
            t = se(lambda s: s.activation(escan[:, 0:DM], vv[:, :], ACT.Square,
                                          accum_out=cols["vsq"][:, 0:1]),
                   R("vv") + Wd("escan") + Wd("vsq"))
            sl["vv"].read(t)
            sl["escan"].wrote(t)
            sl["vsq"].wrote(t)
            t = P.emit_gc(lambda g_: g_.tensor_scalar(cols["mu2"][:, 0:1], cols["vsum"][:, 0:1],
                                               1.0 / DM, None, ALU.mult),
                   R("vsum") + Wd("mu2"))
            sl["mu2"].wrote(t)
            t = P.emit_gc(lambda g_: g_.tensor_tensor(cols["var2"][:, 0:1], cols["mu2"][:, 0:1],
                                               cols["mu2"][:, 0:1], ALU.mult),
                   R("mu2") + Wd("var2"))
            sl["var2"].wrote(t)
            t = P.emit_gc(lambda g_: g_.tensor_scalar(cols["hsqn"][:, 0:1], cols["vsq"][:, 0:1],
                                               1.0 / DM, None, ALU.mult),
                   R("vsq") + Wd("hsqn"))
            sl["hsqn"].wrote(t)
            t = P.emit_gc(lambda g_: g_.tensor_tensor(cols["var2"][:, 0:1], cols["hsqn"][:, 0:1],
                                               cols["var2"][:, 0:1], ALU.subtract),
                   R("hsqn") + Wd("var2"))
            sl["var2"].wrote(t)
            t = se(lambda s: s.activation(cols["sdc"][:, 0:1], cols["var2"][:, 0:1],
                                          ACT.Sqrt, bias=cols["epsc"][:, 0:1]),
                   R("var2") + R("epsc") + Wd("sdc"))
            sl["sdc"].wrote(t)
            t = ve(lambda v: v.reciprocal(cols["rs2"][:, 0:1], cols["sdc"][:, 0:1]),
                   R("sdc") + Wd("rs2"))
            sl["rs2"].wrote(t)
            t = P.emit_gc(lambda g_: g_.tensor_scalar(cols["negmu"][:, 0:1], cols["mu2"][:, 0:1],
                                               -1.0, None, ALU.mult),
                   R("mu2") + Wd("negmu"))
            sl["negmu"].wrote(t)
            t = P.emit_gc(lambda g_: g_.tensor_tensor(cols["nmr2"][:, 0:1], cols["negmu"][:, 0:1],
                                               cols["rs2"][:, 0:1], ALU.mult),
                   R("negmu") + R("rs2") + Wd("nmr2"))
            sl["nmr2"].wrote(t)
            t = se(lambda s: s.activation(x1t[:, :], vv[:, :], ACT.Identity,
                                          bias=cols["nmr2"][:, 0:1], scale=cols["rs2"][:, 0:1]),
                   R("vv") + R("nmr2") + R("rs2") + Wd("x1t"))
            sl["vv"].read(t)
            sl["x1t"].wrote(t)
            t = ve(lambda v: v.reduce_max(amtmp[:, 0:1], x1t[:, :], axis=AX.X,
                                          apply_absolute_value=True),
                   R("x1t") + Wd("amtmp"))
            sl["x1t"].read(t)
            sl["amtmp"].wrote(t)
            ai, aj = c % 2, 1 - (c % 2)
            t = ve((lambda ai, aj: lambda v: v.tensor_tensor(
                amaxc[ai][:, 0:1], amaxc[aj][:, 0:1], amtmp[:, 0:1], ALU.max))(ai, aj),
                sl["amaxc"][aj].rd() + R("amtmp") + sl["amaxc"][ai].wd())
            sl["amaxc"][aj].read(t)
            sl["amtmp"].read(t)
            sl["amaxc"][ai].wrote(t)
            t = g_dma(x1_hbm[c * 128:(c + 1) * 128, :], x1t[:, :], R("x1t"))
            sl["x1t"].read(t)

        load_xbf(0)
        load_x(0)
        xprep(0)
        head(0)
        for c in range(nch):
            body(c)
            if c + 1 < nch:
                head(c + 1)
            tail_mm(c)
        tail_ln(nch - 1)

        lastp = (nch - 1) % 2
        t = g_dma(cc_in.ap(), amaxc[lastp][:, 0:1], sl["amaxc"][lastp].rd())
        alldone = ([("TE", P.cnt["TE"]), ("VE", P.cnt["VE"]), ("SE", P.cnt["SE"]), t]
                   + P.qdone("SQ") + P.qdone("GQ"))
        tokcc = P.emit_cc(lambda g: g.collective_compute(
            "AllReduce", ALU.max, replica_groups=[list(range(8))],
            ins=[cc_in.ap()], outs=[cc_out.ap()]), [t])
        t = g_dma(bass.AP(amrow.ap().tensor, 0, [[128, 1], [1, 128]]),
                  bass.AP(cc_out.ap().tensor, 0, [[128, 1], [1, 128]]), [tokcc])
        sl["amrow"].wrote(t)
        bar = [tokcc, t] + alldone

        es1.close()

        sqd = [sb(es, f"sqd{i}", [128, DM], BF16) for i in range(2)]
        c2 = [{}, {}]
        for pgi in range(2):
            for n in ["vsum", "vsq", "mu2", "var2", "rs2", "nmr2", "sdc"]:
                c2[pgi][n] = sb(es, f"p2_{n}{pgi}", [128, 1], F32)
        s2c = [{n: Slot() for n in c2[0]} for _ in range(2)]
        s2q = [Slot(), Slot()]
        w3full = [sb(es, f"w3f{k}", [128, DM], F32) for k in range(8)]
        w3q = [sb(es, f"w3q{k}", [128, DM], BF16) for k in range(8)]
        x1ld = [sb(es, f"x1ld{i}", [128, DM], F32) for i in range(4)]
        qq = [sb(es, f"qq{i}", [128, DM], F32) for i in range(4)]
        uq = [sb(es, f"uq{i}", [128, DM], F32) for i in range(4)]
        x1q = [sb(es, f"x1q{i}", [128, DM], BF16) for i in range(4)]
        x1qt = [sb(es, f"x1qt{i}", [128, DM], BF16) for i in range(4)]
        f1 = [sb(es, f"f1_{i}", [128, DM], F32) for i in range(2)]
        v2 = [sb(es, f"v2_{i}", [128, DM], F32) for i in range(2)]
        x2 = [sb(es, f"x2_{i}", [128, DM], F32) for i in range(2)]
        onesb = sb(es, "onesb", [1, 128], BF16)
        fbq = sb(es, "fbq", [1, DM], BF16)
        svvc = sb(es, "svvc", [1, 1], F32)
        fbrow = sb(es, "fbrow", [1, DM], F32)
        amg = sb(es, "amg", [1, 1], F32)
        wmrow = sb(es, "wmrow", [1, 128], F32)
        wamax = sb(es, "wamax", [1, 1], F32)
        sxv = sb(es, "sxv", [1, 1], F32)
        swv = sb(es, "swv", [1, 1], F32)
        sxi = sb(es, "sxi", [1, 1], F32)
        swi = sb(es, "swi", [1, 1], F32)
        sxw = sb(es, "sxw", [1, 1], F32)
        onec = sb(es, "onec", [1, 1], F32)
        c448 = sb(es, "c448", [1, 1], F32)
        sxcol = sb(es, "sxcol", [128, 1], F32)
        swcol = sb(es, "swcol", [128, 1], F32)
        sxwcol = sb(es, "sxwcol", [128, 1], F32)
        wmxc = [sb(es, f"wmxc{i}", [128, 1], F32) for i in range(2)]
        for n in ["onesb", "fbq", "svvc", "fbrow", "amg", "wmrow", "wamax",
                  "sxv", "swv", "sxi", "swi", "sxw", "onec", "c448",
                  "sxcol", "swcol", "sxwcol", "w3q"]:
            S(n)
        for n in ["wmxc", "x2", "f1", "v2"]:
            sl[n] = [Slot(), Slot()]
        for n in ["x1ld", "qq", "uq", "x1q", "x1qt"]:
            sl[n] = [Slot() for _ in range(4)]
        sl["w3full"] = [Slot() for _ in range(8)]

        t = ve(lambda v: v.reduce_max(amg[0:1, 0:1], amrow[0:1, :], axis=AX.X),
               bar + R("amrow") + Wd("amg"))
        sl["amg"].wrote(t)
        t = ve(lambda v: v.memset(onec[0:1, 0:1], 1.0), bar + Wd("onec"))
        sl["onec"].wrote(t)
        t = ve(lambda v: v.memset(c448[0:1, 0:1], 448.0), bar + Wd("c448"))
        sl["c448"].wrote(t)
        t = ve(lambda v: v.tensor_scalar(amg[0:1, 0:1], amg[0:1, 0:1], 1e-12, None, ALU.max),
               Wd("amg"))
        sl["amg"].wrote(t)
        t = ve(lambda v: v.reciprocal(sxv[0:1, 0:1], amg[0:1, 0:1]),
               R("amg") + Wd("sxv"))
        sl["sxv"].wrote(t)
        t = ve(lambda v: v.tensor_scalar(sxv[0:1, 0:1], sxv[0:1, 0:1], 448.0, None, ALU.mult),
               Wd("sxv"))
        sl["sxv"].wrote(t)
        t = ve(lambda v: v.tensor_scalar(sxv[0:1, 0:1], sxv[0:1, 0:1], 448.0, None, ALU.min),
               Wd("sxv"))
        sl["sxv"].wrote(t)
        t = ve(lambda v: v.reciprocal(sxi[0:1, 0:1], sxv[0:1, 0:1]),
               R("sxv") + Wd("sxi"))
        sl["sxi"].wrote(t)

        for k in range(8):
            t = q_dma(w3full[k][:, :], w3t[k * 128:(k + 1) * 128, :],
                      bar + sl["w3full"][k].wd())
            sl["w3full"][k].wrote(t)
            t = ve((lambda k: lambda v: v.reduce_max(amtmp[:, 0:1], w3full[k][:, :],
                                                     axis=AX.X, apply_absolute_value=True))(k),
                   sl["w3full"][k].rd() + Wd("amtmp"))
            sl["amtmp"].wrote(t)
            sl["w3full"][k].read(t)
            if k == 0:
                t = ve(lambda v: v.tensor_copy(wmxc[0][:, 0:1], amtmp[:, 0:1]),
                       R("amtmp") + sl["wmxc"][0].wd())
            else:
                ai, aj = k % 2, 1 - k % 2
                t = ve((lambda ai, aj: lambda v: v.tensor_tensor(
                    wmxc[ai][:, 0:1], wmxc[aj][:, 0:1], amtmp[:, 0:1], ALU.max))(ai, aj),
                    sl["wmxc"][aj].rd() + R("amtmp") + sl["wmxc"][ai].wd())
                sl["wmxc"][aj].read(t)
            sl["wmxc"][k % 2].wrote(t)
            sl["amtmp"].read(t)
        t = g_dma(wmx_hbm.ap(), wmxc[1][:, 0:1], sl["wmxc"][1].rd() + bar)
        t = g_dma(bass.AP(wmrow.ap().tensor, 0, [[128, 1], [1, 128]]),
                  bass.AP(wmx_hbm.ap().tensor, 0, [[128, 1], [1, 128]]), [t])
        sl["wmrow"].wrote(t)
        t = ve(lambda v: v.reduce_max(wamax[0:1, 0:1], wmrow[0:1, :], axis=AX.X),
               R("wmrow") + Wd("wamax"))
        sl["wamax"].wrote(t)
        t = ve(lambda v: v.tensor_scalar(wamax[0:1, 0:1], wamax[0:1, 0:1], 1e-12, None, ALU.max),
               Wd("wamax"))
        sl["wamax"].wrote(t)
        t = ve(lambda v: v.reciprocal(swv[0:1, 0:1], wamax[0:1, 0:1]),
               R("wamax") + Wd("swv"))
        sl["swv"].wrote(t)
        t = ve(lambda v: v.tensor_scalar(swv[0:1, 0:1], swv[0:1, 0:1], 448.0, None, ALU.mult),
               Wd("swv"))
        sl["swv"].wrote(t)
        t = ve(lambda v: v.tensor_scalar(swv[0:1, 0:1], swv[0:1, 0:1], 448.0, None, ALU.min),
               Wd("swv"))
        sl["swv"].wrote(t)
        t = ve(lambda v: v.reciprocal(swi[0:1, 0:1], swv[0:1, 0:1]),
               R("swv") + Wd("swi"))
        sl["swi"].wrote(t)
        t = ve(lambda v: v.tensor_tensor(sxw[0:1, 0:1], sxi[0:1, 0:1], swi[0:1, 0:1], ALU.mult),
               R("sxi") + R("swi") + Wd("sxw"))
        sl["sxw"].wrote(t)
        for (src, sn, dst, dn) in ((sxv, "sxv", sxcol, "sxcol"),
                                   (swv, "swv", swcol, "swcol"),
                                   (sxw, "sxw", sxwcol, "sxwcol")):
            t = te((lambda src: lambda tt: tt.matmul(
                ps_ip[0][:, 0:1], ones_s[0:1, :], src[0:1, 0:1],
                start=True, stop=True))(src),
                bar + R(sn) + R("ones") + Wd("ps_ip0"))
            sl["ps_ip0"].wrote(t)
            t = ve((lambda dst: lambda v: v.tensor_copy(dst[:, 0:1], ps_ip[0][:, 0:1]))(dst),
                   R("ps_ip0") + Wd(dn))
            sl["ps_ip0"].read(t)
            sl[dn].wrote(t)

        for k in range(8):
            pk = k % 4
            t = se((lambda k, pk: lambda s: s.activation(
                qq[pk][:, :], w3full[k][:, :], ACT.Identity,
                scale=swcol[:, 0:1]))(k, pk),
                sl["w3full"][k].rd() + R("swcol") + sl["qq"][pk].wd())
            sl["w3full"][k].read(t)
            sl["qq"][pk].wrote(t)
            t = ve((lambda pk: lambda v: v.scalar_tensor_tensor(
                uq[pk][:, :], qq[pk][:, :], C21, qq[pk][:, :],
                ALU.mult, ALU.subtract))(pk),
                sl["qq"][pk].rd() + sl["uq"][pk].wd())
            sl["uq"][pk].wrote(t)
            t = ve((lambda pk, k: lambda v: v.scalar_tensor_tensor(
                w3q[k][:, :], qq[pk][:, :], C21, uq[pk][:, :],
                ALU.mult, ALU.subtract))(pk, k),
                sl["qq"][pk].rd() + sl["uq"][pk].rd() + Wd("w3q"))
            sl["qq"][pk].read(t)
            sl["uq"][pk].read(t)
        sl["w3q"].wrote(t)
        t = g_dma(fbrow[:, :], fbiasd[:, :], bar + Wd("fbrow"))
        sl["fbrow"].wrote(t)
        t = ve(lambda v: v.memset(onesb[0:1, :], 1.0), bar + Wd("onesb"))
        sl["onesb"].wrote(t)
        t = ve(lambda v: v.tensor_tensor(svvc[0:1, 0:1], sxv[0:1, 0:1], swv[0:1, 0:1],
                                         ALU.mult),
               R("sxv") + R("swv") + Wd("svvc"))
        sl["svvc"].wrote(t)
        t = ve(lambda v: v.tensor_scalar(fbq[0:1, :], fbrow[0:1, :], svvc[0:1, 0:1],
                                         None, ALU.mult),
               R("fbrow") + R("svvc") + Wd("fbq"))
        sl["fbq"].wrote(t)

        def x1_load(g):
            pg = g % 4
            t = q_dma(x1ld[pg][:, :], x1_hbm[g * 128:(g + 1) * 128, :],
                      bar + sl["x1ld"][pg].wd())
            sl["x1ld"][pg].wrote(t)

        for g0 in range(4):
            x1_load(g0)

        def quant_a(g):
            pg = g % 4
            t = se((lambda pg: lambda s: s.activation(
                qq[pg][:, :], x1ld[pg][:, :], ACT.Identity,
                scale=sxcol[:, 0:1]))(pg),
                sl["x1ld"][pg].rd() + R("sxcol") + sl["qq"][pg].wd())
            sl["x1ld"][pg].read(t)
            sl["qq"][pg].wrote(t)

        def quant_b(g):
            pg = g % 4
            t = ve((lambda pg: lambda v: v.scalar_tensor_tensor(
                uq[pg][:, :], qq[pg][:, :], C21, qq[pg][:, :],
                ALU.mult, ALU.subtract))(pg),
                sl["qq"][pg].rd() + sl["uq"][pg].wd())
            sl["uq"][pg].wrote(t)
            t = ve((lambda pg: lambda v: v.scalar_tensor_tensor(
                x1q[pg][:, :], qq[pg][:, :], C21, uq[pg][:, :],
                ALU.mult, ALU.subtract))(pg),
                sl["qq"][pg].rd() + sl["uq"][pg].rd() + sl["x1q"][pg].wd())
            sl["qq"][pg].read(t)
            sl["uq"][pg].read(t)
            sl["x1q"][pg].wrote(t)
            pst = ps_tb if (pg % 2) == 0 else ps_tb2
            pn = "ps_tb" if (pg % 2) == 0 else "ps_tb2"
            for j in range(8):
                t = te((lambda j, pg, pst: lambda tt: tt.transpose(
                    pst[:, j * 128:(j + 1) * 128], x1q[pg][:, j * 128:(j + 1) * 128],
                    idb_s[:, :]))(j, pg, pst),
                    sl["x1q"][pg].rd() + (bar if (g == 0 and j == 0) else []) +
                    (Wd(pn) if j == 0 else []))
            sl[pn].wrote(t)
            sl["x1q"][pg].read(t)
            t = se((lambda pg, pst: lambda s: s.copy(x1qt[pg][:, :], pst[:, :]))(pg, pst),
                   sl[pn].rd() + sl["x1qt"][pg].wd())
            sl[pn].read(t)
            sl["x1qt"][pg].wrote(t)

        def ffn_mm(g):
            pg = g % 4
            ph = g % 2
            for nh in range(2):
                bank = nh
                t = te((lambda nh, bank: lambda tt: tt.matmul(
                    ps_y[bank][:, 0:512], onesb[0:1, :],
                    fbq[0:1, nh * 512:(nh + 1) * 512],
                    start=True, stop=False))(nh, bank),
                    R("onesb") + R("fbq") + Wd(f"ps_y{bank}"))
                for k in range(8):
                    t = te((lambda nh, k, bank, pg: lambda tt: tt.matmul(
                        ps_y[bank][:, 0:512],
                        x1qt[pg][:, k * 128:(k + 1) * 128],
                        w3q[k][:, nh * 512:(nh + 1) * 512],
                        start=False, stop=(k == 7)))(nh, k, bank, pg),
                        sl["x1qt"][pg].rd() + R("w3q"))
                sl[f"ps_y{bank}"].wrote(t)
                sl["x1qt"][pg].read(t)
                t = se((lambda nh, bank, ph: lambda s: s.activation(
                    f1[ph][:, nh * 512:(nh + 1) * 512], ps_y[bank][:, 0:512], ACT.Identity,
                    scale=sxwcol[:, 0:1]))(nh, bank, ph),
                    R(f"ps_y{bank}") + R("sxwcol") + (sl["f1"][ph].wd() if nh == 0 else []))
                sl[f"ps_y{bank}"].read(t)
            sl["f1"][ph].wrote(t)

        def tail(g):
            pg = g % 4
            ph = g % 2
            cc_ = c2[ph]
            sc_ = s2c[ph]
            t = ve((lambda pg, ph, cc_: lambda v: v.scalar_tensor_tensor(
                v2[ph][:, :], x1ld[pg][:, :], 0.0, f1[ph][:, :], ALU.add, ALU.add,
                accum_out=cc_["vsum"][:, 0:1]))(pg, ph, cc_),
                sl["x1ld"][pg].rd() + sl["f1"][ph].rd() + sl["v2"][ph].wd() + sc_["vsum"].wd())
            sl["x1ld"][pg].read(t)
            sl["f1"][ph].read(t)
            sl["v2"][ph].wrote(t)
            sc_["vsum"].wrote(t)
            if g + 4 < nch:
                x1_load(g + 4)
            t = se((lambda ph, cc_: lambda s: s.activation(
                sqd[ph][:, 0:DM], v2[ph][:, :], ACT.Square,
                accum_out=cc_["vsq"][:, 0:1]))(ph, cc_),
                sl["v2"][ph].rd() + s2q[ph].wd() + sc_["vsq"].wd())
            sl["v2"][ph].read(t)
            s2q[ph].wrote(t)
            sc_["vsq"].wrote(t)
            t = ve((lambda pg, cc_: lambda v: v.tensor_scalar(
                cc_["mu2"][:, 0:1], cc_["vsum"][:, 0:1], 1.0 / DM, None, ALU.mult))(pg, cc_),
                sc_["vsum"].rd() + sc_["mu2"].wd())
            sc_["mu2"].wrote(t)
            t = ve((lambda pg, cc_: lambda v: v.tensor_tensor(
                cc_["var2"][:, 0:1], cc_["mu2"][:, 0:1], cc_["mu2"][:, 0:1], ALU.mult))(pg, cc_),
                sc_["mu2"].rd() + sc_["var2"].wd())
            sc_["var2"].wrote(t)
            t = ve((lambda pg, cc_: lambda v: v.scalar_tensor_tensor(
                cc_["var2"][:, 0:1], cc_["vsq"][:, 0:1], 1.0 / DM, cc_["var2"][:, 0:1],
                ALU.mult, ALU.subtract))(pg, cc_),
                sc_["vsq"].rd() + sc_["var2"].wd())
            sc_["var2"].wrote(t)
            t = se((lambda pg, cc_: lambda s: s.activation(
                cc_["sdc"][:, 0:1], cc_["var2"][:, 0:1], ACT.Sqrt,
                bias=cols["epsc"][:, 0:1]))(pg, cc_),
                sc_["var2"].rd() + R("epsc") + sc_["sdc"].wd())
            sc_["sdc"].wrote(t)
            t = ve((lambda pg, cc_: lambda v: v.reciprocal(
                cc_["rs2"][:, 0:1], cc_["sdc"][:, 0:1]))(pg, cc_),
                sc_["sdc"].rd() + sc_["rs2"].wd())
            sc_["rs2"].wrote(t)
            t = ve((lambda pg, cc_: lambda v: v.scalar_tensor_tensor(
                cc_["nmr2"][:, 0:1], cc_["mu2"][:, 0:1], -1.0, cc_["rs2"][:, 0:1],
                ALU.mult, ALU.mult))(pg, cc_),
                sc_["mu2"].rd() + sc_["rs2"].rd() + sc_["nmr2"].wd())
            sc_["nmr2"].wrote(t)
            t = se((lambda ph, cc_: lambda s: s.activation(
                x2[ph][:, :], v2[ph][:, :], ACT.Identity,
                bias=cc_["nmr2"][:, 0:1], scale=cc_["rs2"][:, 0:1]))(ph, cc_),
                sl["v2"][ph].rd() + sc_["nmr2"].rd() + sc_["rs2"].rd() + sl["x2"][ph].wd())
            sl["v2"][ph].read(t)
            sl["x2"][ph].wrote(t)
            for s_ in range(2):
                dst = bass.AP(yout.ap().tensor, s_ * L * DM + g * CH * DM, [[DM, 64], [1, DM]])
                t = q_dma(dst, x2[ph][s_ * 64:(s_ + 1) * 64, :], sl["x2"][ph].rd())
                sl["x2"][ph].read(t)

        quant_a(0)
        quant_b(0)
        for g in range(nch):
            if g + 1 < nch:
                quant_a(g + 1)
            ffn_mm(g)
            if g + 1 < nch:
                quant_b(g + 1)
            if g > 0:
                tail(g - 1)
        tail(nch - 1)

        P.emit("SQ", lambda s: s.nop(), P.qdone("SQ"))

        with nc.Block() as block:
            P.realize(block)

    return nc


def _prep_inputs(inputs):
    W = inputs
    pr = {}
    pr["w1t"] = np.ascontiguousarray(np.asarray(W["L0_attn_in_proj_w"], np.float32).T)
    pr["w2t"] = np.ascontiguousarray(np.asarray(W["L0_attn_out_proj_w"], np.float32).T)
    pr["w3t"] = np.ascontiguousarray(np.asarray(W["L0_ffn_weight"], np.float32).T)
    cw = np.asarray(W["L0_attn_conv_w"], np.float32)[:, 0, :]
    pr["convw"] = np.ascontiguousarray(cw.reshape(17, 128, 4).transpose(1, 0, 2).reshape(128, 68))
    cb = np.asarray(W["L0_attn_conv_b"], np.float32)
    pr["convb"] = np.ascontiguousarray(cb.reshape(17, 128).T)
    pr["dtb"] = np.asarray(W["L0_attn_dt_bias"], np.float32).reshape(32, 1)
    pr["alog"] = np.asarray(W["L0_attn_A_log"], np.float32).reshape(32, 1)
    pr["drep"] = np.ascontiguousarray(np.tile(np.asarray(W["L0_attn_D"], np.float32)[None, :], (128, 1)))
    pr["rmsw16"] = np.ascontiguousarray(np.asarray(W["L0_attn_rms_w"], np.float32).reshape(16, 128).T)
    pr["gamma"] = np.asarray(W["L0_attn_ln_gamma"], np.float32).reshape(1, DM)
    pr["beta"] = np.asarray(W["L0_attn_ln_beta"], np.float32).reshape(1, DM)
    pr["fbias"] = np.asarray(W["L0_ffn_bias"], np.float32).reshape(1, DM)
    pr.update(host_constants())
    return pr


def kernel(**inputs):
    x = np.asarray(inputs["L0_attn_x"], np.float32)
    pr = _prep_inputs(inputs)
    nc = build(nch=32, debug=False)
    in_maps = []
    for core in range(8):
        m = dict(pr)
        m["xin"] = np.ascontiguousarray(x[2 * core:2 * core + 2])
        in_maps.append(m)
    res = run_bass_kernel_spmd(nc, in_maps, core_ids=list(range(8)))
    out = np.empty((16, 2048, 1024), np.float32)
    for core in range(8):
        out[2 * core:2 * core + 2] = np.asarray(res.results[core]["yout"]).reshape(2, 2048, 1024)
    return out
